# version 11
# baseline (speedup 1.0000x reference)
import contextlib
import numpy as np
import ml_dtypes
import concourse.bass as bass
import concourse.mybir as mybir
from concourse.bass_utils import run_bass_kernel_spmd

F32 = mybir.dt.float32
BF16 = mybir.dt.bfloat16
AF = mybir.ActivationFunctionType
ALU = mybir.AluOpType
AX = mybir.AxisListType

S = 4096
D = 1024
DI = 2048
DS = 16
DTR = 64
H = 16
DH = 64
DHID = 4096
NE = 8
CH = 512
NCH = S // CH
NEG = -30000.0
EPS = 1e-5

FUSED = True

INPUT_SHAPES = {
    "x": ([S, D], F32),
    "W_in": ([D, 2 * DI], F32), "W_conv": ([DI, 4], F32), "b_conv": ([DI], F32),
    "W_x": ([DI, 96], F32), "W_dt": ([DTR, DI], F32), "b_dt": ([DI], F32),
    "A_log": ([DI, DS], F32), "Dskip": ([DI], F32), "W_out": ([DI, D], F32),
    "gamma0": ([D], F32), "beta0": ([D], F32),
    "Wg": ([D, NE], F32), "bg": ([NE], F32),
    "W1": ([NE, D, 2048], F32), "b1": ([NE, 2048], F32),
    "W2": ([NE, 2048, D], F32), "b2": ([NE, D], F32),
    "gammaF": ([D], F32), "betaF": ([D], F32),
    "Wq": ([D, D], F32), "Wk": ([D, D], F32), "Wv": ([D, D], F32), "Wo": ([D, D], F32),
    "w1_w": ([DHID, D], F32), "w1_s": ([DHID], F32),
    "w2_w": ([DHID, D], F32), "w2_s": ([DHID], F32),
    "h1_w": ([DHID, DHID], F32), "h1_s": ([DHID], F32),
    "h2_w": ([DHID, DHID], F32), "h2_s": ([DHID], F32),
    "w3_w": ([D, DHID], F32), "w3_s": ([D], F32),
    "eye": ([128, 128], F32),
    "mask_cur": ([128, 128], F32),
    "mask_prev": ([128, 128], F32),
    "sel16": ([16, 16 * 128], F32),
}

SCRATCH_SHAPES = {
    "xn_f": ([S, D], F32),
    "xnT_b": ([D, S], BF16),
    "gates_f": ([S, NE], F32),
    "gatesT_b": ([NE, S], BF16),
    "xiT_b": ([DI, 3 + S], BF16),
    "zT_b": ([DI, S], BF16),
    "uT_b": ([DI, S], BF16),
    "dtT_b": ([DI, S], BF16),
    "BmT_b": ([DS, S], BF16),
    "CmT_b": ([DS, S], BF16),
    "yW_f": ([S, D], F32),
    "moe_f": ([S, D], F32),
    "x1_f": ([S, D], F32),
    "xn1T_b": ([D, S], BF16),
    "qT_b": ([D, S], BF16),
    "kT_b": ([D, S], BF16),
    "v_b": ([S, D], BF16),
    "partial_f": ([S, D], F32),
    "aT_b": ([DHID, S], BF16),
    "bT_b": ([DHID, S], BF16),
    "cT_b": ([DHID, S], BF16),
    "Winb": ([D, 2 * DI], BF16),
    "Wxb": ([DI, 96], BF16),
    "Wdtb": ([DTR, DI], BF16),
    "Woutb": ([DI, D], BF16),
    "W1b": ([NE, D, 2048], BF16),
    "W2b": ([NE, 2048, D], BF16),
    "b2b": ([NE, D], BF16),
    "WqT": ([D, D], BF16), "WkT": ([D, D], BF16), "WvT": ([D, D], BF16), "WoT": ([D, D], BF16),
    "w1T": ([D, DHID], BF16), "w2T": ([D, DHID], BF16),
    "h1T": ([DHID, DHID], BF16), "h2T": ([DHID, DHID], BF16),
    "w3Ts": ([DHID, D], BF16),
}

OUTPUT_SHAPES = {"out": ([S, D], F32)}


class Prog:

    def __init__(self, fused):
        self.nc = bass.Bass("TRN2", target_bir_lowering=False, debug=False)
        self.stack = contextlib.ExitStack()
        self.fused = fused
        self.dram = {}
        self.reads = set()
        self.writes = set()
        self.sems = {}
        self.semv = {}
        self.pb_count = 0

    def T(self, name, write=False):
        if name in self.dram:
            ap = self.dram[name]
        else:
            if name in INPUT_SHAPES:
                shape, dt = INPUT_SHAPES[name]
                kind = "ExternalInput"
            elif name in OUTPUT_SHAPES:
                shape, dt = OUTPUT_SHAPES[name]
                kind = "ExternalOutput"
            else:
                shape, dt = SCRATCH_SHAPES[name]
                if self.fused:
                    kind = "ExternalOutput" if getattr(self, "debug_scratch", False) else "Internal"
                else:
                    kind = "ExternalOutput" if write else "ExternalInput"
            ap = self.nc.dram_tensor(name, shape, dt, kind=kind).ap()
            self.dram[name] = ap
        (self.writes if write else self.reads).add(name)
        return ap

    POOL_N = 70
    PINNED = {"pb", "gld", "wpg0", "wpg1", "wtg0", "wtg1", "p2sA", "p2xz0", "p2xz1",
              "p3st", "p4st", "p8st0", "p8st1", "p9st0", "p9st1", "p10st0", "p10st1"}

    def _pool(self):
        if not hasattr(self, "_pool_h"):
            self._pool_h = [self.stack.enter_context(self.nc.semaphore("pool%d" % i))
                            for i in range(self.POOL_N)]
            self._pool_val = [0] * self.POOL_N
            self._pool_next = 0
            self._active = {}
            self._pinned = {}
        return self._pool_h

    def sem(self, name):
        self._pool()
        if name in self.PINNED:
            if name not in self._pinned:
                self._pinned[name] = self.stack.enter_context(self.nc.semaphore(name))
                self.semv[name] = 0
            return self._pinned[name]
        if name not in self._active:
            if len(self._active) >= self.POOL_N:
                raise RuntimeError("semaphore pool exhausted within one pass")
            idx = self._pool_next
            self._pool_next = (self._pool_next + 1) % self.POOL_N
            tries = 0
            while idx in self._active.values():
                idx = (idx + 1) % self.POOL_N
                tries += 1
                assert tries <= self.POOL_N
            self._active[name] = idx
            self.semv[name] = self._pool_val[idx]
        return self._pool_h[self._active[name]]

    def rotate_sems(self):
        if not hasattr(self, "_pool_h"):
            return
        for name, idx in self._active.items():
            self._pool_val[idx] = self.semv[name]
        self._active = {}

    def inc(self, instr, name, n=1):
        instr.then_inc(self.sem(name), n)
        self.semv[name] += n
        return instr

    def bump(self, name, n):
        self.sem(name)
        self.semv[name] += n

    def val(self, name):
        self.sem(name)
        return self.semv[name]

    def wait(self, eng, name, val=None):
        eng.wait_ge(self.sem(name), self.semv[name] if val is None else val)

    def sb(self, name, shape, dt=F32, stack=None):
        return (stack or self.stack).enter_context(self.nc.sbuf_tensor(name, shape, dt))

    def ps(self, name, shape, dt=F32, stack=None):
        return (stack or self.stack).enter_context(self.nc.psum_tensor(name, shape, dt))


ENGS = ["sync", "vector", "scalar", "tensor", "gpsimd"]


def barrier(p, block):
    p.rotate_sems()
    p.pb_count += 1
    tgt = 5 * p.pb_count
    pb = p.sem("pb")
    for ename in ENGS:
        dec = getattr(block, ename)

        def _f(eng, ename=ename):
            if ename in ("vector", "scalar", "tensor", "gpsimd"):
                eng.drain()
            eng.sem_inc(pb, 1)
            eng.wait_ge(pb, tgt)
        dec(_f)
    p.semv["pb"] = tgt


def rr(ap, pat, **kw):
    return ap.rearrange(pat, **kw)


def pt(ap, r0, nr, c0, nc_):
    R = ap.shape[0]
    v = rr(ap, "(rt p) c -> p rt c", p=128)
    return v[:, r0 // 128:r0 // 128 + nr, c0:c0 + nc_]


def emit_prep(p, block):
    nc = p.nc
    c = {}
    c["eyeb"] = p.sb("c_eyeb", [128, 128], BF16)
    c["g0bc"] = p.sb("c_g0bc", [128, D], F32)
    c["b0bc"] = p.sb("c_b0bc", [128, D], F32)
    c["gFbc"] = p.sb("c_gFbc", [128, D], F32)
    c["bFbc"] = p.sb("c_bFbc", [128, D], F32)
    c["bgbc"] = p.sb("c_bgbc", [128, NE], F32)

    c["Wg"] = p.sb("c_Wg", [128, 8 * NE], BF16)
    c["WgF"] = p.sb("c_WgF", [128, 8 * NE], F32)
    c["b1"] = p.sb("c_b1", [128, NE * 16], F32)
    c["bconv"] = p.sb("c_bconv", [128, 16], F32)
    c["bdt"] = p.sb("c_bdt", [128, 16], F32)
    c["Dskip"] = p.sb("c_Dskip", [128, 16], F32)
    c["Wc"] = p.sb("c_Wc", [128, 16 * 4], F32)
    c["A"] = p.sb("c_A", [128, 256], F32)
    c["w1s"] = p.sb("c_w1s", [128, 32], F32)
    c["w2s"] = p.sb("c_w2s", [128, 32], F32)
    c["h1s"] = p.sb("c_h1s", [128, 32], F32)
    c["h2s"] = p.sb("c_h2s", [128, 32], F32)
    c["hstate"] = p.sb("c_hstate", [128, 256], F32)

    c["eyef"] = p.sb("c_eyef", [128, 128], F32)
    cpool = p.sb("c_cpool", [128, 8], F32)
    CONSTS = [EPS, -0.5, -1.0, 1.0, 0.5, 0.125, 2.0, 0.0]
    for j, cv in enumerate(CONSTS):
        p.nc.const_aps.register(F32, cv, cpool[:, j:j + 1])
    prep_stack = contextlib.ExitStack()
    c["ones1"] = p.sb("c_ones1", [1, 128], F32, prep_stack)
    c["rows"] = p.sb("c_rows", [1, 4 * D + NE], F32, prep_stack)
    vstage = p.sb("c_vstage", [128, 128], F32, prep_stack)
    ps_v = p.ps("c_psv", [128, 128], F32, prep_stack)
    vecjobs = [("b1", c["b1"], 128), ("b_conv", c["bconv"], 16),
               ("b_dt", c["bdt"], 16), ("Dskip", c["Dskip"], 16),
               ("w1_s", c["w1s"], 32), ("w2_s", c["w2s"], 32),
               ("h1_s", c["h1s"], 32), ("h2_s", c["h2s"], 32)]

    @block.sync
    def _(sync):
        sync.dma_start(out=c["rows"][:, 0:D], in_=rr(p.T("gamma0"), "(o d) -> o d", o=1)).then_inc(p.sem("ld"), 16)
        sync.dma_start(out=c["rows"][:, D:2 * D], in_=rr(p.T("beta0"), "(o d) -> o d", o=1)).then_inc(p.sem("ld"), 16)
        sync.dma_start(out=c["rows"][:, 2 * D:3 * D], in_=rr(p.T("gammaF"), "(o d) -> o d", o=1)).then_inc(p.sem("ld"), 16)
        sync.dma_start(out=c["rows"][:, 3 * D:4 * D], in_=rr(p.T("betaF"), "(o d) -> o d", o=1)).then_inc(p.sem("ld"), 16)
        sync.dma_start(out=c["rows"][:, 4 * D:], in_=rr(p.T("bg"), "(o d) -> o d", o=1)).then_inc(p.sem("ld"), 16)
        sync.dma_start(out=rr(c["Wc"][:], "p (t k) -> p t k", k=4),
                       in_=rr(p.T("W_conv"), "(t pp) k -> pp t k", pp=128)).then_inc(p.sem("ld"), 16)
        sync.dma_start(out=rr(c["A"][:], "p (t n) -> p t n", n=DS),
                       in_=rr(p.T("A_log"), "(t pp) n -> pp t n", pp=128)).then_inc(p.sem("ld"), 16)

        sync.dma_start(out=c["eyef"][:], in_=p.T("eye")).then_inc(p.sem("ld"), 16)
        sync.dma_start(out=rr(c["WgF"][:], "p (t e) -> p t e", e=NE),
                       in_=rr(p.T("Wg"), "(t pp) e -> pp t e", pp=128)).then_inc(p.sem("ld"), 16)

        for j, (nm, dst, k) in enumerate(vecjobs):
            src = p.T(nm)
            if len(src.shape) > 1:
                src = rr(src, "a b -> (a b)")
            if j > 0:
                sync.wait_ge(p.sem("vtr"), j)
            sync.dma_start(out=vstage[0:k, :], in_=rr(src, "(a pp) -> a pp", pp=128)
                           ).then_inc(p.sem("ldv"), 16)
    p.semv["ld"] += 16 * 9
    p.semv["ldv"] += 16 * 8

    @block.gpsimd
    def _(g):
        g.dma_start(out=c["eyeb"][:], in_=p.T("eye")).then_inc(p.sem("gld"), 16)
        g.dma_start(out=rr(c["Wg"][:], "p (t e) -> p t e", e=NE),
                    in_=rr(p.T("Wg"), "(t pp) e -> pp t e", pp=128)).then_inc(p.sem("gld"), 16)
        g.memset(c["hstate"][:], 0.0).then_inc(p.sem("gv"), 1)
    p.semv["gld"] += 32
    p.semv["gv"] += 1

    @block.tensor
    def _(t):
        t.wait_ge(p.sem("ld"), p.val("ld"))
        for j, (nm, dst, k) in enumerate(vecjobs):
            t.wait_ge(p.sem("ldv"), 16 * (j + 1))
            if j >= 1:
                t.wait_ge(p.sem("vtv"), j)
            t.transpose(ps_v[0:128, 0:k], vstage[0:k, :], c["eyef"][0:k, 0:k]
                        ).then_inc(p.sem("vtt"), 1)

    @block.vector
    def _(v):
        for j, cv in enumerate(CONSTS):
            v.memset(cpool[:, j:j + 1], cv)
        v.memset(c["ones1"][:], 1.0).then_inc(p.sem("vv"), 1)
        p.semv["vv"] += 1
        for j, (nm, dst, k) in enumerate(vecjobs):
            v.wait_ge(p.sem("vtt"), j + 1)
            v.tensor_copy(dst[:, 0:k], ps_v[:, 0:k]).then_inc(p.sem("vtv"), 1)
            v.sem_inc(p.sem("vtr"), 1)
    p.semv["vtt"] = 8
    p.semv["vtv"] = 8
    p.semv["vtr"] = 8

    ps_bc = p.ps("pp_bc", [128, 1024], F32, prep_stack)
    jobs = []
    for i, (nm, off) in enumerate([("g0bc", 0), ("b0bc", D), ("gFbc", 2 * D), ("bFbc", 3 * D)]):
        for hh in range(2):
            jobs.append((ps_bc[:, (2 * i + hh) % 2 * 512:(2 * i + hh) % 2 * 512 + 512],
                         c[nm][:, hh * 512:(hh + 1) * 512],
                         c["rows"][:, off + hh * 512: off + (hh + 1) * 512]))
    jobs.append((ps_bc[:, 0:NE], c["bgbc"][:], c["rows"][:, 4 * D:4 * D + NE]))

    @block.tensor
    def _(t):
        p.wait(t, "ld")
        p.wait(t, "vv")
        for j, (pshalf, dst, src) in enumerate(jobs):
            if j >= 2:
                p.wait(t, "bcv", j - 1)
            p.inc(t.matmul(pshalf, c["ones1"][:], src, start=True, stop=True), "bct", 1)

    @block.vector
    def _(v):
        for j, (pshalf, dst, src) in enumerate(jobs):
            v.wait_ge(p.sem("bct"), j + 1)
            p.inc(v.tensor_copy(dst, pshalf), "bcv", 1)

    @block.scalar
    def _(s):
        p.wait(s, "ld")
        p.wait(s, "vv")
        p.inc(s.activation(c["A"][:], c["A"][:], AF.Exp), "sv", 1)

    @block.vector
    def _(v):
        p.wait(v, "sv")
        p.inc(v.tensor_scalar_mul(c["A"][:], c["A"][:], -1.0), "vv", 1)

    @block.gpsimd
    def _(g):
        p.wait(g, "gld")
        p.wait(g, "gv")

    @block.sync
    def _(sy):
        p.wait(sy, "ld")
        p.wait(sy, "ldv")

    prep_stack.close()
    barrier(p, block)
    return c


def _flat2d(ap):
    return rr(ap, "e r c -> (e r) c") if len(ap.shape) == 3 else ap


def emit_wprep_cast(p, block):
    st = contextlib.ExitStack()
    casts = [("W_in", "Winb", D, 2 * DI), ("W_x", "Wxb", DI, 96),
             ("W_dt", "Wdtb", DTR, DI), ("W_out", "Woutb", DI, D),
             ("W1", "W1b", NE * D, 2048), ("W2", "W2b", NE * 2048, D),
             ("b2", "b2b", NE, D)]
    cbuf = p.sb("wp_cast", [128, 2 * 4096], BF16, st)
    tiles = []
    for src, dst, R, C in casts:
        nb = max(1, R // 128)
        pr = min(128, R)
        for a in range(nb):
            tiles.append((src, dst, a, pr, C))
    g0 = [p.val("wpg0"), p.val("wpg1")]
    s0 = [p.val("wps0"), p.val("wps1")]
    n_slot = [(len(tiles) + 1) // 2, len(tiles) // 2]

    @block.gpsimd
    def _(g):
        for j, (src, dst, a, pr, C) in enumerate(tiles):
            s_ap = _flat2d(p.T(src))
            sl = j % 2
            if j >= 2:
                g.wait_ge(p.sem("wps%d" % sl), s0[sl] + 16 * (j // 2))
            src_t = pt(s_ap, a * 128, 1, 0, C)[:, 0, :] if pr == 128 else s_ap
            g.dma_start(out=cbuf[0:pr, sl * 4096:sl * 4096 + C],
                        in_=src_t).then_inc(p.sem("wpg%d" % sl), 16)
    p.semv["wpg0"] = g0[0] + 16 * n_slot[0]
    p.semv["wpg1"] = g0[1] + 16 * n_slot[1]

    @block.sync
    def _(sy):
        for j, (src, dst, a, pr, C) in enumerate(tiles):
            d_ap = _flat2d(p.T(dst, write=True))
            sl = j % 2
            sy.wait_ge(p.sem("wpg%d" % sl), g0[sl] + 16 * (j // 2 + 1))
            dst_t = pt(d_ap, a * 128, 1, 0, C)[:, 0, :] if pr == 128 else d_ap
            sy.dma_start(out=dst_t, in_=cbuf[0:pr, sl * 4096:sl * 4096 + C]
                         ).then_inc(p.sem("wps%d" % sl), 16)
        sy.wait_ge(p.sem("wps0"), s0[0] + 16 * n_slot[0])
        sy.wait_ge(p.sem("wps1"), s0[1] + 16 * n_slot[1])
    p.semv["wps0"] = s0[0] + 16 * n_slot[0]
    p.semv["wps1"] = s0[1] + 16 * n_slot[1]
    st.close()
    barrier(p, block)


def emit_wprep_transpose(p, block, ctx):
    st = contextlib.ExitStack()
    jobs = [("Wq", "WqT", D, D, False), ("Wk", "WkT", D, D, False),
            ("Wv", "WvT", D, D, False), ("Wo", "WoT", D, D, False),
            ("w1_w", "w1T", DHID, D, False), ("w2_w", "w2T", DHID, D, False),
            ("h1_w", "h1T", DHID, DHID, False), ("h2_w", "h2T", DHID, DHID, False),
            ("w3_w", "w3Ts", D, DHID, True)]
    ibuf = p.sb("wp_tin", [128, 2 * 4096], BF16, st)
    obuf = p.sb("wp_tout", [128, 2 * 4096], BF16, st)
    w3s = p.sb("wp_w3s", [128, 8], F32, st)
    pst = p.ps("wp_pst", [128, 2048], BF16, st)

    blocks_l = []
    for src, dst, R, C, scl in jobs:
        for a in range(R // 128):
            blocks_l.append((src, dst, a, C, scl))
    ngrp_tot = sum(C // 1024 for _, _, _, C, _ in blocks_l)

    g0 = [p.val("wtg0"), p.val("wtg1")]; v0 = p.val("wtv"); t0 = p.val("wtt")
    e0 = p.val("wte"); s0 = [p.val("wts0"), p.val("wts1")]; sc0 = p.val("wtc")

    w3stage = p.sb("wp_w3stage", [8, 128], F32, st)
    ps_w3 = p.ps("wp_psw3", [128, 8], F32, st)

    @block.sync
    def _(sy):
        sy.dma_start(out=w3stage[:], in_=rr(p.T("w3_s"), "(a pp) -> a pp", pp=128)
                     ).then_inc(p.sem("wtc"), 16)

    @block.tensor
    def _(t):
        t.wait_ge(p.sem("wtc"), sc0 + 16)
        t.transpose(ps_w3[:], w3stage[:], ctx["eyef"][0:8, 0:8]).then_inc(p.sem("wtc"), 1)

    @block.vector
    def _(v):
        v.wait_ge(p.sem("wtc"), sc0 + 17)
        v.tensor_copy(w3s[:], ps_w3[:]).then_inc(p.sem("wtc"), 1)
    p.semv["wtc"] = sc0 + 18

    @block.gpsimd
    def _(g):
        for j, (src, dst, a, C, scl) in enumerate(blocks_l):
            sl = j % 2
            if j >= 2:
                g.wait_ge(p.sem("wtt"), t0 + _grp_end(blocks_l, j - 2))
            g.dma_start(out=ibuf[:, sl * 4096:sl * 4096 + C],
                        in_=pt(p.T(src), a * 128, 1, 0, C)[:, 0, :]).then_inc(p.sem("wtg%d" % sl), 16)
    p.semv["wtg0"] = g0[0] + 16 * ((len(blocks_l) + 1) // 2)
    p.semv["wtg1"] = g0[1] + 16 * (len(blocks_l) // 2)

    @block.vector
    def _(v):
        for j, (src, dst, a, C, scl) in enumerate(blocks_l):
            if not scl:
                continue
            v.wait_ge(p.sem("wtg%d" % (j % 2)), g0[j % 2] + 16 * (j // 2 + 1))
            v.wait_ge(p.sem("wtc"), sc0 + 18)
            v.tensor_scalar_mul(ibuf[:, (j % 2) * 4096:(j % 2) * 4096 + C],
                                ibuf[:, (j % 2) * 4096:(j % 2) * 4096 + C],
                                w3s[:, a:a + 1]).then_inc(p.sem("wtv"), 1)
    nscl = sum(1 for b in blocks_l if b[4])
    p.semv["wtv"] = v0 + nscl

    grp = 0

    @block.tensor
    def _(t):
        nonlocal grp
        nv = 0
        t.wait_ge(p.sem("gld"), p.val("gld"))
        for j, (src, dst, a, C, scl) in enumerate(blocks_l):
            t.wait_ge(p.sem("wtg%d" % (j % 2)), g0[j % 2] + 16 * (j // 2 + 1))
            if scl:
                nv += 1
                t.wait_ge(p.sem("wtv"), v0 + nv)
            for gg in range(C // 1024):
                if grp >= 2:
                    t.wait_ge(p.sem("wte"), e0 + grp - 1)
                for b in range(8):
                    cc = gg * 1024 + b * 128
                    ins = t.transpose(pst[:, (grp % 2) * 1024 + b * 128:(grp % 2) * 1024 + (b + 1) * 128],
                                      ibuf[:, (j % 2) * 4096 + cc:(j % 2) * 4096 + cc + 128],
                                      ctx["eyeb"][:])
                    if b == 7:
                        ins.then_inc(p.sem("wtt"), 1)
                grp += 1
    p.semv["wtt"] = t0 + ngrp_tot

    @block.scalar
    def _(s):
        grp2 = 0
        for j, (src, dst, a, C, scl) in enumerate(blocks_l):
            for gg in range(C // 1024):
                s.wait_ge(p.sem("wtt"), t0 + grp2 + 1)
                if gg == 0 and j >= 2:
                    s.wait_ge(p.sem("wts%d" % (j % 2)), s0[j % 2] + 16 * (j // 2))
                s.copy(obuf[:, (j % 2) * 4096 + gg * 1024:(j % 2) * 4096 + (gg + 1) * 1024],
                       pst[:, (grp2 % 2) * 1024:(grp2 % 2 + 1) * 1024]).then_inc(p.sem("wte"), 1)
                grp2 += 1
    p.semv["wte"] = e0 + ngrp_tot

    @block.sync
    def _(sy):
        egrp = 0
        for j, (src, dst, a, C, scl) in enumerate(blocks_l):
            sl = j % 2
            egrp += C // 1024
            sy.wait_ge(p.sem("wte"), e0 + egrp)
            d = p.T(dst, write=True)
            sy.dma_start(out=rr(d, "(it pp) c -> pp it c", pp=128)[:, :, a * 128:(a + 1) * 128],
                         in_=rr(obuf[:, sl * 4096:sl * 4096 + C], "pp (it c) -> pp it c", c=128)
                         ).then_inc(p.sem("wts%d" % sl), 16)
        sy.wait_ge(p.sem("wts0"), s0[0] + 16 * ((len(blocks_l) + 1) // 2))
        sy.wait_ge(p.sem("wts1"), s0[1] + 16 * (len(blocks_l) // 2))
    p.semv["wts0"] = s0[0] + 16 * ((len(blocks_l) + 1) // 2)
    p.semv["wts1"] = s0[1] + 16 * (len(blocks_l) // 2)
    st.close()
    barrier(p, block)


def _grp_end(blocks_l, upto_j):
    tot = 0
    for j, (_, _, _, C, _) in enumerate(blocks_l):
        if j > upto_j:
            break
        tot += C // 1024
    return tot


def _reg():
    return {
        "wcast": (lambda p, b, c: emit_wprep_cast(p, b),
                  ["W_in", "W_x", "W_dt", "W_out", "W1", "W2", "b2"],
                  ["Winb", "Wxb", "Wdtb", "Woutb", "W1b", "W2b", "b2b"]),
        "wtrans": (emit_wprep_transpose_,
                   ["Wq", "Wk", "Wv", "Wo", "w1_w", "w2_w", "h1_w", "h2_w", "w3_w", "w3_s"],
                   ["WqT", "WkT", "WvT", "WoT", "w1T", "w2T", "h1T", "h2T", "w3Ts"]),
        "p2": (emit_p2, ["x", "Winb"],
               ["xn_f", "xnT_b", "xiT_b", "zT_b", "gates_f", "gatesT_b"]),
        "p3": (emit_p3, ["xiT_b"], ["uT_b"]),
        "p4": (emit_p4, ["uT_b", "Wxb", "Wdtb"], ["dtT_b", "BmT_b", "CmT_b"]),
        "p5": (emit_p5, ["dtT_b", "uT_b", "BmT_b", "CmT_b", "zT_b", "Woutb"], ["yW_f"]),
        "p6": (emit_p6, ["x", "xn_f", "xnT_b", "gates_f", "gatesT_b", "yW_f", "W1b", "W2b", "b2b"],
               ["x1_f", "xn1T_b"]),
        "p56": (emit_p56, ["dtT_b", "uT_b", "BmT_b", "CmT_b", "zT_b", "Woutb",
                           "xnT_b", "gates_f", "gatesT_b", "W1b", "W2b", "b2b"],
                ["yW_f", "moe_f"]),
        "p6b": (emit_p6b, ["x", "xn_f", "moe_f", "yW_f"], ["x1_f", "xn1T_b"]),
        "p7": (emit_p7, ["xn1T_b", "x1_f", "WqT", "WkT", "WvT", "WoT"], ["partial_f"]),
        "p8": (emit_p8, ["xn1T_b", "w1T", "w2T"], ["aT_b"]),
        "p9": (emit_p9, ["aT_b", "h1T"], ["bT_b"]),
        "p10": (emit_p10, ["bT_b", "h2T"], ["cT_b"]),
        "p11": (emit_p11, ["cT_b", "w3Ts", "partial_f"], ["out"]),
    }


def emit_wprep_transpose_(p, block, ctx):
    emit_wprep_transpose(p, block, ctx)


PREP_READS = ["gamma0", "beta0", "gammaF", "betaF", "bg", "b1", "b_conv", "b_dt",
              "Dskip", "W_conv", "A_log", "mask_prev", "mask_cur", "w1_s", "w2_s",
              "h1_s", "h2_s", "eye", "Wg", "sel16"]


def _build(names, fused):
    p = Prog(fused)
    reg = _reg()
    with p.stack:
        wr = set()
        for nm in names:
            wr |= set(reg[nm][2])
        for nm in names:
            for t in reg[nm][1]:
                if t not in wr:
                    p.T(t)
        for t in PREP_READS:
            p.T(t)
        for nm in names:
            for t in reg[nm][2]:
                p.T(t, write=True)
        with p.nc.Block() as block:
            ctx = emit_prep(p, block)
            for nm in names:
                reg[nm][0](p, block, ctx)
    return p


_NAME_MAP = {
    "W_in": "L0_attn_W_in", "W_conv": "L0_attn_W_conv", "b_conv": "L0_attn_b_conv",
    "W_x": "L0_attn_W_x", "W_dt": "L0_attn_W_dt", "b_dt": "L0_attn_b_dt",
    "A_log": "L0_attn_A_log", "Dskip": "L0_attn_Dskip", "W_out": "L0_attn_W_out",
    "gamma0": "L0_attn_gamma", "beta0": "L0_attn_beta",
    "Wg": "L0_ffn_Wg", "bg": "L0_ffn_bg", "W1": "L0_ffn_W1", "b1": "L0_ffn_b1",
    "W2": "L0_ffn_W2", "b2": "L0_ffn_b2", "gammaF": "L0_ffn_gamma", "betaF": "L0_ffn_beta",
    "Wq": "L1_attn_Wq", "Wk": "L1_attn_Wk", "Wv": "L1_attn_Wv", "Wo": "L1_attn_Wo",
    "w1_w": "L1_ffn_w1_w", "w1_s": "L1_ffn_w1_s", "w2_w": "L1_ffn_w2_w", "w2_s": "L1_ffn_w2_s",
    "h1_w": "L1_ffn_h1_w", "h1_s": "L1_ffn_h1_s", "h2_w": "L1_ffn_h2_w", "h2_s": "L1_ffn_h2_s",
    "w3_w": "L1_ffn_w3_w", "w3_s": "L1_ffn_w3_s",
}


def _host_consts():
    o = np.arange(128)[:, None]
    q = np.arange(128)[None, :]
    sel = np.zeros((16, 16 * 128), dtype=np.float32)
    for n in range(16):
        sel[n, n * 128:(n + 1) * 128] = 1.0
    return {
        "eye": np.eye(128, dtype=np.float32),
        "mask_cur": np.where(q <= o, 0.0, NEG).astype(np.float32),
        "mask_prev": np.where(q > o, 0.0, NEG).astype(np.float32),
        "sel16": sel,
    }


def _stage_cores(inputs):
    consts = _host_consts()
    x = np.ascontiguousarray(inputs["L0_attn_x"])
    cores = []
    for b in range(8):
        d = {"x": x[b]}
        for k, v in _NAME_MAP.items():
            d[k] = np.ascontiguousarray(inputs[v])
        d.update(consts)
        cores.append(d)
    return cores


GROUPS = [["wcast"], ["wtrans"], ["p2"], ["p3"], ["p4"], ["p56"],
          ["p6b"], ["p7"], ["p8"], ["p9"], ["p10"], ["p11"]]


def kernel(**inputs):
    hb = _stage_cores(inputs)
    if FUSED:
        p = _build([g for grp in GROUPS for g in grp], True)
        need = sorted(p.reads - set(SCRATCH_SHAPES))
        in_maps = [{n: hb[c][n] for n in need} for c in range(8)]
        res = run_bass_kernel_spmd(p.nc, in_maps, core_ids=list(range(8)))
        return np.stack([np.asarray(r["out"], dtype=np.float32) for r in res.results])
    for grp in GROUPS:
        p = _build(grp, False)
        need = sorted(p.reads - p.writes)
        in_maps = [{n: hb[c][n] for n in need} for c in range(8)]
        res = run_bass_kernel_spmd(p.nc, in_maps, core_ids=list(range(8)))
        for c in range(8):
            hb[c].update(res.results[c])
    return np.stack([np.asarray(hb[c]["out"], dtype=np.float32) for c in range(8)])


def emit_p2(p, block, ctx):
    st = contextlib.ExitStack()
    xt = p.sb("p2_xt", [128, 4096], F32, st)
    st6 = p.sb("p2_st6", [128, 4 * 12], F32, st)
    mv = p.sb("p2_mv", [128, 8], F32, st)
    rstd = p.sb("p2_rstd", [128, 4], F32, st)
    st6b = p.sb("p2_st6b", [128, 4 * 12], F32, st)
    mvb = p.sb("p2_mvb", [128, 8], F32, st)
    rstdb = p.sb("p2_rstdb", [128, 4], F32, st)
    xn_sb = p.sb("p2_xn", [128, 4096], F32, st)
    xnbf = p.sb("p2_xnbf", [128, 4096], BF16, st)
    tmp4 = p.sb("p2_tmp4", [128, 4096], F32, st)
    h1bf = p.sb("p2_h1bf", [128, 4096], BF16, st)
    xnT = p.sb("p2_xnT", [128, 8 * 512], BF16, st)
    h1T = p.sb("p2_h1T", [128, 8 * 512], BF16, st)
    Winb_sb = p.sb("p2_Win", [128, 8 * 4096], BF16, st)
    xz2 = p.sb("p2_xz2", [128, 2 * 512], BF16, st)
    l4 = p.sb("p2_l4", [128, 32], F32, st)
    eq1 = p.sb("p2_eq1", [128, 32], F32, st)
    msk4 = p.sb("p2_msk", [128, 32], F32, st)
    m1_4 = p.sb("p2_m1", [128, 4], F32, st)
    m2_4 = p.sb("p2_m2", [128, 4], F32, st)
    d4 = p.sb("p2_d4", [128, 4], F32, st)
    w2g = p.sb("p2_w2g", [128, 4], F32, st)
    w1g = p.sb("p2_w1g", [128, 4], F32, st)
    gt4 = p.sb("p2_gt4", [128, 32], F32, st)
    gates_sb = p.sb("p2_gates", [128, 32], F32, st)
    gbf = p.sb("p2_gbf", [128, 32], BF16, st)
    gT = p.sb("p2_gT", [8, 512], BF16, st)
    z3 = p.sb("p2_z3", [128, 48], BF16, st)
    ps_tr = p.ps("p2_pstr", [128, 2 * 1024], BF16, st)
    ps_mm = p.ps("p2_psmm", [128, 2 * 512], F32, st)
    ps_g = p.ps("p2_psg", [128, 32], F32, st)
    ps_gt = p.ps("p2_psgt", [8, 512], BF16, st)
    xnTf = p.sb("p2_xnTf", [128, 1024], F32, st)
    ps_trf = p.ps("p2_pstrf", [128, 1024], F32, st)

    bV = p.val("p2v"); bS = p.val("p2s"); bG = p.val("p2g")
    bT = p.val("p2t"); bTE = p.val("p2te"); bM = p.val("p2m"); bE = p.val("p2e")
    bGM = p.val("p2gm"); bGT = p.val("p2gt"); bCK = p.val("p2ck"); bXF = p.val("p2xf")
    bTF = p.val("p2tf"); bTFE = p.val("p2tfe")
    bXZ = [p.val("p2xz0"), p.val("p2xz1")]
    bX0 = p.val("p2x0"); bSA = p.val("p2sA"); bW = p.val("p2w"); bSS = p.val("p2ss")
    NV, NS = 8, 3
    stores_per_chunk = [5 if cc == 0 else 4 for cc in range(NCH)]
    cum_st = np.cumsum([0] + stores_per_chunk)

    def V(c, k):
        return bV + c * NV + k

    def SS(c, k):
        return bS + c * NS + k

    @block.sync
    def _(sy):
        sy.dma_start(out=rr(Winb_sb[:], "p (kt n) -> p kt n", kt=8),
                     in_=pt(p.T("Winb"), 0, 8, 0, 4096)).then_inc(p.sem("p2w"), 16)
        for c in range(NCH):
            if c >= 1:
                sy.wait_ge(p.sem("p2xf"), bXF + c)
            sy.dma_start(out=rr(xt[:], "p (t d) -> p t d", t=4),
                         in_=pt(p.T("x"), c * 512, 4, 0, 1024)).then_inc(p.sem("p2x0"), 16)
    p.semv["p2w"] = bW + 16
    p.semv["p2x0"] = bX0 + 16 * NCH

    @block.vector
    def _(v):
        ssn = bSS
        for c in range(NCH):
            xc = xt[:]
            v.wait_ge(p.sem("p2x0"), bX0 + 16 * (c + 1))
            if c >= 1:
                v.wait_ge(p.sem("p2ck"), bCK + c)
            for t in range(4):
                v.bn_stats(st6[:, t * 12:t * 12 + 6], xc[:, t * 1024:t * 1024 + 512])
                ins = v.bn_stats(st6[:, t * 12 + 6:t * 12 + 12], xc[:, t * 1024 + 512:t * 1024 + 1024])
            ins.then_inc(p.sem("p2ss"), 1); ssn += 1
            v.wait_ge(p.sem("p2ss"), ssn)
            for t in range(4):
                ins = v.bn_aggr(mv[:, 2 * t:2 * t + 2], st6[:, t * 12:t * 12 + 12])
            ins.then_inc(p.sem("p2v"), 1)
            v.wait_ge(p.sem("p2s"), SS(c, 1))
            for t in range(4):
                ins = v.tensor_scalar(xn_sb[:, t * 1024:(t + 1) * 1024], xc[:, t * 1024:(t + 1) * 1024],
                                      mv[:, 2 * t:2 * t + 1], rstd[:, t:t + 1], ALU.subtract, ALU.mult)
            ins.then_inc(p.sem("p2xf"), 1)
            v.wait_ge(p.sem("p2xf"), bXF + c + 1)
            ins = v.tensor_copy(xnbf[:], xn_sb[:])
            ins.then_inc(p.sem("p2v"), 1)
            for t in range(4):
                v.bn_stats(st6b[:, t * 12:t * 12 + 6], xn_sb[:, t * 1024:t * 1024 + 512])
                ins = v.bn_stats(st6b[:, t * 12 + 6:t * 12 + 12], xn_sb[:, t * 1024 + 512:t * 1024 + 1024])
            ins.then_inc(p.sem("p2ss"), 1); ssn += 1
            v.wait_ge(p.sem("p2ss"), ssn)
            for t in range(4):
                ins = v.bn_aggr(mvb[:, 2 * t:2 * t + 2], st6b[:, t * 12:t * 12 + 12])
            ins.then_inc(p.sem("p2v"), 1)
            v.wait_ge(p.sem("p2v"), V(c, 3))
            for t in range(4):
                ins = v.tensor_single_scalar(tmp4[:, t * 1024:(t + 1) * 1024],
                                             xn_sb[:, t * 1024:(t + 1) * 1024],
                                             mvb[:, 2 * t:2 * t + 1], ALU.subtract)
            ins.then_inc(p.sem("p2v"), 1)
            v.wait_ge(p.sem("p2g"), bG + c + 1)
            v.wait_ge(p.sem("p2s"), SS(c, 2))
            for t in range(4):
                ins = v.scalar_tensor_tensor(h1bf[:, t * 1024:(t + 1) * 1024],
                                             tmp4[:, t * 1024:(t + 1) * 1024], rstdb[:, t:t + 1],
                                             ctx["b0bc"][:], ALU.mult, ALU.add)
            ins.then_inc(p.sem("p2v"), 1)
            for t in range(4):
                for which in range(2):
                    g = c * 8 + t * 2 + which
                    v.wait_ge(p.sem("p2t"), bT + g + 1)
                    dst = xnT if which == 0 else h1T
                    v.tensor_copy(
                        rr(dst[:], "p (dt s) -> p dt s", s=512)[:, :, t * 128:(t + 1) * 128],
                        rr(ps_tr[:, (g % 2) * 1024:(g % 2 + 1) * 1024], "p (dt s) -> p dt s", s=128)
                    ).then_inc(p.sem("p2te"), 1)
            for t in range(4):
                v.wait_ge(p.sem("p2tf"), bTF + c * 4 + t + 1)
                if c * 4 + t >= 1:
                    v.wait_ge(p.sem("p2gm"), bGM + c * 4 + t)
                v.tensor_copy(xnTf[:], ps_trf[:]).then_inc(p.sem("p2tfe"), 1)
            def vmark(i):
                p.inc(i, "p2vc", 1)

            def vjoin():
                v.wait_ge(p.sem("p2vc"), p.val("p2vc"))
            v.wait_ge(p.sem("p2gm"), bGM + c * 4 + 4)
            for t in range(4):
                ins = v.scalar_tensor_tensor(l4[:, 8 * t:8 * t + 8], ps_g[:, 8 * t:8 * t + 8], 1.0,
                                             ctx["bgbc"][:], ALU.mult, ALU.add)
            vmark(ins); vjoin()
            for t in range(4):
                ins = v.reduce_max(m1_4[:, t:t + 1], l4[:, 8 * t:8 * t + 8], axis=AX.X)
            vmark(ins); vjoin()
            for t in range(4):
                ins = v.tensor_single_scalar(eq1[:, 8 * t:8 * t + 8], l4[:, 8 * t:8 * t + 8],
                                             m1_4[:, t:t + 1], ALU.is_equal)
            vmark(ins); vjoin()
            for t in range(4):
                ins = v.scalar_tensor_tensor(msk4[:, 8 * t:8 * t + 8], eq1[:, 8 * t:8 * t + 8], -1e30,
                                             l4[:, 8 * t:8 * t + 8], ALU.mult, ALU.add)
            vmark(ins); vjoin()
            for t in range(4):
                ins = v.reduce_max(m2_4[:, t:t + 1], msk4[:, 8 * t:8 * t + 8], axis=AX.X)
            vmark(ins); vjoin()
            for t in range(4):
                ins = v.tensor_single_scalar(gt4[:, 8 * t:8 * t + 8], msk4[:, 8 * t:8 * t + 8],
                                             m2_4[:, t:t + 1], ALU.is_equal)
            vmark(ins)
            v.tensor_sub(d4[:], m2_4[:], m1_4[:]).then_inc(p.sem("p2v"), 1)
            v.wait_ge(p.sem("p2s"), SS(c, 3))
            ins = v.tensor_scalar(w1g[:], w2g[:], -1.0, 1.0, ALU.mult, ALU.add)
            vmark(ins); vjoin()
            for t in range(4):
                ins = v.tensor_scalar_mul(eq1[:, 8 * t:8 * t + 8], eq1[:, 8 * t:8 * t + 8], w1g[:, t:t + 1])
            vmark(ins); vjoin()
            for t in range(4):
                ins = v.scalar_tensor_tensor(gates_sb[:, 8 * t:8 * t + 8], gt4[:, 8 * t:8 * t + 8],
                                             w2g[:, t:t + 1], eq1[:, 8 * t:8 * t + 8], ALU.mult, ALU.add)
            vmark(ins); vjoin()
            v.tensor_copy(gbf[:], gates_sb[:]).then_inc(p.sem("p2v"), 1)
            v.wait_ge(p.sem("p2gt"), bGT + c + 1)
            v.tensor_copy(gT[:], ps_gt[:]).then_inc(p.sem("p2v"), 1)
    p.semv["p2v"] = bV + NCH * NV
    p.semv["p2xf"] = bXF + NCH
    p.semv["p2te"] = bTE + NCH * 8
    p.semv["p2ss"] = bSS + NCH * 2
    p.semv["p2tfe"] = bTFE + NCH * 4

    @block.scalar
    def _(s):
        for c in range(NCH):
            s.wait_ge(p.sem("p2v"), V(c, 1))
            for t in range(4):
                ins = s.activation(rstd[:, t:t + 1], mv[:, 2 * t + 1:2 * t + 2], AF.Ln, bias=EPS)
            p.inc(ins, "p2sc", 1)
            s.wait_ge(p.sem("p2sc"), p.val("p2sc"))
            for t in range(4):
                ins = s.activation(rstd[:, t:t + 1], rstd[:, t:t + 1], AF.Exp, scale=-0.5)
            ins.then_inc(p.sem("p2s"), 1)
            s.wait_ge(p.sem("p2v"), V(c, 3))
            for t in range(4):
                ins = s.activation(rstdb[:, t:t + 1], mvb[:, 2 * t + 1:2 * t + 2], AF.Ln, bias=EPS)
            p.inc(ins, "p2sc", 1)
            s.wait_ge(p.sem("p2sc"), p.val("p2sc"))
            for t in range(4):
                ins = s.activation(rstdb[:, t:t + 1], rstdb[:, t:t + 1], AF.Exp, scale=-0.5)
            ins.then_inc(p.sem("p2s"), 1)
            s.wait_ge(p.sem("p2v"), V(c, 6))
            s.activation(w2g[:], d4[:], AF.Sigmoid).then_inc(p.sem("p2s"), 1)
            for och in range(32):
                m = c * 32 + och
                s.wait_ge(p.sem("p2m"), bM + m + 1)
                if m >= 2:
                    s.wait_ge(p.sem("p2xz%d" % (m % 2)), bXZ[m % 2] + 16 * (m // 2))
                if och < 16:
                    s.copy(xz2[:, (m % 2) * 512:(m % 2 + 1) * 512],
                           ps_mm[:, (och % 2) * 512:(och % 2 + 1) * 512]).then_inc(p.sem("p2e"), 1)
                else:
                    s.activation(xz2[:, (m % 2) * 512:(m % 2 + 1) * 512],
                                 ps_mm[:, (och % 2) * 512:(och % 2 + 1) * 512],
                                 AF.Silu).then_inc(p.sem("p2e"), 1)
    p.semv["p2s"] = bS + NCH * NS
    p.semv["p2e"] = bE + NCH * 32

    @block.gpsimd
    def _(g):
        for c in range(NCH):
            g.wait_ge(p.sem("p2v"), V(c, 4))
            for t in range(4):
                ins = g.tensor_mul(tmp4[:, t * 1024:(t + 1) * 1024], tmp4[:, t * 1024:(t + 1) * 1024],
                                   ctx["g0bc"][:])
            ins.then_inc(p.sem("p2g"), 1)
            if c == 0:
                g.memset(z3[:], 0.0).then_inc(p.sem("p2zz"), 1)
                g.wait_ge(p.sem("p2zz"), p.val("p2zz") + 1)
                p.semv["p2zz"] += 1
                g.dma_start(out=rr(p.T("xiT_b", write=True), "(dt pp) s -> pp dt s", pp=128)[:, :, 0:3],
                            in_=rr(z3[:], "p (dt s) -> p dt s", s=3)).then_inc(p.sem("p2sA"), 16)
            g.wait_ge(p.sem("p2v"), V(c, 2))
            g.dma_start(out=pt(p.T("xn_f", write=True), c * 512, 4, 0, 1024),
                        in_=rr(xn_sb[:], "p (t d) -> p t d", t=4)).then_inc(p.sem("p2sA"), 16)
            g.wait_ge(p.sem("p2te"), bTE + c * 8 + 8)
            g.dma_start(out=rr(p.T("xnT_b", write=True), "(dt pp) s -> pp dt s", pp=128)[:, :, c * 512:(c + 1) * 512],
                        in_=rr(xnT[:], "p (dt s) -> p dt s", s=512)).then_inc(p.sem("p2sA"), 16)
            for och in range(32):
                m = c * 32 + och
                g.wait_ge(p.sem("p2e"), bE + m + 1)
                if och < 16:
                    dst = rr(p.T("xiT_b", write=True), "(dt pp) s -> pp dt s", pp=128
                             )[:, och, 3 + c * 512:3 + (c + 1) * 512]
                else:
                    dst = rr(p.T("zT_b", write=True), "(dt pp) s -> pp dt s", pp=128
                             )[:, och - 16, c * 512:(c + 1) * 512]
                g.dma_start(out=dst, in_=xz2[:, (m % 2) * 512:(m % 2 + 1) * 512]
                            ).then_inc(p.sem("p2xz%d" % (m % 2)), 16)
            g.wait_ge(p.sem("p2v"), V(c, 7))
            g.dma_start(out=pt(p.T("gates_f", write=True), c * 512, 4, 0, 8),
                        in_=rr(gates_sb[:], "p (t e) -> p t e", e=8)).then_inc(p.sem("p2sA"), 16)
            g.wait_ge(p.sem("p2v"), V(c, 8))
            g.dma_start(out=p.T("gatesT_b", write=True)[0:8, c * 512:(c + 1) * 512],
                        in_=gT[:]).then_inc(p.sem("p2sA"), 16)
            g.wait_ge(p.sem("p2sA"), bSA + 16 * int(cum_st[c + 1]))
            g.wait_ge(p.sem("p2xz0"), bXZ[0] + 16 * (c + 1) * 16)
            g.wait_ge(p.sem("p2xz1"), bXZ[1] + 16 * (c + 1) * 16)
            g.sem_inc(p.sem("p2ck"), 1)
    p.semv["p2g"] = bG + NCH
    p.semv["p2sA"] = bSA + 16 * int(cum_st[NCH])
    p.semv["p2xz0"] = bXZ[0] + 16 * NCH * 16
    p.semv["p2xz1"] = bXZ[1] + 16 * NCH * 16
    p.semv["p2ck"] = bCK + NCH

    @block.tensor
    def _(t_):
        t_.wait_ge(p.sem("p2w"), bW + 16)
        for c in range(NCH):
            for t in range(4):
                for which in range(2):
                    g = c * 8 + t * 2 + which
                    if t == 0:
                        t_.wait_ge(p.sem("p2v"), V(c, 2 if which == 0 else 5))
                    if g >= 2:
                        t_.wait_ge(p.sem("p2te"), bTE + g - 1)
                    src = xnbf if which == 0 else h1bf
                    for dt in range(8):
                        ins = t_.transpose(
                            ps_tr[:, (g % 2) * 1024 + dt * 128:(g % 2) * 1024 + (dt + 1) * 128],
                            src[:, t * 1024 + dt * 128:t * 1024 + (dt + 1) * 128],
                            ctx["eyeb"][:])
                        if dt == 7:
                            ins.then_inc(p.sem("p2t"), 1)
            if c >= 1:
                t_.wait_ge(p.sem("p2v"), V(c - 1, 6))
            for t in range(4):
                n = c * 4 + t
                if n >= 1:
                    t_.wait_ge(p.sem("p2tfe"), bTFE + n)
                for dt in range(8):
                    ins = t_.transpose(ps_trf[:, dt * 128:(dt + 1) * 128],
                                       xn_sb[:, t * 1024 + dt * 128:t * 1024 + (dt + 1) * 128],
                                       ctx["eyef"][:])
                    if dt == 7:
                        ins.then_inc(p.sem("p2tf"), 1)
                t_.wait_ge(p.sem("p2tfe"), bTFE + n + 1)
                for kt in range(8):
                    ins = t_.matmul(ps_g[:, 8 * t:8 * t + 8],
                                    xnTf[:, kt * 128:(kt + 1) * 128],
                                    ctx["WgF"][:, kt * 8:(kt + 1) * 8],
                                    start=(kt == 0), stop=(kt == 7))
                    if kt == 7:
                        ins.then_inc(p.sem("p2gm"), 1)
            for och in range(32):
                m = c * 32 + och
                if m >= 2:
                    t_.wait_ge(p.sem("p2e"), bE + m - 1)
                for kt in range(8):
                    ins = t_.matmul(ps_mm[:, (och % 2) * 512:(och % 2 + 1) * 512],
                                    Winb_sb[:, kt * 4096 + och * 128:kt * 4096 + (och + 1) * 128],
                                    h1T[:, kt * 512:(kt + 1) * 512],
                                    start=(kt == 0), stop=(kt == 7))
                    if kt == 7:
                        ins.then_inc(p.sem("p2m"), 1)
            t_.wait_ge(p.sem("p2v"), V(c, 7))
            if c >= 1:
                t_.wait_ge(p.sem("p2v"), V(c - 1, 8))
            for t in range(4):
                ins = t_.transpose(ps_gt[0:8, t * 128:(t + 1) * 128],
                                   gbf[:, 8 * t:8 * t + 8], ctx["eyeb"][:])
                if t == 3:
                    ins.then_inc(p.sem("p2gt"), 1)
    p.semv["p2t"] = bT + NCH * 8
    p.semv["p2tf"] = bTF + NCH * 4
    p.semv["p2m"] = bM + NCH * 32
    p.semv["p2gm"] = bGM + NCH * 4
    p.semv["p2gt"] = bGT + NCH

    st.close()
    barrier(p, block)


def emit_p3(p, block, ctx):
    st = contextlib.ExitStack()
    xib = p.sb("p3_xib", [128, 16 * 515], BF16, st)
    acc0 = p.sb("p3_acc0", [128, 16 * 512], F32, st)
    acc1 = p.sb("p3_acc1", [128, 16 * 512], F32, st)
    u_sb = p.sb("p3_u", [128, 16 * 512], BF16, st)

    bL = p.val("p3l"); bV = p.val("p3v"); bS = p.val("p3s")
    bST = p.val("p3st"); bCK = p.val("p3ck")

    def xis(dt, k):
        return xib[:, dt * 515 + k:dt * 515 + k + 512]

    @block.sync
    def _(sy):
        for c in range(NCH):
            if c >= 1:
                sy.wait_ge(p.sem("p3ck"), bCK + c)
            sy.dma_start(out=rr(xib[:], "p (dt s) -> p dt s", s=515),
                         in_=rr(p.T("xiT_b"), "(dt pp) s -> pp dt s", pp=128)[:, :, c * 512:c * 512 + 515]
                         ).then_inc(p.sem("p3l"), 16)
    p.semv["p3l"] = bL + 16 * NCH

    @block.vector
    def _(v):
        nv = bV
        for c in range(NCH):
            v.wait_ge(p.sem("p3l"), bL + 16 * (c + 1))
            if c >= 1:
                v.wait_ge(p.sem("p3s"), bS + c)
            for k in range(4):
                src_acc = [acc0, acc1][(k + 1) % 2]
                dst_acc = [acc0, acc1][k % 2]
                for dt in range(16):
                    wck = ctx["Wc"][:, dt * 4 + k:dt * 4 + k + 1]
                    if k == 0:
                        ins = v.tensor_scalar_mul(dst_acc[:, dt * 512:(dt + 1) * 512], xis(dt, 0), wck)
                    else:
                        ins = v.scalar_tensor_tensor(dst_acc[:, dt * 512:(dt + 1) * 512], xis(dt, k),
                                                     wck, src_acc[:, dt * 512:(dt + 1) * 512],
                                                     ALU.mult, ALU.add)
                p.inc(ins, "p3v", 1); nv += 1
                v.wait_ge(p.sem("p3v"), nv)
    p.semv["p3v"] = bV + NCH * 4

    @block.scalar
    def _(s):
        for c in range(NCH):
            s.wait_ge(p.sem("p3v"), bV + c * 4 + 4)
            if c >= 1:
                s.wait_ge(p.sem("p3ck"), bCK + c)
            for dt in range(16):
                ins = s.activation(u_sb[:, dt * 512:(dt + 1) * 512],
                                   acc1[:, dt * 512:(dt + 1) * 512], AF.Silu,
                                   bias=ctx["bconv"][:, dt:dt + 1])
            ins.then_inc(p.sem("p3s"), 1)
    p.semv["p3s"] = bS + NCH

    @block.gpsimd
    def _(g):
        for c in range(NCH):
            g.wait_ge(p.sem("p3s"), bS + c + 1)
            g.dma_start(out=rr(p.T("uT_b", write=True), "(dt pp) s -> pp dt s", pp=128)[:, :, c * 512:(c + 1) * 512],
                        in_=rr(u_sb[:], "p (dt s) -> p dt s", s=512)).then_inc(p.sem("p3st"), 16)
            g.wait_ge(p.sem("p3st"), bST + 16 * (c + 1))
            g.sem_inc(p.sem("p3ck"), 1)
    p.semv["p3st"] = bST + 16 * NCH
    p.semv["p3ck"] = bCK + NCH
    st.close()
    barrier(p, block)


def emit_p4(p, block, ctx):
    st = contextlib.ExitStack()
    ub = p.sb("p4_ub", [128, 16 * 512], BF16, st)
    Wxb_sb = p.sb("p4_Wx", [128, 16 * 96], BF16, st)
    Wdtb_sb = p.sb("p4_Wdt", [64, 16 * 128], BF16, st)
    dtr_bf = p.sb("p4_dtr", [64, 512], BF16, st)
    BC_f = p.sb("p4_BC", [32, 512], BF16, st)
    t_abs = p.sb("p4_tabs", [128, 512], F32, st)
    t_ln = p.sb("p4_tln", [128, 512], F32, st)
    t_rel = p.sb("p4_trel", [128, 512], F32, st)
    dt_sb = p.sb("p4_dt", [128, 16 * 512], BF16, st)
    ps_x = p.ps("p4_psx", [128, 512], F32, st)
    ps_d = p.ps("p4_psd", [128, 2 * 512], F32, st)

    bL = p.val("p4l"); bW = p.val("p4w"); bX = p.val("p4x"); bXE = p.val("p4xe")
    bD = p.val("p4d"); bS = p.val("p4sc")
    bV = p.val("p4v"); bST = p.val("p4st"); bCK = p.val("p4ck")

    @block.sync
    def _(sy):
        sy.dma_start(out=rr(Wxb_sb[:], "p (kt n) -> p kt n", n=96),
                     in_=rr(p.T("Wxb"), "(kt pp) n -> pp kt n", pp=128)).then_inc(p.sem("p4w"), 16)
        sy.dma_start(out=Wdtb_sb[:], in_=p.T("Wdtb")).then_inc(p.sem("p4w"), 16)
        for c in range(NCH):
            if c >= 1:
                sy.wait_ge(p.sem("p4ck"), bCK + c)
            sy.dma_start(out=rr(ub[:], "p (dt s) -> p dt s", s=512),
                         in_=rr(p.T("uT_b"), "(dt pp) s -> pp dt s", pp=128)[:, :, c * 512:(c + 1) * 512]
                         ).then_inc(p.sem("p4l"), 16)
    p.semv["p4w"] = bW + 32
    p.semv["p4l"] = bL + 16 * NCH

    @block.tensor
    def _(t_):
        t_.wait_ge(p.sem("p4w"), bW + 32)
        for c in range(NCH):
            t_.wait_ge(p.sem("p4l"), bL + 16 * (c + 1))
            if c >= 1:
                t_.wait_ge(p.sem("p4xe"), bXE + c)
            for kt in range(16):
                ins = t_.matmul(ps_x[0:96, :], Wxb_sb[:, kt * 96:(kt + 1) * 96],
                                ub[:, kt * 512:(kt + 1) * 512], start=(kt == 0), stop=(kt == 15))
            ins.then_inc(p.sem("p4x"), 1)
            t_.wait_ge(p.sem("p4xe"), bXE + c + 1)
            for dt in range(16):
                n = c * 16 + dt
                if n >= 2:
                    t_.wait_ge(p.sem("p4sc"), bS + 3 * (n - 2) + 1)
                    t_.wait_ge(p.sem("p4v"), bV + 2 * (n - 1))
                t_.matmul(ps_d[:, (n % 2) * 512:(n % 2 + 1) * 512],
                          Wdtb_sb[:, dt * 128:(dt + 1) * 128], dtr_bf[:],
                          start=True, stop=True).then_inc(p.sem("p4d"), 1)
    p.semv["p4x"] = bX + NCH
    p.semv["p4d"] = bD + NCH * 16

    @block.vector
    def _(v):
        nv = bV
        for c in range(NCH):
            v.wait_ge(p.sem("p4x"), bX + c + 1)
            if c >= 1:
                v.wait_ge(p.sem("p4ck"), bCK + c)
                v.wait_ge(p.sem("p4d"), bD + 16 * c)
            v.tensor_copy(dtr_bf[:], ps_x[0:64, :])
            v.tensor_copy(BC_f[:], ps_x[64:96, :]).then_inc(p.sem("p4xe"), 1)
            for dt in range(16):
                n = c * 16 + dt
                v.wait_ge(p.sem("p4sc"), bS + n * 3 + 3)
                ins = v.tensor_scalar(t_rel[:], ps_d[:, (n % 2) * 512:(n % 2 + 1) * 512],
                                      ctx["bdt"][:, dt:dt + 1], 0.0, ALU.add, ALU.max)
                p.inc(ins, "p4v", 1); nv += 1
                v.wait_ge(p.sem("p4v"), nv)
                ins = v.tensor_add(dt_sb[:, dt * 512:(dt + 1) * 512], t_rel[:], t_ln[:])
                p.inc(ins, "p4v", 1); nv += 1
                v.wait_ge(p.sem("p4v"), nv)
    p.semv["p4v"] = bV + NCH * 32
    p.semv["p4xe"] = bXE + NCH

    @block.scalar
    def _(s):
        nsc = bS
        for c in range(NCH):
            for dt in range(16):
                n = c * 16 + dt
                s.wait_ge(p.sem("p4d"), bD + n + 1)
                if n >= 1:
                    s.wait_ge(p.sem("p4v"), bV + 2 * n)
                ins = s.activation(t_abs[:], ps_d[:, (n % 2) * 512:(n % 2 + 1) * 512],
                                   AF.Abs, bias=ctx["bdt"][:, dt:dt + 1])
                p.inc(ins, "p4sc", 1); nsc += 1
                s.wait_ge(p.sem("p4sc"), nsc)
                ins = s.activation(t_abs[:], t_abs[:], AF.Exp, scale=-1.0)
                p.inc(ins, "p4sc", 1); nsc += 1
                s.wait_ge(p.sem("p4sc"), nsc)
                ins = s.activation(t_ln[:], t_abs[:], AF.Ln, bias=1.0)
                p.inc(ins, "p4sc", 1); nsc += 1
    p.semv["p4sc"] = bS + NCH * 48

    @block.gpsimd
    def _(g):
        for c in range(NCH):
            g.wait_ge(p.sem("p4xe"), bXE + c + 1)
            g.dma_start(out=p.T("BmT_b", write=True)[0:16, c * 512:(c + 1) * 512],
                        in_=BC_f[0:16, :]).then_inc(p.sem("p4st"), 16)
            g.dma_start(out=p.T("CmT_b", write=True)[0:16, c * 512:(c + 1) * 512],
                        in_=BC_f[16:32, :]).then_inc(p.sem("p4st"), 16)
            g.wait_ge(p.sem("p4v"), bV + 32 * (c + 1))
            g.dma_start(out=rr(p.T("dtT_b", write=True), "(dt pp) s -> pp dt s", pp=128)[:, :, c * 512:(c + 1) * 512],
                        in_=rr(dt_sb[:], "p (dt s) -> p dt s", s=512)).then_inc(p.sem("p4st"), 16)
            g.wait_ge(p.sem("p4st"), bST + 16 * 3 * (c + 1))
            g.sem_inc(p.sem("p4ck"), 1)
    p.semv["p4st"] = bST + 16 * 3 * NCH
    p.semv["p4ck"] = bCK + NCH
    st.close()
    barrier(p, block)


def emit_p5(p, block, ctx):
    st = contextlib.ExitStack()
    dtc = p.sb("p5_dtc", [128, 16 * 512], BF16, st)
    ub = p.sb("p5_ub", [128, 16 * 512], BF16, st)
    dtu = p.sb("p5_dtu", [128, 16 * 512], BF16, st)
    BbA = p.sb("p5_BbA", [128, 16 * 512], BF16, st)
    CbA = p.sb("p5_CbA", [128, 16 * 512], BF16, st)
    m16 = p.sb("p5_m16", [128, 16 * 512], BF16, st)
    y2 = p.sb("p5_y2", [128, 2 * 512], BF16, st)
    dA2 = p.sb("p5_dA2", [128, 4 * 512], BF16, st)
    be2 = p.sb("p5_be2", [128, 4 * 512], BF16, st)
    h4 = p.sb("p5_h4", [128, 4 * 512], BF16, st)
    zb16 = p.sb("p5_zb16", [128, 16 * 512], BF16, st)
    teb2 = p.sb("p5_teb2", [128, 2 * 512], F32, st)
    yhat = p.sb("p5_yhat", [128, 16 * 512], BF16, st)
    yw_sb = p.sb("p5_yw", [128, 512], F32, st)
    Woutb_sb = p.sb("p5_Wout", [128, 16 * 512], BF16, st)
    ps_w = p.ps("p5_psw", [128, 2 * 512], F32, st)
    hst = ctx["hstate"]

    bDT = p.val("p5dt"); bU = p.val("p5u"); bWL = p.val("p5wlh")
    bDA = p.val("p5da"); bDU = p.val("p5du"); bB = p.val("p5b"); bH = p.val("p5h")
    bHS = p.val("p5hs"); bM = p.val("p5m"); bY = p.val("p5y"); bTC = p.val("p5tc")
    bEB = p.val("p5eb"); bYE = p.val("p5ye")
    bWM = p.val("p5wm"); bWE = p.val("p5we"); bCK = p.val("p5ck")
    bZ = p.val("p5z0")
    bWS = p.val("p5ws0")
    bBB_ = [p.val("p5bb0"), p.val("p5bb1")]
    bCB_ = [p.val("p5cb0"), p.val("p5cb1")]
    G = 256

    @block.sync
    def _(sy):
        for c in range(NCH):
            if c >= 1:
                sy.wait_ge(p.sem("p5ck"), bCK + c)
            sy.dma_start(out=rr(dtc[:], "p (dt s) -> p dt s", s=512),
                         in_=rr(p.T("dtT_b"), "(dt pp) s -> pp dt s", pp=128)[:, :, c * 512:(c + 1) * 512]
                         ).then_inc(p.sem("p5dt"), 16)
            sy.dma_start(out=rr(ub[:], "p (dt s) -> p dt s", s=512),
                         in_=rr(p.T("uT_b"), "(dt pp) s -> pp dt s", pp=128)[:, :, c * 512:(c + 1) * 512]
                         ).then_inc(p.sem("p5u"), 16)
            sy.dma_start(out=rr(zb16[:], "p (dt s) -> p dt s", s=512),
                         in_=rr(p.T("zT_b"), "(dt pp) s -> pp dt s", pp=128)[:, :, c * 512:(c + 1) * 512]
                         ).then_inc(p.sem("p5z0"), 16)
            for n in range(16):
                sl = n % 2
                boff = n * S + c * 512
                sy.dma_start(out=BbA[:, n * 512:(n + 1) * 512],
                             in_=bass.AP(p.T("BmT_b").tensor, boff, [[0, 128], [1, 512]])
                             ).then_inc(p.sem("p5bb%d" % sl), 16)
                sy.dma_start(out=CbA[:, n * 512:(n + 1) * 512],
                             in_=bass.AP(p.T("CmT_b").tensor, boff, [[0, 128], [1, 512]])
                             ).then_inc(p.sem("p5cb%d" % sl), 16)
            for half in range(2):
                hn = c * 2 + half
                if hn >= 1:
                    sy.wait_ge(p.sem("p5wm"), bWM + 4 * hn)
                sy.dma_start(out=rr(Woutb_sb[:], "p (dt n) -> p dt n", n=512),
                             in_=pt(p.T("Woutb"), 0, 16, half * 512, 512)
                             ).then_inc(p.sem("p5wlh"), 16)
                for i in range(4):
                    wn = c * 8 + half * 4 + i
                    sy.wait_ge(p.sem("p5we"), bWE + wn + 1)
                    sy.dma_start(out=pt(p.T("yW_f", write=True), c * 512 + i * 128, 1, half * 512, 512)[:, 0, :],
                                 in_=yw_sb[:]).then_inc(p.sem("p5ws0"), 16)
            sy.wait_ge(p.sem("p5da"), bDA + G * (c + 1))
            sy.wait_ge(p.sem("p5du"), bDU + c + 1)
            sy.wait_ge(p.sem("p5eb"), bEB + 16 * (c + 1))
            sy.wait_ge(p.sem("p5b"), bB + G * (c + 1))
            sy.wait_ge(p.sem("p5m"), bM + G * (c + 1))
            sy.wait_ge(p.sem("p5ye"), bYE + 16 * (c + 1))
            sy.wait_ge(p.sem("p5ws0"), bWS + 16 * 8 * (c + 1))
            sy.sem_inc(p.sem("p5ck"), 1)
    p.semv["p5dt"] = bDT + 16 * NCH
    p.semv["p5u"] = bU + 16 * NCH
    p.semv["p5z0"] = bZ + 16 * NCH
    p.semv["p5bb0"] = bBB_[0] + 16 * 8 * NCH
    p.semv["p5bb1"] = bBB_[1] + 16 * 8 * NCH
    p.semv["p5cb0"] = bCB_[0] + 16 * 8 * NCH
    p.semv["p5cb1"] = bCB_[1] + 16 * 8 * NCH
    p.semv["p5wlh"] = bWL + 16 * 2 * NCH
    p.semv["p5ws0"] = bWS + 16 * 8 * NCH
    p.semv["p5ck"] = bCK + NCH

    @block.scalar
    def _(s):
        for c in range(NCH):
            s.wait_ge(p.sem("p5dt"), bDT + 16 * (c + 1))
            for dt in range(16):
                for n in range(16):
                    g = dt * 16 + n
                    gg = c * G + g
                    if gg >= 4:
                        s.wait_ge(p.sem("p5h"), bH + gg - 3)
                    s.activation(dA2[:, (g % 4) * 512:(g % 4 + 1) * 512], dtc[:, dt * 512:(dt + 1) * 512],
                                 AF.Exp, scale=ctx["A"][:, dt * 16 + n:dt * 16 + n + 1]
                                 ).then_inc(p.sem("p5da"), 1)
            for half in range(2):
                for i in range(4):
                    wn = c * 8 + half * 4 + i
                    s.wait_ge(p.sem("p5wm"), bWM + wn + 1)
                    if wn >= 1:
                        s.wait_ge(p.sem("p5ws0"), bWS + 16 * wn)
                    s.copy(yw_sb[:], ps_w[:, (wn % 2) * 512:(wn % 2 + 1) * 512]).then_inc(p.sem("p5we"), 1)
    p.semv["p5da"] = bDA + NCH * G
    p.semv["p5we"] = bWE + NCH * 8

    @block.gpsimd
    def _(g_):
        for c in range(NCH):
            g_.wait_ge(p.sem("p5dt"), bDT + 16 * (c + 1))
            g_.wait_ge(p.sem("p5u"), bU + 16 * (c + 1))
            g_.wait_ge(p.sem("p5bb0"), bBB_[0] + 16 * 8 * (c + 1))
            g_.wait_ge(p.sem("p5bb1"), bBB_[1] + 16 * 8 * (c + 1))
            if c >= 1:
                g_.wait_ge(p.sem("p5b"), bB + G * c)
            for dt in range(16):
                ins = g_.tensor_mul(dtu[:, dt * 512:(dt + 1) * 512], dtc[:, dt * 512:(dt + 1) * 512],
                                    ub[:, dt * 512:(dt + 1) * 512])
            ins.then_inc(p.sem("p5du"), 1)
            g_.wait_ge(p.sem("p5du"), bDU + c + 1)
            for dt in range(16):
                for n in range(16):
                    g = dt * 16 + n
                    gg = c * G + g
                    if gg >= 4:
                        g_.wait_ge(p.sem("p5h"), bH + gg - 3)
                    g_.tensor_mul(be2[:, (g % 4) * 512:(g % 4 + 1) * 512], dtu[:, dt * 512:(dt + 1) * 512],
                                  BbA[:, n * 512:(n + 1) * 512]).then_inc(p.sem("p5b"), 1)
                    if g >= 4:
                        gs = gg - 4
                        g_.wait_ge(p.sem("p5h"), bH + gs + 1)
                        g_.tensor_copy(hst[:, gs % G:gs % G + 1],
                                       h4[:, (gs % 4) * 512 + 511:(gs % 4) * 512 + 512]
                                       ).then_inc(p.sem("p5hs"), 1)
                if dt >= 2:
                    zn = c * 16 + dt - 2
                    g_.wait_ge(p.sem("p5eb"), bEB + zn + 1)
                    g_.wait_ge(p.sem("p5z0"), bZ + 16 * (c + 1))
                    g_.tensor_mul(yhat[:, (dt - 2) * 512:(dt - 1) * 512],
                                  teb2[:, (zn % 2) * 512:(zn % 2 + 1) * 512],
                                  zb16[:, (dt - 2) * 512:(dt - 1) * 512]).then_inc(p.sem("p5ye"), 1)
            for gs in range(c * G + G - 4, c * G + G):
                g_.wait_ge(p.sem("p5h"), bH + gs + 1)
                g_.tensor_copy(hst[:, gs % G:gs % G + 1],
                               h4[:, (gs % 4) * 512 + 511:(gs % 4) * 512 + 512]).then_inc(p.sem("p5hs"), 1)
            for dtl in [14, 15]:
                zn = c * 16 + dtl
                g_.wait_ge(p.sem("p5eb"), bEB + zn + 1)
                g_.tensor_mul(yhat[:, dtl * 512:(dtl + 1) * 512],
                              teb2[:, (zn % 2) * 512:(zn % 2 + 1) * 512],
                              zb16[:, dtl * 512:(dtl + 1) * 512]).then_inc(p.sem("p5ye"), 1)
    p.semv["p5du"] = bDU + NCH
    p.semv["p5b"] = bB + NCH * G
    p.semv["p5hs"] = bHS + NCH * G
    p.semv["p5ye"] = bYE + NCH * 16

    @block.vector
    def _(v):
        for c in range(NCH):
            v.wait_ge(p.sem("p5cb0"), bCB_[0] + 16 * 8 * (c + 1))
            v.wait_ge(p.sem("p5cb1"), bCB_[1] + 16 * 8 * (c + 1))
            for dt in range(16):
                if dt == 0 and c >= 1:
                    v.wait_ge(p.sem("p5wm"), bWM + 8 * c)
                for n in range(16):
                    g = dt * 16 + n
                    gg = c * G + g
                    v.wait_ge(p.sem("p5da"), bDA + gg + 1)
                    v.wait_ge(p.sem("p5b"), bB + gg + 1)
                    if gg >= 4:
                        v.wait_ge(p.sem("p5hs"), bHS + gg - 3)
                        v.wait_ge(p.sem("p5m"), bM + gg - 3)
                    v.tensor_tensor_scan(h4[:, (g % 4) * 512:(g % 4 + 1) * 512],
                                         dA2[:, (g % 4) * 512:(g % 4 + 1) * 512],
                                         be2[:, (g % 4) * 512:(g % 4 + 1) * 512],
                                         hst[:, g:g + 1], ALU.mult, ALU.add).then_inc(p.sem("p5h"), 1)
                    v.wait_ge(p.sem("p5h"), bH + gg + 1)
                    if c * 16 + dt >= 1 and n == 0:
                        v.wait_ge(p.sem("p5y"), bY + c * 16 + dt)
                    v.tensor_mul(m16[:, n * 512:(n + 1) * 512], h4[:, (g % 4) * 512:(g % 4 + 1) * 512],
                                 CbA[:, n * 512:(n + 1) * 512]).then_inc(p.sem("p5m"), 1)
                v.wait_ge(p.sem("p5m"), bM + c * G + dt * 16 + 16)
                for lvl, stp in [(8, 1), (4, 2), (2, 4)]:
                    for i in range(lvl):
                        a0 = 2 * i * stp
                        ins = v.tensor_add(m16[:, a0 * 512:(a0 + 1) * 512],
                                           m16[:, a0 * 512:(a0 + 1) * 512],
                                           m16[:, (a0 + stp) * 512:(a0 + stp + 1) * 512])
                    p.inc(ins, "p5tc", 1)
                    v.wait_ge(p.sem("p5tc"), p.val("p5tc"))
                zn = c * 16 + dt
                if zn >= 2:
                    v.wait_ge(p.sem("p5eb"), bEB + zn - 1)
                ins = v.tensor_add(y2[:, (zn % 2) * 512:(zn % 2 + 1) * 512],
                                   m16[:, 0:512], m16[:, 8 * 512:9 * 512])
                ins.then_inc(p.sem("p5y"), 1)
                v.wait_ge(p.sem("p5y"), bY + zn + 1)
                if zn >= 2:
                    v.wait_ge(p.sem("p5ye"), bYE + zn - 1)
                v.scalar_tensor_tensor(teb2[:, (zn % 2) * 512:(zn % 2 + 1) * 512],
                                       ub[:, dt * 512:(dt + 1) * 512], ctx["Dskip"][:, dt:dt + 1],
                                       y2[:, (zn % 2) * 512:(zn % 2 + 1) * 512], ALU.mult, ALU.add
                                       ).then_inc(p.sem("p5eb"), 1)
    p.semv["p5h"] = bH + NCH * G
    p.semv["p5m"] = bM + NCH * G
    p.semv["p5y"] = bY + NCH * 16
    p.semv["p5eb"] = bEB + NCH * 16

    @block.tensor
    def _(t_):
        for c in range(NCH):
            t_.wait_ge(p.sem("p5ye"), bYE + 16 * (c + 1))
            for half in range(2):
                hn = c * 2 + half
                t_.wait_ge(p.sem("p5wlh"), bWL + 16 * (hn + 1))
                for i in range(4):
                    wn = c * 8 + half * 4 + i
                    if wn >= 1:
                        t_.wait_ge(p.sem("p5we"), bWE + wn)
                    for dt in range(16):
                        ins = t_.matmul(ps_w[:, (wn % 2) * 512:(wn % 2 + 1) * 512],
                                        yhat[:, dt * 512 + i * 128:dt * 512 + (i + 1) * 128],
                                        Woutb_sb[:, dt * 512:(dt + 1) * 512],
                                        start=(dt == 0), stop=(dt == 15))
                    ins.then_inc(p.sem("p5wm"), 1)
    p.semv["p5wm"] = bWM + NCH * 8
    st.close()
    barrier(p, block)


def emit_p56(p, block, ctx):
    st = contextlib.ExitStack()
    dt2 = p.sb("p56_dt2", [128, 2 * 512], BF16, st)
    ub2 = p.sb("p56_ub2", [128, 2 * 512], BF16, st)
    dtu2 = p.sb("p56_dtu2", [128, 2 * 512], BF16, st)
    BbA = p.sb("p56_BbA", [128, 16 * 512], BF16, st)
    CbA = p.sb("p56_CbA", [128, 16 * 512], BF16, st)
    m16 = p.sb("p56_m16", [128, 16 * 512], BF16, st)
    y2 = p.sb("p56_y2", [128, 2 * 512], BF16, st)
    dA4 = p.sb("p56_dA4", [128, 4 * 512], BF16, st)
    be4 = p.sb("p56_be4", [128, 4 * 512], BF16, st)
    h4 = p.sb("p56_h4", [128, 4 * 512], F32, st)
    zb2 = p.sb("p56_zb2", [128, 4 * 512], BF16, st)
    teb2 = p.sb("p56_teb2", [128, 2 * 512], BF16, st)
    yhat = p.sb("p56_yhat", [128, 16 * 512], BF16, st)
    yw_sb = p.sb("p56_yw", [128, 512], F32, st)
    Wob = p.sb("p56_Wob", [128, 16 * 512], BF16, st)
    xnT = p.sb("p56_xnT", [128, 8 * 512], BF16, st)
    W1r = p.sb("p56_W1r", [128, 2 * 8 * 128], BF16, st)
    W2h = p.sb("p56_W2h", [128, 16 * 512], BF16, st)
    heT = p.sb("p56_heT", [128, 16 * 512], BF16, st)
    acc = p.sb("p56_acc", [128, 4 * 1024], F32, st)
    gf_sb = p.sb("p56_gf", [128, 32], F32, st)
    gT_sb = p.sb("p56_gT", [8, 512], BF16, st)
    b2b_sb = p.sb("p56_b2b", [8, 1024], BF16, st)
    ps_w = p.ps("p56_psw", [128, 2 * 512], F32, st)
    ps_h = p.ps("p56_psh", [128, 2 * 512], F32, st)
    ps_o = p.ps("p56_pso", [128, 4 * 512], F32, st)
    hst = ctx["hstate"]
    G = 256

    bDT = [p.val("c_dt0"), p.val("c_dt1")]
    bU = [p.val("c_u0"), p.val("c_u1")]
    bZ = [p.val("c_z0"), p.val("c_z1"), p.val("c_z2"), p.val("c_z3")]
    bWL = p.val("c_wl")
    bDA = p.val("c_da"); bDU = p.val("c_du"); bB = p.val("c_b"); bH = p.val("c_h")
    bHS = p.val("c_hs"); bM = p.val("c_m"); bY = p.val("c_y"); bTC = p.val("c_tc")
    bEB = p.val("c_eb"); bYE = p.val("c_ye"); bWM = p.val("c_wm"); bWE = p.val("c_we")
    bCK = p.val("c_ck"); bWS = p.val("c_ws")
    bBB = [p.val("c_bb0"), p.val("c_bb1")]
    bCB = [p.val("c_cb0"), p.val("c_cb1")]
    bXT = p.val("c_xt"); bGF = p.val("c_gf"); bGT = p.val("c_gt"); bB2L = p.val("c_b2l")
    bW1 = [p.val("c_w10"), p.val("c_w11")]
    bW2 = p.val("c_w2")
    bM1 = p.val("c_m1"); bE1 = p.val("c_e1"); bM2 = p.val("c_m2"); bAE = p.val("c_ae")
    bB2 = p.val("c_b2"); bB2M = p.val("c_b2m"); bMS = p.val("c_ms")

    @block.sync
    def _(sy):
        sy.dma_start(out=b2b_sb[:], in_=p.T("b2b")).then_inc(p.sem("c_b2l"), 16)
        for c in range(NCH):
            if c >= 1:
                sy.wait_ge(p.sem("c_ck"), bCK + c)
            for n in range(16):
                sl = n % 2
                boff = n * S + c * 512
                sy.dma_start(out=BbA[:, n * 512:(n + 1) * 512],
                             in_=bass.AP(p.T("BmT_b").tensor, boff, [[0, 128], [1, 512]])
                             ).then_inc(p.sem("c_bb%d" % sl), 16)
                sy.dma_start(out=CbA[:, n * 512:(n + 1) * 512],
                             in_=bass.AP(p.T("CmT_b").tensor, boff, [[0, 128], [1, 512]])
                             ).then_inc(p.sem("c_cb%d" % sl), 16)
            sy.dma_start(out=rr(xnT[:], "p (dt s) -> p dt s", s=512),
                         in_=rr(p.T("xnT_b"), "(dt pp) s -> pp dt s", pp=128)[:, :, c * 512:(c + 1) * 512]
                         ).then_inc(p.sem("c_xt"), 16)
            sy.dma_start(out=rr(gf_sb[:], "p (t e) -> p t e", e=8),
                         in_=pt(p.T("gates_f"), c * 512, 4, 0, 8)).then_inc(p.sem("c_gf"), 16)
            sy.dma_start(out=gT_sb[:], in_=p.T("gatesT_b")[0:8, c * 512:(c + 1) * 512]
                         ).then_inc(p.sem("c_gt"), 16)
            for dt in range(16):
                gdt = c * 16 + dt
                sl = dt % 2
                for j8 in range(8):
                    on_l = dt * 8 + j8
                    on = c * 128 + on_l
                    e_, och_ = divmod(on_l, 16)
                    wsl = on % 2
                    if on >= 2:
                        sy.wait_ge(p.sem("c_m1"), bM1 + on - 1)
                    sy.dma_start(out=rr(W1r[:, wsl * 1024:(wsl + 1) * 1024], "p (kt n) -> p kt n", kt=8),
                                 in_=rr(rr(p.T("W1b"), "e r c -> (e r) c"), "(kt pp) n -> pp kt n", pp=128
                                        )[:, e_ * 8:(e_ + 1) * 8, och_ * 128:(och_ + 1) * 128]
                                 ).then_inc(p.sem("c_w1%d" % wsl), 16)
                if True:
                    wh_l = dt
                    wh = c * 16 + wh_l
                    e_, half_ = divmod(wh_l, 2)
                    if wh >= 1:
                        sy.wait_ge(p.sem("c_m2"), bM2 + 4 * wh)
                    sy.dma_start(out=rr(W2h[:], "p (kt n) -> p kt n", kt=16),
                                 in_=rr(rr(p.T("W2b"), "e r c -> (e r) c"), "(kt pp) n -> pp kt n", pp=128
                                        )[:, e_ * 16:(e_ + 1) * 16, half_ * 512:(half_ + 1) * 512]
                                 ).then_inc(p.sem("c_w2"), 16)
            for half in range(2):
                hn = c * 2 + half
                if hn >= 1:
                    sy.wait_ge(p.sem("c_wm"), bWM + 4 * hn)
                sy.dma_start(out=rr(Wob[:], "p (dt n) -> p dt n", n=512),
                             in_=pt(p.T("Woutb"), 0, 16, half * 512, 512)).then_inc(p.sem("c_wl"), 16)
                for i in range(4):
                    wn = c * 8 + half * 4 + i
                    sy.wait_ge(p.sem("c_we"), bWE + wn + 1)
                    sy.dma_start(out=pt(p.T("yW_f", write=True), c * 512 + i * 128, 1, half * 512, 512)[:, 0, :],
                                 in_=yw_sb[:]).then_inc(p.sem("c_ws"), 16)
            sy.wait_ge(p.sem("c_b2"), bB2 + 8 * (c + 1))
            sy.dma_start(out=pt(p.T("moe_f", write=True), c * 512, 4, 0, 1024),
                         in_=rr(acc[:], "p (t d) -> p t d", t=4)).then_inc(p.sem("c_ms"), 16)
            sy.wait_ge(p.sem("c_b"), bB + G * (c + 1))
            sy.wait_ge(p.sem("c_m"), bM + G * (c + 1))
            sy.wait_ge(p.sem("c_ye"), bYE + 16 * (c + 1))
            sy.wait_ge(p.sem("c_ws"), bWS + 16 * 8 * (c + 1))
            sy.wait_ge(p.sem("c_ms"), bMS + 16 * (c + 1))
            sy.sem_inc(p.sem("c_ck"), 1)
    p.semv["c_b2l"] = bB2L + 16
    for sl_ in range(2):
        p.semv["c_dt%d" % sl_] = bDT[sl_] + 16 * 8 * NCH
        p.semv["c_u%d" % sl_] = bU[sl_] + 16 * 8 * NCH
    for sl_ in range(4):
        p.semv["c_z%d" % sl_] = bZ[sl_] + 16 * 4 * NCH
    p.semv["c_bb0"] = bBB[0] + 16 * 8 * NCH
    p.semv["c_bb1"] = bBB[1] + 16 * 8 * NCH
    p.semv["c_cb0"] = bCB[0] + 16 * 8 * NCH
    p.semv["c_cb1"] = bCB[1] + 16 * 8 * NCH
    p.semv["c_xt"] = bXT + 16 * NCH
    p.semv["c_gf"] = bGF + 16 * NCH
    p.semv["c_gt"] = bGT + 16 * NCH
    p.semv["c_w10"] = bW1[0] + 16 * 64 * NCH
    p.semv["c_w11"] = bW1[1] + 16 * 64 * NCH
    p.semv["c_w2"] = bW2 + 16 * 16 * NCH
    p.semv["c_wl"] = bWL + 16 * 2 * NCH
    p.semv["c_ws"] = bWS + 16 * 8 * NCH
    p.semv["c_ms"] = bMS + 16 * NCH
    p.semv["c_ck"] = bCK + NCH

    @block.scalar
    def _(s):
        for c in range(NCH):
            for dt in range(16):
                gdt = c * 16 + dt
                sl = dt % 2
                s.wait_ge(p.sem("c_dt%d" % sl), bDT[sl] + 16 * (gdt // 2 + 1))
                for n in range(16):
                    g = dt * 16 + n
                    gg = c * G + g
                    if gg >= 4:
                        s.wait_ge(p.sem("c_h"), bH + gg - 3)
                    s.activation(dA4[:, (g % 4) * 512:(g % 4 + 1) * 512], dt2[:, sl * 512:(sl + 1) * 512],
                                 AF.Exp, scale=ctx["A"][:, dt * 16 + n:dt * 16 + n + 1]
                                 ).then_inc(p.sem("c_da"), 1)
                for och8 in range(8):
                    on = c * 128 + dt * 8 + och8
                    s.wait_ge(p.sem("c_m1"), bM1 + on + 1)
                    e_, och_ = divmod(on % 128, 16)
                    s.activation(heT[:, och_ * 512:(och_ + 1) * 512],
                                 ps_h[:, (on % 2) * 512:(on % 2 + 1) * 512], AF.Relu,
                                 bias=ctx["b1"][:, e_ * 16 + och_:e_ * 16 + och_ + 1]
                                 ).then_inc(p.sem("c_e1"), 1)
            for half in range(2):
                for i in range(4):
                    wn = c * 8 + half * 4 + i
                    s.wait_ge(p.sem("c_wm"), bWM + wn + 1)
                    if wn >= 1:
                        s.wait_ge(p.sem("c_ws"), bWS + 16 * wn)
                    s.copy(yw_sb[:], ps_w[:, (wn % 2) * 512:(wn % 2 + 1) * 512]).then_inc(p.sem("c_we"), 1)
    p.semv["c_da"] = bDA + NCH * G
    p.semv["c_e1"] = bE1 + NCH * 128
    p.semv["c_we"] = bWE + NCH * 8

    @block.gpsimd
    def _(g_):
        def ld_rings(gdt):
            if gdt >= NCH * 16:
                return
            cc, dtt = divmod(gdt, 16)
            sl_ = gdt % 2
            if gdt >= 2:
                g_.wait_ge(p.sem("c_da"), bDA + (gdt - 1) * 16)
                g_.wait_ge(p.sem("c_du"), bDU + gdt - 1)
            g_.dma_start(out=dt2[:, sl_ * 512:(sl_ + 1) * 512],
                         in_=rr(p.T("dtT_b"), "(dt pp) s -> pp dt s", pp=128)[:, dtt, cc * 512:(cc + 1) * 512]
                         ).then_inc(p.sem("c_dt%d" % sl_), 16)
            if gdt >= 2:
                g_.wait_ge(p.sem("c_eb"), bEB + gdt - 1)
            g_.dma_start(out=ub2[:, sl_ * 512:(sl_ + 1) * 512],
                         in_=rr(p.T("uT_b"), "(dt pp) s -> pp dt s", pp=128)[:, dtt, cc * 512:(cc + 1) * 512]
                         ).then_inc(p.sem("c_u%d" % sl_), 16)
            zsl = gdt % 4
            if gdt >= 4:
                g_.wait_ge(p.sem("c_ye"), bYE + gdt - 3)
            g_.dma_start(out=zb2[:, zsl * 512:(zsl + 1) * 512],
                         in_=rr(p.T("zT_b"), "(dt pp) s -> pp dt s", pp=128)[:, dtt, cc * 512:(cc + 1) * 512]
                         ).then_inc(p.sem("c_z%d" % zsl), 16)

        ld_rings(0)
        ld_rings(1)
        for c in range(NCH):
            for dt in range(16):
                gdt = c * 16 + dt
                sl = dt % 2
                if dt >= 2:
                    zn = c * 16 + dt - 2
                    g_.wait_ge(p.sem("c_eb"), bEB + zn + 1)
                    g_.wait_ge(p.sem("c_z%d" % (zn % 4)), bZ[zn % 4] + 16 * (zn // 4 + 1))
                    g_.tensor_mul(yhat[:, (dt - 2) * 512:(dt - 1) * 512],
                                  teb2[:, (zn % 2) * 512:(zn % 2 + 1) * 512],
                                  zb2[:, (zn % 4) * 512:(zn % 4 + 1) * 512]).then_inc(p.sem("c_ye"), 1)
                g_.wait_ge(p.sem("c_dt%d" % sl), bDT[sl] + 16 * (gdt // 2 + 1))
                g_.wait_ge(p.sem("c_u%d" % sl), bU[sl] + 16 * (gdt // 2 + 1))
                if gdt >= 2:
                    g_.wait_ge(p.sem("c_b"), bB + (gdt - 1) * 16)
                g_.tensor_mul(dtu2[:, sl * 512:(sl + 1) * 512], dt2[:, sl * 512:(sl + 1) * 512],
                              ub2[:, sl * 512:(sl + 1) * 512]).then_inc(p.sem("c_du"), 1)
                g_.wait_ge(p.sem("c_du"), bDU + gdt + 1)
                if dt == 0:
                    g_.wait_ge(p.sem("c_bb0"), bBB[0] + 16 * 8 * (c + 1))
                    g_.wait_ge(p.sem("c_bb1"), bBB[1] + 16 * 8 * (c + 1))
                for n in range(16):
                    g = dt * 16 + n
                    gg = c * G + g
                    if gg >= 4:
                        g_.wait_ge(p.sem("c_h"), bH + gg - 3)
                    g_.tensor_mul(be4[:, (g % 4) * 512:(g % 4 + 1) * 512], dtu2[:, sl * 512:(sl + 1) * 512],
                                  BbA[:, n * 512:(n + 1) * 512]).then_inc(p.sem("c_b"), 1)
                    if g % 4 == 3 and g >= 7:
                        k = g // 4 - 1
                        gs0 = c * G + 4 * k
                        g_.wait_ge(p.sem("c_h"), bH + gs0 + 4)
                        g_.tensor_copy(hst[:, 4 * k:4 * k + 4],
                                       bass.AP(h4, 511, [[2048, 128], [512, 4]])
                                       ).then_inc(p.sem("c_hs"), 4)
                if True:
                    ld_rings(gdt + 2)
            for k in range(G // 4 - 1, G // 4):
                gs0 = c * G + 4 * k
                g_.wait_ge(p.sem("c_h"), bH + gs0 + 4)
                g_.tensor_copy(hst[:, 4 * k:4 * k + 4],
                               bass.AP(h4, 511, [[2048, 128], [512, 4]])
                               ).then_inc(p.sem("c_hs"), 4)
            for dtl in [14, 15]:
                zn = c * 16 + dtl
                g_.wait_ge(p.sem("c_eb"), bEB + zn + 1)
                g_.wait_ge(p.sem("c_z%d" % (zn % 4)), bZ[zn % 4] + 16 * (zn // 4 + 1))
                g_.tensor_mul(yhat[:, dtl * 512:(dtl + 1) * 512],
                              teb2[:, (zn % 2) * 512:(zn % 2 + 1) * 512],
                              zb2[:, (zn % 4) * 512:(zn % 4 + 1) * 512]).then_inc(p.sem("c_ye"), 1)
    p.semv["c_du"] = bDU + NCH * 16
    p.semv["c_b"] = bB + NCH * G
    p.semv["c_hs"] = bHS + NCH * G
    p.semv["c_ye"] = bYE + NCH * 16

    @block.vector
    def _(v):
        for c in range(NCH):
            v.wait_ge(p.sem("c_cb0"), bCB[0] + 16 * 8 * (c + 1))
            v.wait_ge(p.sem("c_cb1"), bCB[1] + 16 * 8 * (c + 1))
            v.wait_ge(p.sem("c_gf"), bGF + 16 * (c + 1))
            if c >= 1:
                v.wait_ge(p.sem("c_ck"), bCK + c)
            for dt in range(16):
                if dt == 0 and c >= 1:
                    v.wait_ge(p.sem("c_wm"), bWM + 8 * c)
                for n in range(16):
                    g = dt * 16 + n
                    gg = c * G + g
                    v.wait_ge(p.sem("c_da"), bDA + gg + 1)
                    v.wait_ge(p.sem("c_b"), bB + gg + 1)
                    if gg >= 4:
                        v.wait_ge(p.sem("c_hs"), bHS + 4 * ((gg - 4) // 4 + 1))
                        v.wait_ge(p.sem("c_m"), bM + gg - 3)
                    v.tensor_tensor_scan(h4[:, (g % 4) * 512:(g % 4 + 1) * 512],
                                         dA4[:, (g % 4) * 512:(g % 4 + 1) * 512],
                                         be4[:, (g % 4) * 512:(g % 4 + 1) * 512],
                                         hst[:, g:g + 1], ALU.mult, ALU.add).then_inc(p.sem("c_h"), 1)
                    v.wait_ge(p.sem("c_h"), bH + gg + 1)
                    if c * 16 + dt >= 1 and n == 0:
                        v.wait_ge(p.sem("c_y"), bY + c * 16 + dt)
                    v.tensor_mul(m16[:, n * 512:(n + 1) * 512], h4[:, (g % 4) * 512:(g % 4 + 1) * 512],
                                 CbA[:, n * 512:(n + 1) * 512]).then_inc(p.sem("c_m"), 1)
                v.wait_ge(p.sem("c_m"), bM + c * G + dt * 16 + 16)
                for lvl, stp in [(8, 1), (4, 2), (2, 4)]:
                    for i in range(lvl):
                        a0 = 2 * i * stp
                        ins = v.tensor_add(m16[:, a0 * 512:(a0 + 1) * 512],
                                           m16[:, a0 * 512:(a0 + 1) * 512],
                                           m16[:, (a0 + stp) * 512:(a0 + stp + 1) * 512])
                    p.inc(ins, "c_tc", 1)
                    v.wait_ge(p.sem("c_tc"), p.val("c_tc"))
                zn = c * 16 + dt
                if zn >= 2:
                    v.wait_ge(p.sem("c_eb"), bEB + zn - 1)
                ins = v.tensor_add(y2[:, (zn % 2) * 512:(zn % 2 + 1) * 512],
                                   m16[:, 0:512], m16[:, 8 * 512:9 * 512])
                ins.then_inc(p.sem("c_y"), 1)
                v.wait_ge(p.sem("c_y"), bY + zn + 1)
                v.wait_ge(p.sem("c_u%d" % (dt % 2)), bU[dt % 2] + 16 * (zn // 2 + 1))
                if zn >= 2:
                    v.wait_ge(p.sem("c_ye"), bYE + zn - 1)
                v.scalar_tensor_tensor(teb2[:, (zn % 2) * 512:(zn % 2 + 1) * 512],
                                       ub2[:, (dt % 2) * 512:(dt % 2 + 1) * 512], ctx["Dskip"][:, dt:dt + 1],
                                       y2[:, (zn % 2) * 512:(zn % 2 + 1) * 512], ALU.mult, ALU.add
                                       ).then_inc(p.sem("c_eb"), 1)
                for q in (range(4) if dt >= 1 else []):
                    mn = c * 64 + (dt - 1) * 4 + q
                    e_, tq = divmod(mn % 64, 8)
                    half_i, t_i = divmod(tq, 4)
                    v.wait_ge(p.sem("c_m2"), bM2 + mn + 1)
                    dst = acc[:, t_i * 1024 + half_i * 512:t_i * 1024 + (half_i + 1) * 512]
                    gcol = gf_sb[:, t_i * 8 + e_:t_i * 8 + e_ + 1]
                    src2 = ps_o[:, (mn % 4) * 512:(mn % 4 + 1) * 512]
                    if e_ == 0:
                        ins = v.tensor_scalar_mul(dst, src2, gcol)
                    else:
                        ins = v.scalar_tensor_tensor(dst, src2, gcol, dst, ALU.mult, ALU.add)
                    ins.then_inc(p.sem("c_ae"), 1)
            for q in range(4):
                mn = c * 64 + 60 + q
                e_, tq = divmod(mn % 64, 8)
                half_i, t_i = divmod(tq, 4)
                v.wait_ge(p.sem("c_m2"), bM2 + mn + 1)
                dst = acc[:, t_i * 1024 + half_i * 512:t_i * 1024 + (half_i + 1) * 512]
                gcol = gf_sb[:, t_i * 8 + e_:t_i * 8 + e_ + 1]
                src2 = ps_o[:, (mn % 4) * 512:(mn % 4 + 1) * 512]
                if e_ == 0:
                    ins = v.tensor_scalar_mul(dst, src2, gcol)
                else:
                    ins = v.scalar_tensor_tensor(dst, src2, gcol, dst, ALU.mult, ALU.add)
                ins.then_inc(p.sem("c_ae"), 1)
            for bn8 in range(8):
                bn = c * 8 + bn8
                t_i, half_i = divmod(bn8, 2)
                v.wait_ge(p.sem("c_b2m"), bB2M + bn + 1)
                dst = acc[:, t_i * 1024 + half_i * 512:t_i * 1024 + (half_i + 1) * 512]
                ins = v.tensor_add(dst, dst, ps_o[:, (bn % 4) * 512:(bn % 4 + 1) * 512])
                ins.then_inc(p.sem("c_b2"), 1)
    p.semv["c_h"] = bH + NCH * G
    p.semv["c_m"] = bM + NCH * G
    p.semv["c_y"] = bY + NCH * 16
    p.semv["c_eb"] = bEB + NCH * 16
    p.semv["c_ae"] = bAE + NCH * 64
    p.semv["c_b2"] = bB2 + NCH * 8

    @block.tensor
    def _(t_):
        for c in range(NCH):
            t_.wait_ge(p.sem("c_xt"), bXT + 16 * (c + 1))
            for e in range(NE):
                en = c * 8 + e
                for och in range(16):
                    on = en * 16 + och
                    sl = on % 2
                    t_.wait_ge(p.sem("c_w1%d" % sl), bW1[sl] + 16 * (on // 2 + 1))
                    if on >= 2:
                        t_.wait_ge(p.sem("c_e1"), bE1 + on - 1)
                    for kt in range(8):
                        ins = t_.matmul(ps_h[:, (on % 2) * 512:(on % 2 + 1) * 512],
                                        W1r[:, sl * 1024 + kt * 128:sl * 1024 + (kt + 1) * 128],
                                        xnT[:, kt * 512:(kt + 1) * 512],
                                        start=(kt == 0), stop=(kt == 7))
                    ins.then_inc(p.sem("c_m1"), 1)
                t_.wait_ge(p.sem("c_e1"), bE1 + (en + 1) * 16)
                for half in range(2):
                    wh = en * 2 + half
                    t_.wait_ge(p.sem("c_w2"), bW2 + 16 * (wh + 1))
                    for t in range(4):
                        mn = en * 8 + half * 4 + t
                        if mn >= 4:
                            t_.wait_ge(p.sem("c_ae"), bAE + mn - 3)
                        for kt in range(16):
                            ins = t_.matmul(ps_o[:, (mn % 4) * 512:(mn % 4 + 1) * 512],
                                            heT[:, kt * 512 + t * 128:kt * 512 + (t + 1) * 128],
                                            W2h[:, kt * 512:(kt + 1) * 512],
                                            start=(kt == 0), stop=(kt == 15))
                        ins.then_inc(p.sem("c_m2"), 1)
            t_.wait_ge(p.sem("c_gt"), bGT + 16 * (c + 1))
            t_.wait_ge(p.sem("c_b2l"), bB2L + 16)
            t_.wait_ge(p.sem("c_ae"), bAE + 64 * (c + 1))
            for bn8 in range(8):
                bn = c * 8 + bn8
                tt, hh = divmod(bn8, 2)
                if bn >= 4:
                    t_.wait_ge(p.sem("c_b2"), bB2 + bn - 3)
                t_.matmul(ps_o[:, (bn % 4) * 512:(bn % 4 + 1) * 512],
                          gT_sb[0:8, tt * 128:(tt + 1) * 128],
                          b2b_sb[0:8, hh * 512:(hh + 1) * 512],
                          start=True, stop=True).then_inc(p.sem("c_b2m"), 1)
            t_.wait_ge(p.sem("c_ye"), bYE + 16 * (c + 1))
            for half in range(2):
                hn = c * 2 + half
                t_.wait_ge(p.sem("c_wl"), bWL + 16 * (hn + 1))
                for i in range(4):
                    wn = c * 8 + half * 4 + i
                    if wn >= 1:
                        t_.wait_ge(p.sem("c_we"), bWE + wn)
                    for dt in range(16):
                        ins = t_.matmul(ps_w[:, (wn % 2) * 512:(wn % 2 + 1) * 512],
                                        yhat[:, dt * 512 + i * 128:dt * 512 + (i + 1) * 128],
                                        Wob[:, dt * 512:(dt + 1) * 512],
                                        start=(dt == 0), stop=(dt == 15))
                    ins.then_inc(p.sem("c_wm"), 1)
    p.semv["c_m1"] = bM1 + NCH * 128
    p.semv["c_m2"] = bM2 + NCH * 64
    p.semv["c_b2m"] = bB2M + NCH * 8
    p.semv["c_wm"] = bWM + NCH * 8
    st.close()
    barrier(p, block)


def emit_p6b(p, block, ctx):
    st = contextlib.ExitStack()
    acc = p.sb("p6b_acc", [128, 4 * 1024], F32, st)
    xn_tm = p.sb("p6b_xntm", [128, 4 * 1024], F32, st)
    x2 = p.sb("p6b_x2", [128, 2 * 1024], F32, st)
    yW2 = p.sb("p6b_yW2", [128, 2 * 1024], F32, st)
    st6 = p.sb("p6b_st6", [128, 4 * 12], F32, st)
    mv = p.sb("p6b_mv", [128, 8], F32, st)
    rstdF = p.sb("p6b_rstdF", [128, 4], F32, st)
    st6b = p.sb("p6b_st6b", [128, 4 * 12], F32, st)
    mvb = p.sb("p6b_mvb", [128, 8], F32, st)
    rstd1 = p.sb("p6b_rstd1", [128, 4], F32, st)
    xn1bf = p.sb("p6b_xn1bf", [128, 4096], BF16, st)
    xn1T = p.sb("p6b_xn1T", [128, 8 * 512], BF16, st)
    ps_tr = p.ps("p6b_pstr", [128, 2 * 1024], BF16, st)

    bMO = p.val("b_mo"); bXN = p.val("b_xn"); bSS = p.val("b_ss"); bS = p.val("b_s")
    bG = p.val("b_g"); bV = p.val("b_v"); bCB = p.val("b_cb"); bT = p.val("b_t")
    bTE = p.val("b_te"); bSA = p.val("b_sA"); bCK = p.val("b_ck")
    bXL = [p.val("b_xl0"), p.val("b_xl1")]
    bYL = [p.val("b_yl0"), p.val("b_yl1")]
    NV = 4

    def V(c, k):
        return bV + c * NV + k

    @block.sync
    def _(sy):
        for c in range(NCH):
            if c >= 1:
                sy.wait_ge(p.sem("b_ck"), bCK + c)
            sy.dma_start(out=rr(acc[:], "p (t d) -> p t d", t=4),
                         in_=pt(p.T("moe_f"), c * 512, 4, 0, 1024)).then_inc(p.sem("b_mo"), 16)
            sy.dma_start(out=rr(xn_tm[:], "p (t d) -> p t d", t=4),
                         in_=pt(p.T("xn_f"), c * 512, 4, 0, 1024)).then_inc(p.sem("b_xn"), 16)
            for t in range(4):
                tn = c * 4 + t
                sl = t % 2
                if tn >= 2:
                    sy.wait_ge(p.sem("b_cb"), bCB + tn - 1)
                sy.dma_start(out=x2[:, sl * 1024:(sl + 1) * 1024],
                             in_=pt(p.T("x"), c * 512 + t * 128, 1, 0, 1024)[:, 0, :]
                             ).then_inc(p.sem("b_xl%d" % sl), 16)
                sy.dma_start(out=yW2[:, sl * 1024:(sl + 1) * 1024],
                             in_=pt(p.T("yW_f"), c * 512 + t * 128, 1, 0, 1024)[:, 0, :]
                             ).then_inc(p.sem("b_yl%d" % sl), 16)
            sy.wait_ge(p.sem("b_cb"), bCB + 4 * (c + 1))
            sy.dma_start(out=pt(p.T("x1_f", write=True), c * 512, 4, 0, 1024),
                         in_=rr(acc[:], "p (t d) -> p t d", t=4)).then_inc(p.sem("b_sA"), 16)
            sy.wait_ge(p.sem("b_te"), bTE + 4 * (c + 1))
            sy.dma_start(out=rr(p.T("xn1T_b", write=True), "(dt pp) s -> pp dt s", pp=128)[:, :, c * 512:(c + 1) * 512],
                         in_=rr(xn1T[:], "p (dt s) -> p dt s", s=512)).then_inc(p.sem("b_sA"), 16)
            sy.wait_ge(p.sem("b_sA"), bSA + 16 * 2 * (c + 1))
            sy.sem_inc(p.sem("b_ck"), 1)
    p.semv["b_mo"] = bMO + 16 * NCH
    p.semv["b_xn"] = bXN + 16 * NCH
    p.semv["b_xl0"] = bXL[0] + 16 * 2 * NCH
    p.semv["b_xl1"] = bXL[1] + 16 * 2 * NCH
    p.semv["b_yl0"] = bYL[0] + 16 * 2 * NCH
    p.semv["b_yl1"] = bYL[1] + 16 * 2 * NCH
    p.semv["b_sA"] = bSA + 16 * 2 * NCH
    p.semv["b_ck"] = bCK + NCH

    @block.vector
    def _(v):
        for c in range(NCH):
            v.wait_ge(p.sem("b_mo"), bMO + 16 * (c + 1))
            v.wait_ge(p.sem("b_xn"), bXN + 16 * (c + 1))
            for t in range(4):
                dst = acc[:, t * 1024:(t + 1) * 1024]
                ins = v.tensor_add(dst, dst, xn_tm[:, t * 1024:(t + 1) * 1024])
            p.inc(ins, "b_vs", 1)
            v.wait_ge(p.sem("b_vs"), p.val("b_vs"))
            for t in range(4):
                v.bn_stats(st6[:, t * 12:t * 12 + 6], acc[:, t * 1024:t * 1024 + 512])
                ins = v.bn_stats(st6[:, t * 12 + 6:t * 12 + 12], acc[:, t * 1024 + 512:t * 1024 + 1024])
            p.inc(ins, "b_vs", 1)
            v.wait_ge(p.sem("b_vs"), p.val("b_vs"))
            for t in range(4):
                ins = v.bn_aggr(mv[:, 2 * t:2 * t + 2], st6[:, t * 12:t * 12 + 12])
            ins.then_inc(p.sem("b_v"), 1)
            v.wait_ge(p.sem("b_v"), V(c, 1))
            for t in range(4):
                ins = v.tensor_single_scalar(acc[:, t * 1024:(t + 1) * 1024],
                                             acc[:, t * 1024:(t + 1) * 1024],
                                             mv[:, 2 * t:2 * t + 1], ALU.subtract)
            ins.then_inc(p.sem("b_v"), 1)
            v.wait_ge(p.sem("b_g"), bG + c + 1)
            v.wait_ge(p.sem("b_s"), bS + c * 2 + 1)
            for t in range(4):
                tn = c * 4 + t
                sl = t % 2
                dst = acc[:, t * 1024:(t + 1) * 1024]
                ins = v.scalar_tensor_tensor(dst, dst, rstdF[:, t:t + 1],
                                             ctx["bFbc"][:], ALU.mult, ALU.add)
                p.inc(ins, "b_vs", 1)
                v.wait_ge(p.sem("b_vs"), p.val("b_vs"))
                v.wait_ge(p.sem("b_xl%d" % sl), bXL[sl] + 16 * (tn // 2 + 1))
                v.wait_ge(p.sem("b_yl%d" % sl), bYL[sl] + 16 * (tn // 2 + 1))
                ins = v.tensor_add(dst, dst, x2[:, sl * 1024:(sl + 1) * 1024])
                p.inc(ins, "b_vs", 1)
                v.wait_ge(p.sem("b_vs"), p.val("b_vs"))
                ins = v.tensor_add(dst, dst, xn_tm[:, t * 1024:(t + 1) * 1024])
                p.inc(ins, "b_vs", 1)
                v.wait_ge(p.sem("b_vs"), p.val("b_vs"))
                ins = v.tensor_add(dst, dst, yW2[:, sl * 1024:(sl + 1) * 1024])
                ins.then_inc(p.sem("b_cb"), 1)
            v.wait_ge(p.sem("b_cb"), bCB + 4 * (c + 1))
            for t in range(4):
                v.bn_stats(st6b[:, t * 12:t * 12 + 6], acc[:, t * 1024:t * 1024 + 512])
                ins = v.bn_stats(st6b[:, t * 12 + 6:t * 12 + 12], acc[:, t * 1024 + 512:t * 1024 + 1024])
            p.inc(ins, "b_vs", 1)
            v.wait_ge(p.sem("b_vs"), p.val("b_vs"))
            for t in range(4):
                ins = v.bn_aggr(mvb[:, 2 * t:2 * t + 2], st6b[:, t * 12:t * 12 + 12])
            ins.then_inc(p.sem("b_v"), 1)
            v.wait_ge(p.sem("b_v"), V(c, 3))
            v.wait_ge(p.sem("b_s"), bS + c * 2 + 2)
            for t in range(4):
                ins = v.tensor_scalar(xn1bf[:, t * 1024:(t + 1) * 1024], acc[:, t * 1024:(t + 1) * 1024],
                                      mvb[:, 2 * t:2 * t + 1], rstd1[:, t:t + 1], ALU.subtract, ALU.mult)
            ins.then_inc(p.sem("b_v"), 1)
            for t in range(4):
                g = c * 4 + t
                v.wait_ge(p.sem("b_t"), bT + g + 1)
                v.tensor_copy(
                    rr(xn1T[:], "p (dt s) -> p dt s", s=512)[:, :, t * 128:(t + 1) * 128],
                    rr(ps_tr[:, (g % 2) * 1024:(g % 2 + 1) * 1024], "p (dt s) -> p dt s", s=128)
                ).then_inc(p.sem("b_te"), 1)
    p.semv["b_v"] = bV + NCH * NV
    p.semv["b_cb"] = bCB + NCH * 4
    p.semv["b_te"] = bTE + NCH * 4

    @block.scalar
    def _(s):
        for c in range(NCH):
            s.wait_ge(p.sem("b_v"), V(c, 1))
            for t in range(4):
                ins = s.activation(rstdF[:, t:t + 1], mv[:, 2 * t + 1:2 * t + 2], AF.Ln, bias=EPS)
            p.inc(ins, "b_sc", 1)
            s.wait_ge(p.sem("b_sc"), p.val("b_sc"))
            for t in range(4):
                ins = s.activation(rstdF[:, t:t + 1], rstdF[:, t:t + 1], AF.Exp, scale=-0.5)
            ins.then_inc(p.sem("b_s"), 1)
            s.wait_ge(p.sem("b_v"), V(c, 3))
            for t in range(4):
                ins = s.activation(rstd1[:, t:t + 1], mvb[:, 2 * t + 1:2 * t + 2], AF.Ln, bias=EPS)
            p.inc(ins, "b_sc", 1)
            s.wait_ge(p.sem("b_sc"), p.val("b_sc"))
            for t in range(4):
                ins = s.activation(rstd1[:, t:t + 1], rstd1[:, t:t + 1], AF.Exp, scale=-0.5)
            ins.then_inc(p.sem("b_s"), 1)
    p.semv["b_s"] = bS + NCH * 2

    @block.gpsimd
    def _(g_):
        for c in range(NCH):
            g_.wait_ge(p.sem("b_v"), V(c, 2))
            for t in range(4):
                ins = g_.tensor_mul(acc[:, t * 1024:(t + 1) * 1024], acc[:, t * 1024:(t + 1) * 1024],
                                    ctx["gFbc"][:])
            ins.then_inc(p.sem("b_g"), 1)
    p.semv["b_g"] = bG + NCH

    @block.tensor
    def _(t_):
        for c in range(NCH):
            for t in range(4):
                g = c * 4 + t
                t_.wait_ge(p.sem("b_v"), V(c, 4))
                if g >= 2:
                    t_.wait_ge(p.sem("b_te"), bTE + g - 1)
                for dt in range(8):
                    ins = t_.transpose(ps_tr[:, (g % 2) * 1024 + dt * 128:(g % 2) * 1024 + (dt + 1) * 128],
                                       xn1bf[:, t * 1024 + dt * 128:t * 1024 + (dt + 1) * 128],
                                       ctx["eyeb"][:])
                    if dt == 7:
                        ins.then_inc(p.sem("b_t"), 1)
    p.semv["b_t"] = bT + NCH * 4
    st.close()
    barrier(p, block)


def emit_p6(p, block, ctx):
    st = contextlib.ExitStack()
    xnT = p.sb("p6_xnT", [128, 8 * 512], BF16, st)
    W1b_sb = p.sb("p6_W1", [128, 8 * 2048], BF16, st)
    W2b_sb = p.sb("p6_W2", [128, 16 * 1024], BF16, st)
    heT = p.sb("p6_heT", [128, 16 * 512], BF16, st)
    acc = p.sb("p6_acc", [128, 4 * 1024], F32, st)
    xn_tm = p.sb("p6_xntm", [128, 4 * 1024], F32, st)
    x2 = p.sb("p6_x2", [128, 2 * 1024], F32, st)
    yW2 = p.sb("p6_yW2", [128, 2 * 1024], F32, st)
    gf_sb = p.sb("p6_gf", [128, 32], F32, st)
    gT_sb = p.sb("p6_gT", [8, 512], BF16, st)
    b2b_sb = p.sb("p6_b2b", [8, 1024], BF16, st)
    st6 = p.sb("p6_st6", [128, 4 * 12], F32, st)
    mv = p.sb("p6_mv", [128, 8], F32, st)
    rstdF = p.sb("p6_rstdF", [128, 4], F32, st)
    st6b = p.sb("p6_st6b", [128, 4 * 12], F32, st)
    mvb = p.sb("p6_mvb", [128, 8], F32, st)
    rstd1 = p.sb("p6_rstd1", [128, 4], F32, st)
    xn1bf = p.sb("p6_xn1bf", [128, 4096], BF16, st)
    xn1T = p.sb("p6_xn1T", [128, 8 * 512], BF16, st)
    ps_h = p.ps("p6_psh", [128, 2 * 512], F32, st)
    ps_o = p.ps("p6_pso", [128, 2 * 512], F32, st)
    ps_b2 = p.ps("p6_psb2", [128, 2 * 512], F32, st)
    ps_tr = p.ps("p6_pstr", [128, 2 * 1024], BF16, st)

    bXT = p.val("p6xt"); bXN = p.val("p6xn"); bGF = p.val("p6gf"); bGT = p.val("p6gt")
    bW1 = p.val("p6w1"); bW2 = p.val("p6w2"); bB2L = p.val("p6b2l")
    bM1 = p.val("p6m1"); bE1 = p.val("p6e1"); bM2 = p.val("p6m2"); bAE = p.val("p6ae")
    bB2 = p.val("p6b2"); bCB1 = p.val("p6cb1"); bSS = p.val("p6ss"); bS = p.val("p6s")
    bG = p.val("p6g"); bV = p.val("p6v"); bCB = p.val("p6cb"); bT = p.val("p6t")
    bTE = p.val("p6te"); bSA = p.val("p6sA"); bCK = p.val("p6ck")
    bXL = [p.val("p6xl0"), p.val("p6xl1")]
    bYL = [p.val("p6yl0"), p.val("p6yl1")]
    NV = 4

    def V(c, k):
        return bV + c * NV + k

    @block.sync
    def _(sy):
        sy.dma_start(out=b2b_sb[:], in_=p.T("b2b")).then_inc(p.sem("p6b2l"), 16)
        for c in range(NCH):
            if c >= 1:
                sy.wait_ge(p.sem("p6ck"), bCK + c)
            sy.dma_start(out=rr(xnT[:], "p (dt s) -> p dt s", s=512),
                         in_=rr(p.T("xnT_b"), "(dt pp) s -> pp dt s", pp=128)[:, :, c * 512:(c + 1) * 512]
                         ).then_inc(p.sem("p6xt"), 16)
            sy.dma_start(out=rr(xn_tm[:], "p (t d) -> p t d", t=4),
                         in_=pt(p.T("xn_f"), c * 512, 4, 0, 1024)).then_inc(p.sem("p6xn"), 16)
            sy.dma_start(out=rr(gf_sb[:], "p (t e) -> p t e", e=8),
                         in_=pt(p.T("gates_f"), c * 512, 4, 0, 8)).then_inc(p.sem("p6gf"), 16)
            sy.dma_start(out=gT_sb[:], in_=p.T("gatesT_b")[0:8, c * 512:(c + 1) * 512]
                         ).then_inc(p.sem("p6gt"), 16)
            for e in range(NE):
                en = c * 8 + e
                if en >= 1:
                    sy.wait_ge(p.sem("p6m1"), bM1 + 16 * en)
                sy.dma_start(out=rr(W1b_sb[:], "p (kt n) -> p kt n", kt=8),
                             in_=pt(rr(p.T("W1b"), "e r c -> (e r) c"), e * 1024, 8, 0, 2048)
                             ).then_inc(p.sem("p6w1"), 16)
                if en >= 1:
                    sy.wait_ge(p.sem("p6m2"), bM2 + 8 * en)
                sy.dma_start(out=rr(W2b_sb[:], "p (kt n) -> p kt n", kt=16),
                             in_=pt(rr(p.T("W2b"), "e r c -> (e r) c"), e * 2048, 16, 0, 1024)
                             ).then_inc(p.sem("p6w2"), 16)
            for t in range(4):
                tn = c * 4 + t
                sl = t % 2
                if tn >= 2:
                    sy.wait_ge(p.sem("p6cb"), bCB + tn - 1)
                sy.dma_start(out=x2[:, sl * 1024:(sl + 1) * 1024],
                             in_=pt(p.T("x"), c * 512 + t * 128, 1, 0, 1024)[:, 0, :]
                             ).then_inc(p.sem("p6xl%d" % sl), 16)
                sy.dma_start(out=yW2[:, sl * 1024:(sl + 1) * 1024],
                             in_=pt(p.T("yW_f"), c * 512 + t * 128, 1, 0, 1024)[:, 0, :]
                             ).then_inc(p.sem("p6yl%d" % sl), 16)
            sy.wait_ge(p.sem("p6cb"), bCB + 4 * (c + 1))
            sy.dma_start(out=pt(p.T("x1_f", write=True), c * 512, 4, 0, 1024),
                         in_=rr(acc[:], "p (t d) -> p t d", t=4)).then_inc(p.sem("p6sA"), 16)
            sy.wait_ge(p.sem("p6te"), bTE + 4 * (c + 1))
            sy.dma_start(out=rr(p.T("xn1T_b", write=True), "(dt pp) s -> pp dt s", pp=128)[:, :, c * 512:(c + 1) * 512],
                         in_=rr(xn1T[:], "p (dt s) -> p dt s", s=512)).then_inc(p.sem("p6sA"), 16)
            sy.wait_ge(p.sem("p6sA"), bSA + 16 * 2 * (c + 1))
            sy.sem_inc(p.sem("p6ck"), 1)
    p.semv["p6b2l"] = bB2L + 16
    p.semv["p6xt"] = bXT + 16 * NCH
    p.semv["p6xn"] = bXN + 16 * NCH
    p.semv["p6gf"] = bGF + 16 * NCH
    p.semv["p6gt"] = bGT + 16 * NCH
    p.semv["p6w1"] = bW1 + 16 * 8 * NCH
    p.semv["p6w2"] = bW2 + 16 * 8 * NCH
    p.semv["p6xl0"] = bXL[0] + 16 * 2 * NCH
    p.semv["p6xl1"] = bXL[1] + 16 * 2 * NCH
    p.semv["p6yl0"] = bYL[0] + 16 * 2 * NCH
    p.semv["p6yl1"] = bYL[1] + 16 * 2 * NCH
    p.semv["p6sA"] = bSA + 16 * 2 * NCH
    p.semv["p6ck"] = bCK + NCH

    @block.tensor
    def _(t_):
        for c in range(NCH):
            t_.wait_ge(p.sem("p6xt"), bXT + 16 * (c + 1))
            for e in range(NE):
                en = c * 8 + e
                t_.wait_ge(p.sem("p6w1"), bW1 + 16 * (en + 1))
                for och in range(16):
                    on = en * 16 + och
                    if on >= 2:
                        t_.wait_ge(p.sem("p6e1"), bE1 + on - 1)
                    for kt in range(8):
                        ins = t_.matmul(ps_h[:, (on % 2) * 512:(on % 2 + 1) * 512],
                                        W1b_sb[:, kt * 2048 + och * 128:kt * 2048 + (och + 1) * 128],
                                        xnT[:, kt * 512:(kt + 1) * 512],
                                        start=(kt == 0), stop=(kt == 7))
                    ins.then_inc(p.sem("p6m1"), 1)
                t_.wait_ge(p.sem("p6w2"), bW2 + 16 * (en + 1))
                t_.wait_ge(p.sem("p6e1"), bE1 + (en + 1) * 16)
                for t in range(4):
                    for half in range(2):
                        mn = en * 8 + t * 2 + half
                        if mn >= 2:
                            t_.wait_ge(p.sem("p6ae"), bAE + mn - 1)
                        for kt in range(16):
                            ins = t_.matmul(ps_o[:, (mn % 2) * 512:(mn % 2 + 1) * 512],
                                            heT[:, kt * 512 + t * 128:kt * 512 + (t + 1) * 128],
                                            W2b_sb[:, kt * 1024 + half * 512:kt * 1024 + (half + 1) * 512],
                                            start=(kt == 0), stop=(kt == 15))
                        ins.then_inc(p.sem("p6m2"), 1)
            t_.wait_ge(p.sem("p6gt"), bGT + 16 * (c + 1))
            t_.wait_ge(p.sem("p6b2l"), bB2L + 16)
            for t in range(4):
                for half in range(2):
                    bn = c * 8 + t * 2 + half
                    if bn >= 2:
                        t_.wait_ge(p.sem("p6cb1"), bCB1 + bn - 1)
                    t_.matmul(ps_b2[:, (bn % 2) * 512:(bn % 2 + 1) * 512],
                              gT_sb[0:8, t * 128:(t + 1) * 128],
                              b2b_sb[0:8, half * 512:(half + 1) * 512],
                              start=True, stop=True).then_inc(p.sem("p6b2"), 1)
            for t in range(4):
                g = c * 4 + t
                t_.wait_ge(p.sem("p6v"), V(c, 4))
                if g >= 2:
                    t_.wait_ge(p.sem("p6te"), bTE + g - 1)
                for dt in range(8):
                    ins = t_.transpose(ps_tr[:, (g % 2) * 1024 + dt * 128:(g % 2) * 1024 + (dt + 1) * 128],
                                       xn1bf[:, t * 1024 + dt * 128:t * 1024 + (dt + 1) * 128],
                                       ctx["eyeb"][:])
                    if dt == 7:
                        ins.then_inc(p.sem("p6t"), 1)
    p.semv["p6m1"] = bM1 + NCH * 128
    p.semv["p6m2"] = bM2 + NCH * 64
    p.semv["p6b2"] = bB2 + NCH * 8
    p.semv["p6t"] = bT + NCH * 4

    @block.scalar
    def _(s):
        for c in range(NCH):
            for e in range(NE):
                en = c * 8 + e
                for och in range(16):
                    on = en * 16 + och
                    s.wait_ge(p.sem("p6m1"), bM1 + on + 1)
                    s.activation(heT[:, och * 512:(och + 1) * 512],
                                 ps_h[:, (on % 2) * 512:(on % 2 + 1) * 512], AF.Relu,
                                 bias=ctx["b1"][:, e * 16 + och:e * 16 + och + 1]
                                 ).then_inc(p.sem("p6e1"), 1)
            s.wait_ge(p.sem("p6v"), V(c, 1))
            for t in range(4):
                ins = s.activation(rstdF[:, t:t + 1], mv[:, 2 * t + 1:2 * t + 2], AF.Ln, bias=EPS)
            p.inc(ins, "p6sc", 1)
            s.wait_ge(p.sem("p6sc"), p.val("p6sc"))
            for t in range(4):
                ins = s.activation(rstdF[:, t:t + 1], rstdF[:, t:t + 1], AF.Exp, scale=-0.5)
            ins.then_inc(p.sem("p6s"), 1)
            s.wait_ge(p.sem("p6v"), V(c, 3))
            for t in range(4):
                ins = s.activation(rstd1[:, t:t + 1], mvb[:, 2 * t + 1:2 * t + 2], AF.Ln, bias=EPS)
            p.inc(ins, "p6sc", 1)
            s.wait_ge(p.sem("p6sc"), p.val("p6sc"))
            for t in range(4):
                ins = s.activation(rstd1[:, t:t + 1], rstd1[:, t:t + 1], AF.Exp, scale=-0.5)
            ins.then_inc(p.sem("p6s"), 1)
    p.semv["p6e1"] = bE1 + NCH * 128
    p.semv["p6s"] = bS + NCH * 2

    @block.gpsimd
    def _(g_):
        for c in range(NCH):
            g_.wait_ge(p.sem("p6v"), V(c, 2))
            for t in range(4):
                ins = g_.tensor_mul(acc[:, t * 1024:(t + 1) * 1024], acc[:, t * 1024:(t + 1) * 1024],
                                    ctx["gFbc"][:])
            ins.then_inc(p.sem("p6g"), 1)
    p.semv["p6g"] = bG + NCH

    @block.vector
    def _(v):
        for c in range(NCH):
            v.wait_ge(p.sem("p6gf"), bGF + 16 * (c + 1))
            if c >= 1:
                v.wait_ge(p.sem("p6ck"), bCK + c)
            for e in range(NE):
                en = c * 8 + e
                for t in range(4):
                    for half in range(2):
                        mn = en * 8 + t * 2 + half
                        v.wait_ge(p.sem("p6m2"), bM2 + mn + 1)
                        dst = acc[:, t * 1024 + half * 512:t * 1024 + (half + 1) * 512]
                        gcol = gf_sb[:, t * 8 + e:t * 8 + e + 1]
                        src = ps_o[:, (mn % 2) * 512:(mn % 2 + 1) * 512]
                        if e == 0:
                            ins = v.tensor_scalar_mul(dst, src, gcol)
                        else:
                            ins = v.scalar_tensor_tensor(dst, src, gcol, dst, ALU.mult, ALU.add)
                        ins.then_inc(p.sem("p6ae"), 1)
            v.wait_ge(p.sem("p6xn"), bXN + 16 * (c + 1))
            for t in range(4):
                for half in range(2):
                    bn = c * 8 + t * 2 + half
                    v.wait_ge(p.sem("p6b2"), bB2 + bn + 1)
                    dst = acc[:, t * 1024 + half * 512:t * 1024 + (half + 1) * 512]
                    ins = v.tensor_add(dst, dst, ps_b2[:, (bn % 2) * 512:(bn % 2 + 1) * 512])
                    p.inc(ins, "p6vs", 1)
                    v.wait_ge(p.sem("p6vs"), p.val("p6vs"))
                    ins = v.tensor_add(dst, dst, xn_tm[:, t * 1024 + half * 512:t * 1024 + (half + 1) * 512])
                    ins.then_inc(p.sem("p6cb1"), 1)
            v.wait_ge(p.sem("p6cb1"), bCB1 + 8 * (c + 1))
            for t in range(4):
                v.bn_stats(st6[:, t * 12:t * 12 + 6], acc[:, t * 1024:t * 1024 + 512])
                ins = v.bn_stats(st6[:, t * 12 + 6:t * 12 + 12], acc[:, t * 1024 + 512:t * 1024 + 1024])
            p.inc(ins, "p6vs", 1)
            v.wait_ge(p.sem("p6vs"), p.val("p6vs"))
            for t in range(4):
                ins = v.bn_aggr(mv[:, 2 * t:2 * t + 2], st6[:, t * 12:t * 12 + 12])
            ins.then_inc(p.sem("p6v"), 1)
            v.wait_ge(p.sem("p6v"), V(c, 1))
            for t in range(4):
                ins = v.tensor_single_scalar(acc[:, t * 1024:(t + 1) * 1024],
                                             acc[:, t * 1024:(t + 1) * 1024],
                                             mv[:, 2 * t:2 * t + 1], ALU.subtract)
            ins.then_inc(p.sem("p6v"), 1)
            v.wait_ge(p.sem("p6g"), bG + c + 1)
            v.wait_ge(p.sem("p6s"), bS + c * 2 + 1)
            for t in range(4):
                ins = v.scalar_tensor_tensor(acc[:, t * 1024:(t + 1) * 1024],
                                             acc[:, t * 1024:(t + 1) * 1024], rstdF[:, t:t + 1],
                                             ctx["bFbc"][:], ALU.mult, ALU.add)
            p.inc(ins, "p6vs", 1)
            v.wait_ge(p.sem("p6vs"), p.val("p6vs"))
            for t in range(4):
                tn = c * 4 + t
                sl = t % 2
                v.wait_ge(p.sem("p6xl%d" % sl), bXL[sl] + 16 * (tn // 2 + 1))
                v.wait_ge(p.sem("p6yl%d" % sl), bYL[sl] + 16 * (tn // 2 + 1))
                dst = acc[:, t * 1024:(t + 1) * 1024]
                ins = v.tensor_add(dst, dst, x2[:, sl * 1024:(sl + 1) * 1024])
                p.inc(ins, "p6vs", 1)
                v.wait_ge(p.sem("p6vs"), p.val("p6vs"))
                ins = v.tensor_add(dst, dst, xn_tm[:, t * 1024:(t + 1) * 1024])
                p.inc(ins, "p6vs", 1)
                v.wait_ge(p.sem("p6vs"), p.val("p6vs"))
                ins = v.tensor_add(dst, dst, yW2[:, sl * 1024:(sl + 1) * 1024])
                ins.then_inc(p.sem("p6cb"), 1)
            v.wait_ge(p.sem("p6cb"), bCB + 4 * (c + 1))
            for t in range(4):
                v.bn_stats(st6b[:, t * 12:t * 12 + 6], acc[:, t * 1024:t * 1024 + 512])
                ins = v.bn_stats(st6b[:, t * 12 + 6:t * 12 + 12], acc[:, t * 1024 + 512:t * 1024 + 1024])
            p.inc(ins, "p6vs", 1)
            v.wait_ge(p.sem("p6vs"), p.val("p6vs"))
            for t in range(4):
                ins = v.bn_aggr(mvb[:, 2 * t:2 * t + 2], st6b[:, t * 12:t * 12 + 12])
            ins.then_inc(p.sem("p6v"), 1)
            v.wait_ge(p.sem("p6v"), V(c, 3))
            v.wait_ge(p.sem("p6s"), bS + c * 2 + 2)
            for t in range(4):
                ins = v.tensor_scalar(xn1bf[:, t * 1024:(t + 1) * 1024], acc[:, t * 1024:(t + 1) * 1024],
                                      mvb[:, 2 * t:2 * t + 1], rstd1[:, t:t + 1], ALU.subtract, ALU.mult)
            ins.then_inc(p.sem("p6v"), 1)
            for t in range(4):
                g = c * 4 + t
                v.wait_ge(p.sem("p6t"), bT + g + 1)
                v.tensor_copy(
                    rr(xn1T[:], "p (dt s) -> p dt s", s=512)[:, :, t * 128:(t + 1) * 128],
                    rr(ps_tr[:, (g % 2) * 1024:(g % 2 + 1) * 1024], "p (dt s) -> p dt s", s=128)
                ).then_inc(p.sem("p6te"), 1)
    p.semv["p6ae"] = bAE + NCH * 64
    p.semv["p6cb1"] = bCB1 + NCH * 8
    p.semv["p6v"] = bV + NCH * NV
    p.semv["p6cb"] = bCB + NCH * 4
    p.semv["p6te"] = bTE + NCH * 4
    st.close()
    barrier(p, block)


def emit_p7(p, block, ctx):
    st = contextlib.ExitStack()
    xn1T = p.sb("p7_xn1T", [128, 8 * 512], BF16, st)
    WqT_sb = p.sb("p7_WqT", [128, 8 * 1024], BF16, st)
    WkT_sb = p.sb("p7_WkT", [128, 8 * 1024], BF16, st)
    WvT_sb = p.sb("p7_WvT", [128, 8 * 1024], BF16, st)
    WoT_sb = p.sb("p7_WoT", [128, 8 * 1024], BF16, st)
    qT_c = p.sb("p7_qT", [128, 8 * 512], BF16, st)
    kT_c = p.sb("p7_kT", [128, 8 * 512], BF16, st)
    v_c = p.sb("p7_v", [128, 4 * 1024], BF16, st)
    yT_c = p.sb("p7_yT", [128, 8 * 512], BF16, st)
    Sm = p.sb("p7_Sm", [128, 2 * 256], F32, st)
    P_sb = p.sb("p7_P", [128, 2 * 256], BF16, st)
    PT_sb = p.sb("p7_PT", [128, 2 * 256], BF16, st)
    nmx = p.sb("p7_nmx", [128, 2], F32, st)
    rsum = p.sb("p7_rsum", [128, 2], F32, st)
    rinv = p.sb("p7_rinv", [128, 2], F32, st)
    x12 = p.sb("p7_x12", [128, 2 * 1024], F32, st)
    part2 = p.sb("p7_part2", [128, 2 * 1024], F32, st)
    ps_proj = p.ps("p7_psproj", [128, 2 * 512], F32, st)
    ps_S = p.ps("p7_psS", [128, 1024], F32, st)
    ps_PT = p.ps("p7_psPT", [128, 2 * 1024], BF16, st)
    ps_y = p.ps("p7_psy", [64, 2 * 512], F32, st)
    mask = p.sb("p7_mask", [128, 3 * 128], F32, st)
    kprev = p.sb("p7_kprev", [128, 8 * 128], BF16, st)
    vprev = p.sb("p7_vprev", [128, D], BF16, st)

    bWL = p.val("p7wl"); bX = p.val("p7x"); bPM = p.val("p7pm"); bPE = p.val("p7pe")
    bS7 = p.val("p7s"); bSM = p.val("p7sm"); bVC = p.val("p7vc"); bSC = p.val("p7sc")
    bPT = p.val("p7pt"); bPTE = p.val("p7pte"); bYM = p.val("p7ym"); bYE = p.val("p7ye")
    bWM = p.val("p7wm"); bWE = p.val("p7we"); bKV = p.val("p7kv"); bCK = p.val("p7ck")
    bSA0 = p.val("p7sA0"); bSA1 = p.val("p7sA1")
    bXL = [p.val("p7xl0"), p.val("p7xl1")]

    @block.sync
    def _(sy):
        for i, nm in enumerate(["WqT", "WkT", "WvT", "WoT"]):
            dst = [WqT_sb, WkT_sb, WvT_sb, WoT_sb][i]
            sy.dma_start(out=rr(dst[:], "p (kt n) -> p kt n", kt=8),
                         in_=pt(p.T(nm), 0, 8, 0, 1024)).then_inc(p.sem("p7wl"), 16)
        sy.dma_start(out=mask[:, 0:128], in_=p.T("mask_prev")).then_inc(p.sem("p7wl"), 16)
        sy.dma_start(out=mask[:, 128:256], in_=p.T("mask_cur")).then_inc(p.sem("p7wl"), 16)
        for c in range(NCH):
            if c >= 1:
                sy.wait_ge(p.sem("p7ck"), bCK + c)
            sy.dma_start(out=rr(xn1T[:], "p (dt s) -> p dt s", s=512),
                         in_=rr(p.T("xn1T_b"), "(dt pp) s -> pp dt s", pp=128)[:, :, c * 512:(c + 1) * 512]
                         ).then_inc(p.sem("p7x"), 16)
            for t in range(4):
                tn = c * 4 + t
                sl = t % 2
                if tn >= 2:
                    sy.wait_ge(p.sem("p7we"), bWE + 2 * (tn - 1))
                sy.dma_start(out=x12[:, sl * 1024:(sl + 1) * 1024],
                             in_=pt(p.T("x1_f"), c * 512 + t * 128, 1, 0, 1024)[:, 0, :]
                             ).then_inc(p.sem("p7xl%d" % sl), 16)
            for t in range(4):
                tn = c * 4 + t
                sl = t % 2
                sy.wait_ge(p.sem("p7we"), bWE + 8 * c + 2 * t + 2)
                sy.dma_start(out=pt(p.T("partial_f", write=True), c * 512 + t * 128, 1, 0, 1024)[:, 0, :],
                             in_=part2[:, sl * 1024:(sl + 1) * 1024]).then_inc(p.sem("p7sA%d" % sl), 16)
            sy.wait_ge(p.sem("p7sA0"), bSA0 + 16 * 2 * (c + 1))
            sy.wait_ge(p.sem("p7sA1"), bSA1 + 16 * 2 * (c + 1))
            sy.wait_ge(p.sem("p7kv"), bKV + (c + 1))
            sy.sem_inc(p.sem("p7ck"), 1)
    p.semv["p7wl"] = bWL + 96
    p.semv["p7x"] = bX + 16 * NCH
    p.semv["p7xl0"] = bXL[0] + 16 * 2 * NCH
    p.semv["p7xl1"] = bXL[1] + 16 * 2 * NCH
    p.semv["p7sA0"] = bSA0 + 16 * 2 * NCH
    p.semv["p7sA1"] = bSA1 + 16 * 2 * NCH
    p.semv["p7ck"] = bCK + NCH

    @block.gpsimd
    def _(g_):
        g_.memset(mask[:, 256:384], NEG)
        g_.memset(kprev[:], 0.0)
        g_.memset(vprev[:], 0.0).then_inc(p.sem("p7mi"), 1)
    p.semv["p7mi"] = p.val("p7mi") + 1 - 1 + 1

    @block.tensor
    def _(t_):
        t_.wait_ge(p.sem("p7wl"), bWL + 96)
        t_.wait_ge(p.sem("p7mi"), p.val("p7mi"))
        for c in range(NCH):
            t_.wait_ge(p.sem("p7x"), bX + 16 * (c + 1))
            if c >= 1:
                t_.wait_ge(p.sem("p7kv"), bKV + c)
                t_.wait_ge(p.sem("p7we"), bWE + 8 * c)
            pj = 0
            for wsb, dstname in [(WqT_sb, "q"), (WkT_sb, "k")]:
                for och in range(8):
                    pn = c * 24 + pj
                    if pn >= 2:
                        t_.wait_ge(p.sem("p7pe"), bPE + pn - 1)
                    for kt in range(8):
                        ins = t_.matmul(ps_proj[:, (pn % 2) * 512:(pn % 2 + 1) * 512],
                                        wsb[:, kt * 1024 + och * 128:kt * 1024 + (och + 1) * 128],
                                        xn1T[:, kt * 512:(kt + 1) * 512],
                                        start=(kt == 0), stop=(kt == 7))
                    ins.then_inc(p.sem("p7pm"), 1)
                    pj += 1
            for t in range(4):
                for half in range(2):
                    pn = c * 24 + pj
                    if pn >= 2:
                        t_.wait_ge(p.sem("p7pe"), bPE + pn - 1)
                    for kt in range(8):
                        ins = t_.matmul(ps_proj[:, (pn % 2) * 512:(pn % 2 + 1) * 512],
                                        xn1T[:, kt * 512 + t * 128:kt * 512 + (t + 1) * 128],
                                        WvT_sb[:, kt * 1024 + half * 512:kt * 1024 + (half + 1) * 512],
                                        start=(kt == 0), stop=(kt == 7))
                    ins.then_inc(p.sem("p7pm"), 1)
                    pj += 1
            t_.wait_ge(p.sem("p7pe"), bPE + c * 24 + 24)
            for b in range(4):
                for h in range(H):
                    an = (c * 4 + b) * H + h
                    ht, hp = h // 2, (h % 2) * 64
                    sslot = an % 2
                    if an >= 2:
                        t_.wait_ge(p.sem("p7sm"), bSM + 2 * (an - 1))
                    qsl = qT_c[hp:hp + 64, ht * 512 + b * 128:ht * 512 + (b + 1) * 128]
                    if b == 0:
                        kprev_sl = kprev[hp:hp + 64, ht * 128:(ht + 1) * 128]
                    else:
                        kprev_sl = kT_c[hp:hp + 64, ht * 512 + (b - 1) * 128:ht * 512 + b * 128]
                    kcur_sl = kT_c[hp:hp + 64, ht * 512 + b * 128:ht * 512 + (b + 1) * 128]
                    t_.matmul(ps_S[:, sslot * 512:sslot * 512 + 128], qsl, kprev_sl,
                              start=True, stop=True, tile_position=(hp, 0))
                    t_.matmul(ps_S[:, sslot * 512 + 128:sslot * 512 + 256], qsl, kcur_sl,
                              start=True, stop=True, tile_position=(hp, 0)).then_inc(p.sem("p7s"), 1)
                    so = sslot * 256
                    t_.wait_ge(p.sem("p7vc"), bVC + an * 3 + 3)
                    if an >= 2:
                        t_.wait_ge(p.sem("p7pte"), bPTE + an - 1)
                    t_.transpose(ps_PT[:, sslot * 1024:sslot * 1024 + 128], P_sb[:, so:so + 128], ctx["eyeb"][:])
                    t_.transpose(ps_PT[:, sslot * 1024 + 128:sslot * 1024 + 256], P_sb[:, so + 128:so + 256],
                                 ctx["eyeb"][:]).then_inc(p.sem("p7pt"), 1)
                    t_.wait_ge(p.sem("p7pte"), bPTE + an + 1)
                    if an >= 2:
                        t_.wait_ge(p.sem("p7ye"), bYE + an - 1)
                    if b == 0:
                        vprev_sl = vprev[:, h * 64:(h + 1) * 64]
                    else:
                        vprev_sl = v_c[:, (b - 1) * 1024 + h * 64:(b - 1) * 1024 + (h + 1) * 64]
                    vcur_sl = v_c[:, b * 1024 + h * 64:b * 1024 + (h + 1) * 64]
                    t_.matmul(ps_y[0:64, sslot * 512:sslot * 512 + 128], vprev_sl, PT_sb[:, so:so + 128],
                              start=True, stop=False)
                    t_.matmul(ps_y[0:64, sslot * 512:sslot * 512 + 128], vcur_sl, PT_sb[:, so + 128:so + 256],
                              start=False, stop=True).then_inc(p.sem("p7ym"), 1)
            t_.wait_ge(p.sem("p7ye"), bYE + (c + 1) * 64)
            for t in range(4):
                for half in range(2):
                    wn = c * 8 + t * 2 + half
                    if wn >= 2:
                        t_.wait_ge(p.sem("p7we"), bWE + wn - 1)
                    for kt in range(8):
                        ins = t_.matmul(ps_proj[:, (wn % 2) * 512:(wn % 2 + 1) * 512],
                                        yT_c[:, kt * 512 + t * 128:kt * 512 + (t + 1) * 128],
                                        WoT_sb[:, kt * 1024 + half * 512:kt * 1024 + (half + 1) * 512],
                                        start=(kt == 0), stop=(kt == 7))
                    ins.then_inc(p.sem("p7wm"), 1)
    p.semv["p7pm"] = bPM + NCH * 24
    p.semv["p7s"] = bS7 + NCH * 64
    p.semv["p7pt"] = bPT + NCH * 64
    p.semv["p7ym"] = bYM + NCH * 64
    p.semv["p7wm"] = bWM + NCH * 8

    @block.scalar
    def _(s):
        for c in range(NCH):
            pj = 0
            for dst in [qT_c, kT_c]:
                for och in range(8):
                    pn = c * 24 + pj
                    s.wait_ge(p.sem("p7pm"), bPM + pn + 1)
                    s.copy(dst[:, och * 512:(och + 1) * 512],
                           ps_proj[:, (pn % 2) * 512:(pn % 2 + 1) * 512]).then_inc(p.sem("p7pe"), 1)
                    pj += 1
            for t in range(4):
                for half in range(2):
                    pn = c * 24 + pj
                    s.wait_ge(p.sem("p7pm"), bPM + pn + 1)
                    s.copy(v_c[:, t * 1024 + half * 512:t * 1024 + (half + 1) * 512],
                           ps_proj[:, (pn % 2) * 512:(pn % 2 + 1) * 512]).then_inc(p.sem("p7pe"), 1)
                    pj += 1
            for b in range(4):
                for h in range(H):
                    an = (c * 4 + b) * H + h
                    sslot = an % 2
                    so = sslot * 256
                    s.wait_ge(p.sem("p7vc"), bVC + an * 3 + 1)
                    if an >= 2:
                        s.wait_ge(p.sem("p7pt"), bPT + an - 1)
                    s.activation(P_sb[:, so:so + 256], Sm[:, so:so + 256], AF.Exp,
                                 bias=nmx[:, sslot:sslot + 1],
                                 accum_out=rsum[:, sslot:sslot + 1]).then_inc(p.sem("p7sc"), 1)
    p.semv["p7pe"] = bPE + NCH * 24
    p.semv["p7sc"] = bSC + NCH * 64

    @block.vector
    def _(v):
        v.wait_ge(p.sem("p7wl"), bWL + 96)
        v.wait_ge(p.sem("p7mi"), p.val("p7mi"))
        for c in range(NCH):
            for b in range(4):
                for h in range(H):
                    an = (c * 4 + b) * H + h
                    ht, hp = h // 2, (h % 2) * 64
                    sslot = an % 2
                    first = (c == 0 and b == 0)
                    v.wait_ge(p.sem("p7s"), bS7 + an + 1)
                    if an >= 2:
                        v.wait_ge(p.sem("p7sc"), bSC + an - 1)
                    so = sslot * 256
                    mprev = mask[:, 256:384] if first else mask[:, 0:128]
                    v.scalar_tensor_tensor(Sm[:, so:so + 128], ps_S[:, sslot * 512:sslot * 512 + 128],
                                           0.125, mprev, ALU.mult, ALU.add).then_inc(p.sem("p7sm"), 1)
                    ins = v.scalar_tensor_tensor(Sm[:, so + 128:so + 256], ps_S[:, sslot * 512 + 128:sslot * 512 + 256],
                                                 0.125, mask[:, 128:256], ALU.mult, ALU.add)
                    ins.then_inc(p.sem("p7sm"), 1)
                    v.wait_ge(p.sem("p7sm"), bSM + 2 * an + 2)
                    v.reduce_max(nmx[:, sslot:sslot + 1], Sm[:, so:so + 256], axis=AX.X,
                                 negate=True).then_inc(p.sem("p7vc"), 1)
                    v.wait_ge(p.sem("p7sc"), bSC + an + 1)
                    v.reciprocal(rinv[:, sslot:sslot + 1], rsum[:, sslot:sslot + 1]).then_inc(p.sem("p7vc"), 1)
                    v.wait_ge(p.sem("p7vc"), bVC + an * 3 + 2)
                    v.tensor_scalar_mul(P_sb[:, so:so + 256], P_sb[:, so:so + 256],
                                        rinv[:, sslot:sslot + 1]).then_inc(p.sem("p7vc"), 1)
                    v.wait_ge(p.sem("p7pt"), bPT + an + 1)
                    v.tensor_copy(PT_sb[:, so:so + 128], ps_PT[:, sslot * 1024:sslot * 1024 + 128])
                    v.tensor_copy(PT_sb[:, so + 128:so + 256], ps_PT[:, sslot * 1024 + 128:sslot * 1024 + 256]
                                  ).then_inc(p.sem("p7pte"), 1)
                    v.wait_ge(p.sem("p7ym"), bYM + an + 1)
                    v.tensor_copy(yT_c[hp:hp + 64, ht * 512 + b * 128:ht * 512 + (b + 1) * 128],
                                  ps_y[0:64, sslot * 512:sslot * 512 + 128]).then_inc(p.sem("p7ye"), 1)
            v.tensor_copy(rr(kprev[:], "p (ht s) -> p ht s", s=128),
                          rr(kT_c[:], "p (ht s) -> p ht s", s=512)[:, :, 384:512])
            v.tensor_copy(vprev[:], v_c[:, 3 * 1024:4 * 1024]).then_inc(p.sem("p7kv"), 1)
            for t in range(4):
                for half in range(2):
                    wn = c * 8 + t * 2 + half
                    v.wait_ge(p.sem("p7wm"), bWM + wn + 1)
                    tn = c * 4 + t
                    sl = t % 2
                    v.wait_ge(p.sem("p7xl%d" % sl), bXL[sl] + 16 * (tn // 2 + 1))
                    if tn >= 2:
                        v.wait_ge(p.sem("p7sA%d" % sl), [bSA0, bSA1][sl] + 16 * (tn // 2))
                    v.tensor_add(part2[:, sl * 1024 + half * 512:sl * 1024 + (half + 1) * 512],
                                 ps_proj[:, (wn % 2) * 512:(wn % 2 + 1) * 512],
                                 x12[:, sl * 1024 + half * 512:sl * 1024 + (half + 1) * 512]
                                 ).then_inc(p.sem("p7we"), 1)
    p.semv["p7sm"] = bSM + NCH * 128
    p.semv["p7vc"] = bVC + NCH * 192
    p.semv["p7pte"] = bPTE + NCH * 64
    p.semv["p7ye"] = bYE + NCH * 64
    p.semv["p7kv"] = bKV + NCH
    p.semv["p7we"] = bWE + NCH * 8
    st.close()
    barrier(p, block)


def emit_p8(p, block, ctx):
    st = contextlib.ExitStack()
    xfull = p.sb("p8_x", [128, 8 * 4096], BF16, st)
    w1b = p.sb("p8_w1", [128, 2 * 8 * 128], BF16, st)
    w2b = p.sb("p8_w2", [128, 2 * 8 * 128], BF16, st)
    s1b = p.sb("p8_s1", [128, 2 * 512], BF16, st)
    ab = p.sb("p8_ab", [128, 2 * 512], BF16, st)
    ps1 = p.ps("p8_ps1", [128, 2 * 512], F32, st)
    ps2 = p.ps("p8_ps2", [128, 2 * 512], F32, st)

    bX = p.val("p8x"); bW = [p.val("p8w0"), p.val("p8w1")]
    bM = p.val("p8m"); bE1 = p.val("p8e1"); bE2 = p.val("p8e2")
    bST = [p.val("p8st0"), p.val("p8st1")]

    @block.sync
    def _(sy):
        sy.dma_start(out=rr(xfull[:], "p (kt s) -> p kt s", kt=8),
                     in_=rr(p.T("xn1T_b"), "(kt pp) s -> pp kt s", pp=128)).then_inc(p.sem("p8x"), 16)
        for och in range(32):
            sl = och % 2
            if och >= 2:
                sy.wait_ge(p.sem("p8m"), bM + NCH * (och - 1))
            sy.dma_start(out=rr(w1b[:, sl * 1024:(sl + 1) * 1024], "p (kt n) -> p kt n", kt=8),
                         in_=rr(p.T("w1T"), "(kt pp) n -> pp kt n", pp=128)[:, :, och * 128:(och + 1) * 128]
                         ).then_inc(p.sem("p8w%d" % sl), 16)
            sy.dma_start(out=rr(w2b[:, sl * 1024:(sl + 1) * 1024], "p (kt n) -> p kt n", kt=8),
                         in_=rr(p.T("w2T"), "(kt pp) n -> pp kt n", pp=128)[:, :, och * 128:(och + 1) * 128]
                         ).then_inc(p.sem("p8w%d" % sl), 16)

    @block.gpsimd
    def _(g):
        for och in range(32):
            for tc in range(NCH):
                j = och * NCH + tc
                sl = j % 2
                g.wait_ge(p.sem("p8e2"), bE2 + j + 1)
                g.dma_start(out=rr(p.T("aT_b", write=True), "(oc pp) s -> pp oc s", pp=128
                                   )[:, och, tc * 512:(tc + 1) * 512],
                            in_=ab[:, sl * 512:(sl + 1) * 512]).then_inc(p.sem("p8st%d" % sl), 16)

    @block.tensor
    def _(t_):
        t_.wait_ge(p.sem("p8x"), bX + 16)
        for och in range(32):
            wsl = och % 2
            t_.wait_ge(p.sem("p8w%d" % wsl), bW[wsl] + 32 * ((och - wsl) // 2 + 1))
            for tc in range(NCH):
                j = och * NCH + tc
                psl = j % 2
                if j >= 2:
                    t_.wait_ge(p.sem("p8e1"), bE1 + j - 1)
                for kt in range(8):
                    ins = t_.matmul(ps1[:, psl * 512:(psl + 1) * 512],
                                    w1b[:, wsl * 1024 + kt * 128:wsl * 1024 + (kt + 1) * 128],
                                    xfull[:, kt * 4096 + tc * 512:kt * 4096 + (tc + 1) * 512],
                                    start=(kt == 0), stop=(kt == 7))
                if j >= 2:
                    t_.wait_ge(p.sem("p8e2"), bE2 + j - 1)
                for kt in range(8):
                    ins2 = t_.matmul(ps2[:, psl * 512:(psl + 1) * 512],
                                     w2b[:, wsl * 1024 + kt * 128:wsl * 1024 + (kt + 1) * 128],
                                     xfull[:, kt * 4096 + tc * 512:kt * 4096 + (tc + 1) * 512],
                                     start=(kt == 0), stop=(kt == 7))
                ins2.then_inc(p.sem("p8m"), 1)
    p.semv["p8x"] = bX + 16
    p.semv["p8m"] = bM + 32 * NCH
    p.semv["p8w0"] = bW[0] + 16 * 32
    p.semv["p8w1"] = bW[1] + 16 * 32

    @block.scalar
    def _(s):
        for och in range(32):
            for tc in range(NCH):
                j = och * NCH + tc
                psl = j % 2
                s.wait_ge(p.sem("p8m"), bM + j + 1)
                if j >= 2:
                    s.wait_ge(p.sem("p8e2"), bE2 + j - 1)
                s.activation(s1b[:, psl * 512:(psl + 1) * 512], ps1[:, psl * 512:(psl + 1) * 512],
                             AF.Silu, scale=ctx["w1s"][:, och:och + 1]).then_inc(p.sem("p8e1"), 1)
    p.semv["p8e1"] = bE1 + 32 * NCH

    @block.vector
    def _(v):
        for och in range(32):
            for tc in range(NCH):
                j = och * NCH + tc
                psl = j % 2
                v.wait_ge(p.sem("p8m"), bM + j + 1)
                v.wait_ge(p.sem("p8e1"), bE1 + j + 1)
                if j >= 2:
                    v.wait_ge(p.sem("p8st%d" % psl), bST[psl] + 16 * (j // 2))
                v.scalar_tensor_tensor(ab[:, psl * 512:(psl + 1) * 512],
                                       ps2[:, psl * 512:(psl + 1) * 512], ctx["w2s"][:, och:och + 1],
                                       s1b[:, psl * 512:(psl + 1) * 512], ALU.mult, ALU.mult
                                       ).then_inc(p.sem("p8e2"), 1)
    p.semv["p8e2"] = bE2 + 32 * NCH
    p.semv["p8st0"] = bST[0] + 16 * 16 * NCH
    p.semv["p8st1"] = bST[1] + 16 * 16 * NCH

    @block.sync
    def _(sy):
        sy.wait_ge(p.sem("p8st0"), bST[0] + 16 * 16 * NCH)
        sy.wait_ge(p.sem("p8st1"), bST[1] + 16 * 16 * NCH)
    st.close()
    barrier(p, block)


def emit_hpass(p, block, ctx, src_nm, w_nm, s_key, dst_nm, tag):
    st = contextlib.ExitStack()
    aq = p.sb(tag + "_aq", [128, 32 * 1024], BF16, st)
    wb = p.sb(tag + "_wb", [128, 2 * 32 * 128], BF16, st)
    bb = p.sb(tag + "_bb", [128, 2 * 512], BF16, st)
    ps = p.ps(tag + "_ps", [128, 2 * 512], F32, st)

    bA = p.val(tag + "a"); bW = [p.val(tag + "w0"), p.val(tag + "w1")]
    bM = p.val(tag + "m"); bE = p.val(tag + "e")
    bST = [p.val(tag + "st0"), p.val(tag + "st1")]
    bCK = p.val(tag + "ck")

    @block.sync
    def _(sy):
        for q in range(4):
            if q >= 1:
                sy.wait_ge(p.sem(tag + "ck"), bCK + q)
            sy.dma_start(out=rr(aq[:], "p (kt s) -> p kt s", kt=32),
                         in_=rr(p.T(src_nm), "(kt pp) s -> pp kt s", pp=128)[:, :, q * 1024:(q + 1) * 1024]
                         ).then_inc(p.sem(tag + "a"), 16)
            for och in range(32):
                n = q * 32 + och
                sl = n % 2
                if n >= 2:
                    sy.wait_ge(p.sem(tag + "m"), bM + 2 * (n - 1))
                sy.dma_start(out=rr(wb[:, sl * 4096:(sl + 1) * 4096], "p (kt n) -> p kt n", kt=32),
                             in_=rr(p.T(w_nm), "(kt pp) n -> pp kt n", pp=128)[:, :, och * 128:(och + 1) * 128]
                             ).then_inc(p.sem(tag + "w%d" % sl), 16)
            sy.wait_ge(p.sem(tag + "m"), bM + 2 * 32 * (q + 1))
            sy.sem_inc(p.sem(tag + "ck"), 1)
    p.semv[tag + "a"] = bA + 16 * 4
    p.semv[tag + "ck"] = bCK + 4

    @block.tensor
    def _(t_):
        for q in range(4):
            t_.wait_ge(p.sem(tag + "a"), bA + 16 * (q + 1))
            for och in range(32):
                n = q * 32 + och
                wsl = n % 2
                t_.wait_ge(p.sem(tag + "w%d" % wsl), bW[wsl] + 16 * (n // 2 + 1))
                for tt in range(2):
                    j = n * 2 + tt
                    psl = j % 2
                    if j >= 2:
                        t_.wait_ge(p.sem(tag + "e"), bE + j - 1)
                    for kt in range(32):
                        ins = t_.matmul(ps[:, psl * 512:(psl + 1) * 512],
                                        wb[:, wsl * 4096 + kt * 128:wsl * 4096 + (kt + 1) * 128],
                                        aq[:, kt * 1024 + tt * 512:kt * 1024 + (tt + 1) * 512],
                                        start=(kt == 0), stop=(kt == 31))
                    ins.then_inc(p.sem(tag + "m"), 1)
    p.semv[tag + "m"] = bM + 256
    p.semv[tag + "w0"] = bW[0] + 16 * 64
    p.semv[tag + "w1"] = bW[1] + 16 * 64

    @block.scalar
    def _(s):
        for q in range(4):
            for och in range(32):
                for tt in range(2):
                    j = (q * 32 + och) * 2 + tt
                    psl = j % 2
                    s.wait_ge(p.sem(tag + "m"), bM + j + 1)
                    if j >= 2:
                        s.wait_ge(p.sem(tag + "st%d" % psl), bST[psl] + 16 * (j // 2))
                    s.activation(bb[:, psl * 512:(psl + 1) * 512], ps[:, psl * 512:(psl + 1) * 512],
                                 AF.Gelu, scale=ctx[s_key][:, och:och + 1]).then_inc(p.sem(tag + "e"), 1)
    p.semv[tag + "e"] = bE + 256

    @block.gpsimd
    def _(g):
        for q in range(4):
            for och in range(32):
                for tt in range(2):
                    j = (q * 32 + och) * 2 + tt
                    psl = j % 2
                    g.wait_ge(p.sem(tag + "e"), bE + j + 1)
                    g.dma_start(out=rr(p.T(dst_nm, write=True), "(oc pp) s -> pp oc s", pp=128
                                       )[:, och, q * 1024 + tt * 512:q * 1024 + (tt + 1) * 512],
                                in_=bb[:, psl * 512:(psl + 1) * 512]).then_inc(p.sem(tag + "st%d" % psl), 16)
            g.wait_ge(p.sem(tag + "st0"), bST[0] + 16 * 32 * (q + 1))
            g.wait_ge(p.sem(tag + "st1"), bST[1] + 16 * 32 * (q + 1))
    p.semv[tag + "st0"] = bST[0] + 16 * 128
    p.semv[tag + "st1"] = bST[1] + 16 * 128
    st.close()
    barrier(p, block)


def emit_p9(p, block, ctx):
    emit_hpass(p, block, ctx, "aT_b", "h1T", "h1s", "bT_b", "p9")


def emit_p10(p, block, ctx):
    emit_hpass(p, block, ctx, "bT_b", "h2T", "h2s", "cT_b", "p10")


def emit_p11(p, block, ctx):
    st = contextlib.ExitStack()
    w3_sb = p.sb("p11_w3", [128, 32 * 1024], BF16, st)
    cc = p.sb("p11_c", [128, 32 * 512], BF16, st)
    pr = p.sb("p11_pr", [128, 2 * 1024], F32, st)
    ob = p.sb("p11_ob", [128, 2 * 1024], F32, st)
    ps = p.ps("p11_ps", [128, 2 * 512], F32, st)

    bW = p.val("p11w"); bC = p.val("p11c"); bM = p.val("p11m"); bE = p.val("p11e")
    bST = [p.val("p11st0"), p.val("p11st1")]
    bPR = [p.val("p11pr0"), p.val("p11pr1")]
    bCK = p.val("p11ck")

    @block.sync
    def _(sy):
        sy.dma_start(out=rr(w3_sb[:], "p (kt n) -> p kt n", kt=32),
                     in_=pt(p.T("w3Ts"), 0, 32, 0, 1024)).then_inc(p.sem("p11w"), 16)
        for c in range(NCH):
            if c >= 1:
                sy.wait_ge(p.sem("p11ck"), bCK + c)
            sy.dma_start(out=rr(cc[:], "p (kt s) -> p kt s", kt=32),
                         in_=rr(p.T("cT_b"), "(kt pp) s -> pp kt s", pp=128)[:, :, c * 512:(c + 1) * 512]
                         ).then_inc(p.sem("p11c"), 16)
            for t in range(4):
                tn = c * 4 + t
                sl = t % 2
                if tn >= 2:
                    sy.wait_ge(p.sem("p11e"), bE + 2 * (tn - 1))
                sy.dma_start(out=pr[:, sl * 1024:(sl + 1) * 1024],
                             in_=pt(p.T("partial_f"), c * 512 + t * 128, 1, 0, 1024)[:, 0, :]
                             ).then_inc(p.sem("p11pr%d" % sl), 16)
            for t in range(4):
                tn = c * 4 + t
                sl = t % 2
                sy.wait_ge(p.sem("p11e"), bE + 2 * tn + 2)
                sy.dma_start(out=pt(p.T("out", write=True), c * 512 + t * 128, 1, 0, 1024)[:, 0, :],
                             in_=ob[:, sl * 1024:(sl + 1) * 1024]).then_inc(p.sem("p11st%d" % sl), 16)
            sy.wait_ge(p.sem("p11st0"), bST[0] + 16 * 2 * (c + 1))
            sy.wait_ge(p.sem("p11st1"), bST[1] + 16 * 2 * (c + 1))
            sy.sem_inc(p.sem("p11ck"), 1)
    p.semv["p11w"] = bW + 16
    p.semv["p11c"] = bC + 16 * NCH
    p.semv["p11pr0"] = bPR[0] + 16 * 2 * NCH
    p.semv["p11pr1"] = bPR[1] + 16 * 2 * NCH
    p.semv["p11st0"] = bST[0] + 16 * 2 * NCH
    p.semv["p11st1"] = bST[1] + 16 * 2 * NCH
    p.semv["p11ck"] = bCK + NCH

    @block.tensor
    def _(t_):
        t_.wait_ge(p.sem("p11w"), bW + 16)
        for c in range(NCH):
            t_.wait_ge(p.sem("p11c"), bC + 16 * (c + 1))
            for t in range(4):
                for half in range(2):
                    j = (c * 4 + t) * 2 + half
                    psl = j % 2
                    if j >= 2:
                        t_.wait_ge(p.sem("p11e"), bE + j - 1)
                    for kt in range(32):
                        ins = t_.matmul(ps[:, psl * 512:(psl + 1) * 512],
                                        cc[:, kt * 512 + t * 128:kt * 512 + (t + 1) * 128],
                                        w3_sb[:, kt * 1024 + half * 512:kt * 1024 + (half + 1) * 512],
                                        start=(kt == 0), stop=(kt == 31))
                    ins.then_inc(p.sem("p11m"), 1)
    p.semv["p11m"] = bM + NCH * 8

    @block.vector
    def _(v):
        for c in range(NCH):
            for t in range(4):
                tn = c * 4 + t
                sl = t % 2
                for half in range(2):
                    j = tn * 2 + half
                    psl = j % 2
                    v.wait_ge(p.sem("p11m"), bM + j + 1)
                    v.wait_ge(p.sem("p11pr%d" % sl), bPR[sl] + 16 * (tn // 2 + 1))
                    if tn >= 2:
                        v.wait_ge(p.sem("p11st%d" % sl), bST[sl] + 16 * (tn // 2))
                    v.tensor_add(ob[:, sl * 1024 + half * 512:sl * 1024 + (half + 1) * 512],
                                 ps[:, psl * 512:(psl + 1) * 512],
                                 pr[:, sl * 1024 + half * 512:sl * 1024 + (half + 1) * 512]
                                 ).then_inc(p.sem("p11e"), 1)
    p.semv["p11e"] = bE + NCH * 8
    st.close()
    barrier(p, block)


# version 12 (final)
# speedup vs baseline: 1.0436x; 1.0436x over previous
import contextlib
import numpy as np
import ml_dtypes
import concourse.bass as bass
import concourse.mybir as mybir
from concourse.bass_utils import run_bass_kernel_spmd

F32 = mybir.dt.float32
BF16 = mybir.dt.bfloat16
AF = mybir.ActivationFunctionType
ALU = mybir.AluOpType
AX = mybir.AxisListType

S = 4096
D = 1024
DI = 2048
DS = 16
DTR = 64
H = 16
DH = 64
DHID = 4096
NE = 8
CH = 512
NCH = S // CH
NEG = -30000.0
EPS = 1e-5

FUSED = True

INPUT_SHAPES = {
    "x": ([S, D], F32),
    "W_in": ([D, 2 * DI], F32), "W_conv": ([DI, 4], F32), "b_conv": ([DI], F32),
    "W_x": ([DI, 96], F32), "W_dt": ([DTR, DI], F32), "b_dt": ([DI], F32),
    "A_log": ([DI, DS], F32), "Dskip": ([DI], F32), "W_out": ([DI, D], F32),
    "gamma0": ([D], F32), "beta0": ([D], F32),
    "Wg": ([D, NE], F32), "bg": ([NE], F32),
    "W1": ([NE, D, 2048], F32), "b1": ([NE, 2048], F32),
    "W2": ([NE, 2048, D], F32), "b2": ([NE, D], F32),
    "gammaF": ([D], F32), "betaF": ([D], F32),
    "Wq": ([D, D], F32), "Wk": ([D, D], F32), "Wv": ([D, D], F32), "Wo": ([D, D], F32),
    "w1_w": ([DHID, D], F32), "w1_s": ([DHID], F32),
    "w2_w": ([DHID, D], F32), "w2_s": ([DHID], F32),
    "h1_w": ([DHID, DHID], F32), "h1_s": ([DHID], F32),
    "h2_w": ([DHID, DHID], F32), "h2_s": ([DHID], F32),
    "w3_w": ([D, DHID], F32), "w3_s": ([D], F32),
    "eye": ([128, 128], F32),
    "mask_cur": ([128, 128], F32),
    "mask_prev": ([128, 128], F32),
    "sel16": ([16, 16 * 128], F32),
}

SCRATCH_SHAPES = {
    "xn_f": ([S, D], F32),
    "xnT_b": ([D, S], BF16),
    "gates_f": ([S, NE], F32),
    "gatesT_b": ([NE, S], BF16),
    "xiT_b": ([DI, 3 + S], BF16),
    "zT_b": ([DI, S], BF16),
    "uT_b": ([DI, S], BF16),
    "dtT_b": ([DI, S], BF16),
    "BmT_b": ([DS, S], BF16),
    "CmT_b": ([DS, S], BF16),
    "yW_f": ([S, D], F32),
    "moe_f": ([S, D], F32),
    "x1_f": ([S, D], F32),
    "xn1T_b": ([D, S], BF16),
    "qT_b": ([D, S], BF16),
    "kT_b": ([D, S], BF16),
    "v_b": ([S, D], BF16),
    "partial_f": ([S, D], F32),
    "aT_b": ([DHID, S], BF16),
    "bT_b": ([DHID, S], BF16),
    "cT_b": ([DHID, S], BF16),
    "Winb": ([D, 2 * DI], BF16),
    "Wxb": ([DI, 96], BF16),
    "Wdtb": ([DTR, DI], BF16),
    "Woutb": ([DI, D], BF16),
    "W1b": ([NE, D, 2048], BF16),
    "W2b": ([NE, 2048, D], BF16),
    "b2b": ([NE, D], BF16),
    "WqT": ([D, D], BF16), "WkT": ([D, D], BF16), "WvT": ([D, D], BF16), "WoT": ([D, D], BF16),
    "w1T": ([D, DHID], BF16), "w2T": ([D, DHID], BF16),
    "h1T": ([DHID, DHID], BF16), "h2T": ([DHID, DHID], BF16),
    "w3Ts": ([DHID, D], BF16),
}

OUTPUT_SHAPES = {"out": ([S, D], F32)}


class Prog:

    def __init__(self, fused):
        self.nc = bass.Bass("TRN2", target_bir_lowering=False, debug=False)
        self.stack = contextlib.ExitStack()
        self.fused = fused
        self.dram = {}
        self.reads = set()
        self.writes = set()
        self.sems = {}
        self.semv = {}
        self.pb_count = 0

    def T(self, name, write=False):
        if name in self.dram:
            ap = self.dram[name]
        else:
            if name in INPUT_SHAPES:
                shape, dt = INPUT_SHAPES[name]
                kind = "ExternalInput"
            elif name in OUTPUT_SHAPES:
                shape, dt = OUTPUT_SHAPES[name]
                kind = "ExternalOutput"
            else:
                shape, dt = SCRATCH_SHAPES[name]
                if self.fused:
                    kind = "ExternalOutput" if getattr(self, "debug_scratch", False) else "Internal"
                else:
                    kind = "ExternalOutput" if write else "ExternalInput"
            ap = self.nc.dram_tensor(name, shape, dt, kind=kind).ap()
            self.dram[name] = ap
        (self.writes if write else self.reads).add(name)
        return ap

    POOL_N = 70
    PINNED = {"pb", "gld", "wpg0", "wpg1", "wtg0", "wtg1", "p2sA", "p2xz0", "p2xz1",
              "p3st", "p4st", "p8st0", "p8st1", "p9st0", "p9st1", "p10st0", "p10st1"}

    def _pool(self):
        if not hasattr(self, "_pool_h"):
            self._pool_h = [self.stack.enter_context(self.nc.semaphore("pool%d" % i))
                            for i in range(self.POOL_N)]
            self._pool_val = [0] * self.POOL_N
            self._pool_next = 0
            self._active = {}
            self._pinned = {}
        return self._pool_h

    def sem(self, name):
        self._pool()
        if name in self.PINNED:
            if name not in self._pinned:
                self._pinned[name] = self.stack.enter_context(self.nc.semaphore(name))
                self.semv[name] = 0
            return self._pinned[name]
        if name not in self._active:
            if len(self._active) >= self.POOL_N:
                raise RuntimeError("semaphore pool exhausted within one pass")
            idx = self._pool_next
            self._pool_next = (self._pool_next + 1) % self.POOL_N
            tries = 0
            while idx in self._active.values():
                idx = (idx + 1) % self.POOL_N
                tries += 1
                assert tries <= self.POOL_N
            self._active[name] = idx
            self.semv[name] = self._pool_val[idx]
        return self._pool_h[self._active[name]]

    def rotate_sems(self):
        if not hasattr(self, "_pool_h"):
            return
        for name, idx in self._active.items():
            self._pool_val[idx] = self.semv[name]
        self._active = {}

    def inc(self, instr, name, n=1):
        instr.then_inc(self.sem(name), n)
        self.semv[name] += n
        return instr

    def bump(self, name, n):
        self.sem(name)
        self.semv[name] += n

    def val(self, name):
        self.sem(name)
        return self.semv[name]

    def wait(self, eng, name, val=None):
        eng.wait_ge(self.sem(name), self.semv[name] if val is None else val)

    def sb(self, name, shape, dt=F32, stack=None):
        return (stack or self.stack).enter_context(self.nc.sbuf_tensor(name, shape, dt))

    def ps(self, name, shape, dt=F32, stack=None):
        return (stack or self.stack).enter_context(self.nc.psum_tensor(name, shape, dt))


ENGS = ["sync", "vector", "scalar", "tensor", "gpsimd"]


def barrier(p, block):
    p.rotate_sems()
    p.pb_count += 1
    tgt = 5 * p.pb_count
    pb = p.sem("pb")
    for ename in ENGS:
        dec = getattr(block, ename)

        def _f(eng, ename=ename):
            if ename in ("vector", "scalar", "tensor", "gpsimd"):
                eng.drain()
            eng.sem_inc(pb, 1)
            eng.wait_ge(pb, tgt)
        dec(_f)
    p.semv["pb"] = tgt


def rr(ap, pat, **kw):
    return ap.rearrange(pat, **kw)


def pt(ap, r0, nr, c0, nc_):
    R = ap.shape[0]
    v = rr(ap, "(rt p) c -> p rt c", p=128)
    return v[:, r0 // 128:r0 // 128 + nr, c0:c0 + nc_]


def emit_prep(p, block):
    nc = p.nc
    c = {}
    c["eyeb"] = p.sb("c_eyeb", [128, 128], BF16)
    c["g0bc"] = p.sb("c_g0bc", [128, D], F32)
    c["b0bc"] = p.sb("c_b0bc", [128, D], F32)
    c["gFbc"] = p.sb("c_gFbc", [128, D], F32)
    c["bFbc"] = p.sb("c_bFbc", [128, D], F32)
    c["bgbc"] = p.sb("c_bgbc", [128, NE], F32)

    c["Wg"] = p.sb("c_Wg", [128, 8 * NE], BF16)
    c["WgF"] = p.sb("c_WgF", [128, 8 * NE], F32)
    c["b1"] = p.sb("c_b1", [128, NE * 16], F32)
    c["bconv"] = p.sb("c_bconv", [128, 16], F32)
    c["bdt"] = p.sb("c_bdt", [128, 16], F32)
    c["Dskip"] = p.sb("c_Dskip", [128, 16], F32)
    c["Wc"] = p.sb("c_Wc", [128, 16 * 4], F32)
    c["A"] = p.sb("c_A", [128, 256], F32)
    c["w1s"] = p.sb("c_w1s", [128, 32], F32)
    c["w2s"] = p.sb("c_w2s", [128, 32], F32)
    c["h1s"] = p.sb("c_h1s", [128, 32], F32)
    c["h2s"] = p.sb("c_h2s", [128, 32], F32)
    c["hstate"] = p.sb("c_hstate", [128, 256], F32)

    c["eyef"] = p.sb("c_eyef", [128, 128], F32)
    cpool = p.sb("c_cpool", [128, 8], F32)
    CONSTS = [EPS, -0.5, -1.0, 1.0, 0.5, 0.125, 2.0, 0.0]
    for j, cv in enumerate(CONSTS):
        p.nc.const_aps.register(F32, cv, cpool[:, j:j + 1])
    prep_stack = contextlib.ExitStack()
    c["ones1"] = p.sb("c_ones1", [1, 128], F32, prep_stack)
    c["rows"] = p.sb("c_rows", [1, 4 * D + NE], F32, prep_stack)
    vstage = p.sb("c_vstage", [128, 128], F32, prep_stack)
    ps_v = p.ps("c_psv", [128, 128], F32, prep_stack)
    vecjobs = [("b1", c["b1"], 128), ("b_conv", c["bconv"], 16),
               ("b_dt", c["bdt"], 16), ("Dskip", c["Dskip"], 16),
               ("w1_s", c["w1s"], 32), ("w2_s", c["w2s"], 32),
               ("h1_s", c["h1s"], 32), ("h2_s", c["h2s"], 32)]

    @block.sync
    def _(sync):
        sync.dma_start(out=c["rows"][:, 0:D], in_=rr(p.T("gamma0"), "(o d) -> o d", o=1)).then_inc(p.sem("ld"), 16)
        sync.dma_start(out=c["rows"][:, D:2 * D], in_=rr(p.T("beta0"), "(o d) -> o d", o=1)).then_inc(p.sem("ld"), 16)
        sync.dma_start(out=c["rows"][:, 2 * D:3 * D], in_=rr(p.T("gammaF"), "(o d) -> o d", o=1)).then_inc(p.sem("ld"), 16)
        sync.dma_start(out=c["rows"][:, 3 * D:4 * D], in_=rr(p.T("betaF"), "(o d) -> o d", o=1)).then_inc(p.sem("ld"), 16)
        sync.dma_start(out=c["rows"][:, 4 * D:], in_=rr(p.T("bg"), "(o d) -> o d", o=1)).then_inc(p.sem("ld"), 16)
        sync.dma_start(out=rr(c["Wc"][:], "p (t k) -> p t k", k=4),
                       in_=rr(p.T("W_conv"), "(t pp) k -> pp t k", pp=128)).then_inc(p.sem("ld"), 16)
        sync.dma_start(out=rr(c["A"][:], "p (t n) -> p t n", n=DS),
                       in_=rr(p.T("A_log"), "(t pp) n -> pp t n", pp=128)).then_inc(p.sem("ld"), 16)

        sync.dma_start(out=c["eyef"][:], in_=p.T("eye")).then_inc(p.sem("ld"), 16)
        sync.dma_start(out=rr(c["WgF"][:], "p (t e) -> p t e", e=NE),
                       in_=rr(p.T("Wg"), "(t pp) e -> pp t e", pp=128)).then_inc(p.sem("ld"), 16)

        for j, (nm, dst, k) in enumerate(vecjobs):
            src = p.T(nm)
            if len(src.shape) > 1:
                src = rr(src, "a b -> (a b)")
            if j > 0:
                sync.wait_ge(p.sem("vtr"), j)
            sync.dma_start(out=vstage[0:k, :], in_=rr(src, "(a pp) -> a pp", pp=128)
                           ).then_inc(p.sem("ldv"), 16)
    p.semv["ld"] += 16 * 9
    p.semv["ldv"] += 16 * 8

    @block.gpsimd
    def _(g):
        g.dma_start(out=c["eyeb"][:], in_=p.T("eye")).then_inc(p.sem("gld"), 16)
        g.dma_start(out=rr(c["Wg"][:], "p (t e) -> p t e", e=NE),
                    in_=rr(p.T("Wg"), "(t pp) e -> pp t e", pp=128)).then_inc(p.sem("gld"), 16)
        g.memset(c["hstate"][:], 0.0).then_inc(p.sem("gv"), 1)
    p.semv["gld"] += 32
    p.semv["gv"] += 1

    @block.tensor
    def _(t):
        t.wait_ge(p.sem("ld"), p.val("ld"))
        for j, (nm, dst, k) in enumerate(vecjobs):
            t.wait_ge(p.sem("ldv"), 16 * (j + 1))
            if j >= 1:
                t.wait_ge(p.sem("vtv"), j)
            t.transpose(ps_v[0:128, 0:k], vstage[0:k, :], c["eyef"][0:k, 0:k]
                        ).then_inc(p.sem("vtt"), 1)

    @block.vector
    def _(v):
        for j, cv in enumerate(CONSTS):
            v.memset(cpool[:, j:j + 1], cv)
        v.memset(c["ones1"][:], 1.0).then_inc(p.sem("vv"), 1)
        p.semv["vv"] += 1
        for j, (nm, dst, k) in enumerate(vecjobs):
            v.wait_ge(p.sem("vtt"), j + 1)
            v.tensor_copy(dst[:, 0:k], ps_v[:, 0:k]).then_inc(p.sem("vtv"), 1)
            v.sem_inc(p.sem("vtr"), 1)
    p.semv["vtt"] = 8
    p.semv["vtv"] = 8
    p.semv["vtr"] = 8

    ps_bc = p.ps("pp_bc", [128, 1024], F32, prep_stack)
    jobs = []
    for i, (nm, off) in enumerate([("g0bc", 0), ("b0bc", D), ("gFbc", 2 * D), ("bFbc", 3 * D)]):
        for hh in range(2):
            jobs.append((ps_bc[:, (2 * i + hh) % 2 * 512:(2 * i + hh) % 2 * 512 + 512],
                         c[nm][:, hh * 512:(hh + 1) * 512],
                         c["rows"][:, off + hh * 512: off + (hh + 1) * 512]))
    jobs.append((ps_bc[:, 0:NE], c["bgbc"][:], c["rows"][:, 4 * D:4 * D + NE]))

    @block.tensor
    def _(t):
        p.wait(t, "ld")
        p.wait(t, "vv")
        for j, (pshalf, dst, src) in enumerate(jobs):
            if j >= 2:
                p.wait(t, "bcv", j - 1)
            p.inc(t.matmul(pshalf, c["ones1"][:], src, start=True, stop=True), "bct", 1)

    @block.vector
    def _(v):
        for j, (pshalf, dst, src) in enumerate(jobs):
            v.wait_ge(p.sem("bct"), j + 1)
            p.inc(v.tensor_copy(dst, pshalf), "bcv", 1)

    @block.scalar
    def _(s):
        p.wait(s, "ld")
        p.wait(s, "vv")
        p.inc(s.activation(c["A"][:], c["A"][:], AF.Exp), "sv", 1)

    @block.vector
    def _(v):
        p.wait(v, "sv")
        p.inc(v.tensor_scalar_mul(c["A"][:], c["A"][:], -1.0), "vv", 1)

    @block.gpsimd
    def _(g):
        p.wait(g, "gld")
        p.wait(g, "gv")

    @block.sync
    def _(sy):
        p.wait(sy, "ld")
        p.wait(sy, "ldv")

    prep_stack.close()
    barrier(p, block)
    return c


def _flat2d(ap):
    return rr(ap, "e r c -> (e r) c") if len(ap.shape) == 3 else ap


def emit_wprep_cast(p, block):
    st = contextlib.ExitStack()
    casts = [("W_in", "Winb", D, 2 * DI), ("W_x", "Wxb", DI, 96),
             ("W_dt", "Wdtb", DTR, DI), ("W_out", "Woutb", DI, D),
             ("W1", "W1b", NE * D, 2048), ("W2", "W2b", NE * 2048, D),
             ("b2", "b2b", NE, D)]
    cbuf = p.sb("wp_cast", [128, 2 * 4096], BF16, st)
    tiles = []
    for src, dst, R, C in casts:
        nb = max(1, R // 128)
        pr = min(128, R)
        for a in range(nb):
            tiles.append((src, dst, a, pr, C))
    g0 = [p.val("wpg0"), p.val("wpg1")]
    s0 = [p.val("wps0"), p.val("wps1")]
    n_slot = [(len(tiles) + 1) // 2, len(tiles) // 2]

    @block.gpsimd
    def _(g):
        for j, (src, dst, a, pr, C) in enumerate(tiles):
            s_ap = _flat2d(p.T(src))
            sl = j % 2
            if j >= 2:
                g.wait_ge(p.sem("wps%d" % sl), s0[sl] + 16 * (j // 2))
            src_t = pt(s_ap, a * 128, 1, 0, C)[:, 0, :] if pr == 128 else s_ap
            g.dma_start(out=cbuf[0:pr, sl * 4096:sl * 4096 + C],
                        in_=src_t).then_inc(p.sem("wpg%d" % sl), 16)
    p.semv["wpg0"] = g0[0] + 16 * n_slot[0]
    p.semv["wpg1"] = g0[1] + 16 * n_slot[1]

    @block.sync
    def _(sy):
        for j, (src, dst, a, pr, C) in enumerate(tiles):
            d_ap = _flat2d(p.T(dst, write=True))
            sl = j % 2
            sy.wait_ge(p.sem("wpg%d" % sl), g0[sl] + 16 * (j // 2 + 1))
            dst_t = pt(d_ap, a * 128, 1, 0, C)[:, 0, :] if pr == 128 else d_ap
            sy.dma_start(out=dst_t, in_=cbuf[0:pr, sl * 4096:sl * 4096 + C]
                         ).then_inc(p.sem("wps%d" % sl), 16)
        sy.wait_ge(p.sem("wps0"), s0[0] + 16 * n_slot[0])
        sy.wait_ge(p.sem("wps1"), s0[1] + 16 * n_slot[1])
    p.semv["wps0"] = s0[0] + 16 * n_slot[0]
    p.semv["wps1"] = s0[1] + 16 * n_slot[1]
    st.close()
    barrier(p, block)


def emit_wprep_transpose(p, block, ctx):
    st = contextlib.ExitStack()
    jobs = [("Wq", "WqT", D, D, False), ("Wk", "WkT", D, D, False),
            ("Wv", "WvT", D, D, False), ("Wo", "WoT", D, D, False),
            ("w1_w", "w1T", DHID, D, False), ("w2_w", "w2T", DHID, D, False),
            ("h1_w", "h1T", DHID, DHID, False), ("h2_w", "h2T", DHID, DHID, False),
            ("w3_w", "w3Ts", D, DHID, True)]
    ibuf = p.sb("wp_tin", [128, 2 * 4096], BF16, st)
    obuf = p.sb("wp_tout", [128, 2 * 4096], BF16, st)
    w3s = p.sb("wp_w3s", [128, 8], F32, st)
    pst = p.ps("wp_pst", [128, 2048], BF16, st)

    blocks_l = []
    for src, dst, R, C, scl in jobs:
        for a in range(R // 128):
            blocks_l.append((src, dst, a, C, scl))
    ngrp_tot = sum(C // 1024 for _, _, _, C, _ in blocks_l)

    g0 = [p.val("wtg0"), p.val("wtg1")]; v0 = p.val("wtv"); t0 = p.val("wtt")
    e0 = p.val("wte"); s0 = [p.val("wts0"), p.val("wts1")]; sc0 = p.val("wtc")

    w3stage = p.sb("wp_w3stage", [8, 128], F32, st)
    ps_w3 = p.ps("wp_psw3", [128, 8], F32, st)

    @block.sync
    def _(sy):
        sy.dma_start(out=w3stage[:], in_=rr(p.T("w3_s"), "(a pp) -> a pp", pp=128)
                     ).then_inc(p.sem("wtc"), 16)

    @block.tensor
    def _(t):
        t.wait_ge(p.sem("wtc"), sc0 + 16)
        t.transpose(ps_w3[:], w3stage[:], ctx["eyef"][0:8, 0:8]).then_inc(p.sem("wtc"), 1)

    @block.vector
    def _(v):
        v.wait_ge(p.sem("wtc"), sc0 + 17)
        v.tensor_copy(w3s[:], ps_w3[:]).then_inc(p.sem("wtc"), 1)
    p.semv["wtc"] = sc0 + 18

    @block.gpsimd
    def _(g):
        for j, (src, dst, a, C, scl) in enumerate(blocks_l):
            sl = j % 2
            if j >= 2:
                g.wait_ge(p.sem("wtt"), t0 + _grp_end(blocks_l, j - 2))
            g.dma_start(out=ibuf[:, sl * 4096:sl * 4096 + C],
                        in_=pt(p.T(src), a * 128, 1, 0, C)[:, 0, :]).then_inc(p.sem("wtg%d" % sl), 16)
    p.semv["wtg0"] = g0[0] + 16 * ((len(blocks_l) + 1) // 2)
    p.semv["wtg1"] = g0[1] + 16 * (len(blocks_l) // 2)

    @block.vector
    def _(v):
        for j, (src, dst, a, C, scl) in enumerate(blocks_l):
            if not scl:
                continue
            v.wait_ge(p.sem("wtg%d" % (j % 2)), g0[j % 2] + 16 * (j // 2 + 1))
            v.wait_ge(p.sem("wtc"), sc0 + 18)
            v.tensor_scalar_mul(ibuf[:, (j % 2) * 4096:(j % 2) * 4096 + C],
                                ibuf[:, (j % 2) * 4096:(j % 2) * 4096 + C],
                                w3s[:, a:a + 1]).then_inc(p.sem("wtv"), 1)
    nscl = sum(1 for b in blocks_l if b[4])
    p.semv["wtv"] = v0 + nscl

    grp = 0

    @block.tensor
    def _(t):
        nonlocal grp
        nv = 0
        t.wait_ge(p.sem("gld"), p.val("gld"))
        for j, (src, dst, a, C, scl) in enumerate(blocks_l):
            t.wait_ge(p.sem("wtg%d" % (j % 2)), g0[j % 2] + 16 * (j // 2 + 1))
            if scl:
                nv += 1
                t.wait_ge(p.sem("wtv"), v0 + nv)
            for gg in range(C // 1024):
                if grp >= 2:
                    t.wait_ge(p.sem("wte"), e0 + grp - 1)
                for b in range(8):
                    cc = gg * 1024 + b * 128
                    ins = t.transpose(pst[:, (grp % 2) * 1024 + b * 128:(grp % 2) * 1024 + (b + 1) * 128],
                                      ibuf[:, (j % 2) * 4096 + cc:(j % 2) * 4096 + cc + 128],
                                      ctx["eyeb"][:])
                    if b == 7:
                        ins.then_inc(p.sem("wtt"), 1)
                grp += 1
    p.semv["wtt"] = t0 + ngrp_tot

    @block.scalar
    def _(s):
        grp2 = 0
        for j, (src, dst, a, C, scl) in enumerate(blocks_l):
            for gg in range(C // 1024):
                s.wait_ge(p.sem("wtt"), t0 + grp2 + 1)
                if gg == 0 and j >= 2:
                    s.wait_ge(p.sem("wts%d" % (j % 2)), s0[j % 2] + 16 * (j // 2))
                s.copy(obuf[:, (j % 2) * 4096 + gg * 1024:(j % 2) * 4096 + (gg + 1) * 1024],
                       pst[:, (grp2 % 2) * 1024:(grp2 % 2 + 1) * 1024]).then_inc(p.sem("wte"), 1)
                grp2 += 1
    p.semv["wte"] = e0 + ngrp_tot

    @block.sync
    def _(sy):
        egrp = 0
        for j, (src, dst, a, C, scl) in enumerate(blocks_l):
            sl = j % 2
            egrp += C // 1024
            sy.wait_ge(p.sem("wte"), e0 + egrp)
            d = p.T(dst, write=True)
            sy.dma_start(out=rr(d, "(it pp) c -> pp it c", pp=128)[:, :, a * 128:(a + 1) * 128],
                         in_=rr(obuf[:, sl * 4096:sl * 4096 + C], "pp (it c) -> pp it c", c=128)
                         ).then_inc(p.sem("wts%d" % sl), 16)
        sy.wait_ge(p.sem("wts0"), s0[0] + 16 * ((len(blocks_l) + 1) // 2))
        sy.wait_ge(p.sem("wts1"), s0[1] + 16 * (len(blocks_l) // 2))
    p.semv["wts0"] = s0[0] + 16 * ((len(blocks_l) + 1) // 2)
    p.semv["wts1"] = s0[1] + 16 * (len(blocks_l) // 2)
    st.close()
    barrier(p, block)


def _grp_end(blocks_l, upto_j):
    tot = 0
    for j, (_, _, _, C, _) in enumerate(blocks_l):
        if j > upto_j:
            break
        tot += C // 1024
    return tot


def _reg():
    return {
        "wcast": (lambda p, b, c: emit_wprep_cast(p, b),
                  ["W_in", "W_x", "W_dt", "W_out", "W1", "W2", "b2"],
                  ["Winb", "Wxb", "Wdtb", "Woutb", "W1b", "W2b", "b2b"]),
        "wtrans": (emit_wprep_transpose_,
                   ["Wq", "Wk", "Wv", "Wo", "w1_w", "w2_w", "h1_w", "h2_w", "w3_w", "w3_s"],
                   ["WqT", "WkT", "WvT", "WoT", "w1T", "w2T", "h1T", "h2T", "w3Ts"]),
        "p2": (emit_p2, ["x", "Winb"],
               ["xn_f", "xnT_b", "xiT_b", "zT_b", "gates_f", "gatesT_b"]),
        "p3": (emit_p3, ["xiT_b"], ["uT_b"]),
        "p4": (emit_p4, ["uT_b", "Wxb", "Wdtb"], ["dtT_b", "BmT_b", "CmT_b"]),
        "p5": (emit_p5, ["dtT_b", "uT_b", "BmT_b", "CmT_b", "zT_b", "Woutb"], ["yW_f"]),
        "p6": (emit_p6, ["x", "xn_f", "xnT_b", "gates_f", "gatesT_b", "yW_f", "W1b", "W2b", "b2b"],
               ["x1_f", "xn1T_b"]),
        "p56": (emit_p56, ["dtT_b", "uT_b", "BmT_b", "CmT_b", "zT_b", "Woutb",
                           "xnT_b", "gates_f", "gatesT_b", "W1b", "W2b", "b2b"],
                ["yW_f", "moe_f"]),
        "p6b": (emit_p6b, ["x", "xn_f", "moe_f", "yW_f"], ["x1_f", "xn1T_b"]),
        "p7": (emit_p7, ["xn1T_b", "x1_f", "WqT", "WkT", "WvT", "WoT"], ["partial_f"]),
        "p8": (emit_p8, ["xn1T_b", "w1T", "w2T"], ["aT_b"]),
        "p9": (emit_p9, ["aT_b", "h1T"], ["bT_b"]),
        "p10": (emit_p10, ["bT_b", "h2T"], ["cT_b"]),
        "p11": (emit_p11, ["cT_b", "w3Ts", "partial_f"], ["out"]),
    }


def emit_wprep_transpose_(p, block, ctx):
    emit_wprep_transpose(p, block, ctx)


PREP_READS = ["gamma0", "beta0", "gammaF", "betaF", "bg", "b1", "b_conv", "b_dt",
              "Dskip", "W_conv", "A_log", "mask_prev", "mask_cur", "w1_s", "w2_s",
              "h1_s", "h2_s", "eye", "Wg", "sel16"]


def _build(names, fused):
    p = Prog(fused)
    reg = _reg()
    with p.stack:
        wr = set()
        for nm in names:
            wr |= set(reg[nm][2])
        for nm in names:
            for t in reg[nm][1]:
                if t not in wr:
                    p.T(t)
        for t in PREP_READS:
            p.T(t)
        for nm in names:
            for t in reg[nm][2]:
                p.T(t, write=True)
        with p.nc.Block() as block:
            ctx = emit_prep(p, block)
            for nm in names:
                reg[nm][0](p, block, ctx)
    return p


_NAME_MAP = {
    "W_in": "L0_attn_W_in", "W_conv": "L0_attn_W_conv", "b_conv": "L0_attn_b_conv",
    "W_x": "L0_attn_W_x", "W_dt": "L0_attn_W_dt", "b_dt": "L0_attn_b_dt",
    "A_log": "L0_attn_A_log", "Dskip": "L0_attn_Dskip", "W_out": "L0_attn_W_out",
    "gamma0": "L0_attn_gamma", "beta0": "L0_attn_beta",
    "Wg": "L0_ffn_Wg", "bg": "L0_ffn_bg", "W1": "L0_ffn_W1", "b1": "L0_ffn_b1",
    "W2": "L0_ffn_W2", "b2": "L0_ffn_b2", "gammaF": "L0_ffn_gamma", "betaF": "L0_ffn_beta",
    "Wq": "L1_attn_Wq", "Wk": "L1_attn_Wk", "Wv": "L1_attn_Wv", "Wo": "L1_attn_Wo",
    "w1_w": "L1_ffn_w1_w", "w1_s": "L1_ffn_w1_s", "w2_w": "L1_ffn_w2_w", "w2_s": "L1_ffn_w2_s",
    "h1_w": "L1_ffn_h1_w", "h1_s": "L1_ffn_h1_s", "h2_w": "L1_ffn_h2_w", "h2_s": "L1_ffn_h2_s",
    "w3_w": "L1_ffn_w3_w", "w3_s": "L1_ffn_w3_s",
}


def _host_consts():
    o = np.arange(128)[:, None]
    q = np.arange(128)[None, :]
    sel = np.zeros((16, 16 * 128), dtype=np.float32)
    for n in range(16):
        sel[n, n * 128:(n + 1) * 128] = 1.0
    return {
        "eye": np.eye(128, dtype=np.float32),
        "mask_cur": np.where(q <= o, 0.0, NEG).astype(np.float32),
        "mask_prev": np.where(q > o, 0.0, NEG).astype(np.float32),
        "sel16": sel,
    }


def _stage_cores(inputs):
    consts = _host_consts()
    x = np.ascontiguousarray(inputs["L0_attn_x"])
    cores = []
    for b in range(8):
        d = {"x": x[b]}
        for k, v in _NAME_MAP.items():
            d[k] = np.ascontiguousarray(inputs[v])
        d.update(consts)
        cores.append(d)
    return cores


GROUPS = [["wcast"], ["wtrans"], ["p2"], ["p3"], ["p4"], ["p56"],
          ["p6b"], ["p7"], ["p8"], ["p9"], ["p10"], ["p11"]]


def kernel(**inputs):
    hb = _stage_cores(inputs)
    if FUSED:
        p = _build([g for grp in GROUPS for g in grp], True)
        need = sorted(p.reads - set(SCRATCH_SHAPES))
        in_maps = [{n: hb[c][n] for n in need} for c in range(8)]
        res = run_bass_kernel_spmd(p.nc, in_maps, core_ids=list(range(8)))
        return np.stack([np.asarray(r["out"], dtype=np.float32) for r in res.results])
    for grp in GROUPS:
        p = _build(grp, False)
        need = sorted(p.reads - p.writes)
        in_maps = [{n: hb[c][n] for n in need} for c in range(8)]
        res = run_bass_kernel_spmd(p.nc, in_maps, core_ids=list(range(8)))
        for c in range(8):
            hb[c].update(res.results[c])
    return np.stack([np.asarray(hb[c]["out"], dtype=np.float32) for c in range(8)])


def emit_p2(p, block, ctx):
    st = contextlib.ExitStack()
    xt = p.sb("p2_xt", [128, 4096], F32, st)
    st6 = p.sb("p2_st6", [128, 4 * 12], F32, st)
    mv = p.sb("p2_mv", [128, 8], F32, st)
    rstd = p.sb("p2_rstd", [128, 4], F32, st)
    st6b = p.sb("p2_st6b", [128, 4 * 12], F32, st)
    mvb = p.sb("p2_mvb", [128, 8], F32, st)
    rstdb = p.sb("p2_rstdb", [128, 4], F32, st)
    xn_sb = p.sb("p2_xn", [128, 4096], F32, st)
    xnbf = p.sb("p2_xnbf", [128, 4096], BF16, st)
    tmp4 = p.sb("p2_tmp4", [128, 4096], F32, st)
    h1bf = p.sb("p2_h1bf", [128, 4096], BF16, st)
    xnT = p.sb("p2_xnT", [128, 8 * 512], BF16, st)
    h1T = p.sb("p2_h1T", [128, 8 * 512], BF16, st)
    Winb_sb = p.sb("p2_Win", [128, 8 * 4096], BF16, st)
    xz2 = p.sb("p2_xz2", [128, 2 * 512], BF16, st)
    l4 = p.sb("p2_l4", [128, 32], F32, st)
    eq1 = p.sb("p2_eq1", [128, 32], F32, st)
    msk4 = p.sb("p2_msk", [128, 32], F32, st)
    m1_4 = p.sb("p2_m1", [128, 4], F32, st)
    m2_4 = p.sb("p2_m2", [128, 4], F32, st)
    d4 = p.sb("p2_d4", [128, 4], F32, st)
    w2g = p.sb("p2_w2g", [128, 4], F32, st)
    w1g = p.sb("p2_w1g", [128, 4], F32, st)
    gt4 = p.sb("p2_gt4", [128, 32], F32, st)
    gates_sb = p.sb("p2_gates", [128, 32], F32, st)
    gbf = p.sb("p2_gbf", [128, 32], BF16, st)
    gT = p.sb("p2_gT", [8, 512], BF16, st)
    z3 = p.sb("p2_z3", [128, 48], BF16, st)
    ps_tr = p.ps("p2_pstr", [128, 2 * 1024], BF16, st)
    ps_mm = p.ps("p2_psmm", [128, 2 * 512], F32, st)
    ps_g = p.ps("p2_psg", [128, 32], F32, st)
    ps_gt = p.ps("p2_psgt", [8, 512], BF16, st)
    xnTf = p.sb("p2_xnTf", [128, 1024], F32, st)
    ps_trf = p.ps("p2_pstrf", [128, 1024], F32, st)

    bV = p.val("p2v"); bS = p.val("p2s"); bG = p.val("p2g")
    bT = p.val("p2t"); bTE = p.val("p2te"); bM = p.val("p2m"); bE = p.val("p2e")
    bGM = p.val("p2gm"); bGT = p.val("p2gt"); bCK = p.val("p2ck"); bXF = p.val("p2xf")
    bTF = p.val("p2tf"); bTFE = p.val("p2tfe")
    bXZ = [p.val("p2xz0"), p.val("p2xz1")]
    bX0 = p.val("p2x0"); bSA = p.val("p2sA"); bW = p.val("p2w"); bSS = p.val("p2ss")
    NV, NS = 8, 3
    stores_per_chunk = [5 if cc == 0 else 4 for cc in range(NCH)]
    cum_st = np.cumsum([0] + stores_per_chunk)

    def V(c, k):
        return bV + c * NV + k

    def SS(c, k):
        return bS + c * NS + k

    @block.sync
    def _(sy):
        sy.dma_start(out=rr(Winb_sb[:], "p (kt n) -> p kt n", kt=8),
                     in_=pt(p.T("Winb"), 0, 8, 0, 4096)).then_inc(p.sem("p2w"), 16)
        for c in range(NCH):
            if c >= 1:
                sy.wait_ge(p.sem("p2xf"), bXF + c)
            sy.dma_start(out=rr(xt[:], "p (t d) -> p t d", t=4),
                         in_=pt(p.T("x"), c * 512, 4, 0, 1024)).then_inc(p.sem("p2x0"), 16)
    p.semv["p2w"] = bW + 16
    p.semv["p2x0"] = bX0 + 16 * NCH

    @block.vector
    def _(v):
        ssn = bSS
        for c in range(NCH):
            xc = xt[:]
            v.wait_ge(p.sem("p2x0"), bX0 + 16 * (c + 1))
            if c >= 1:
                v.wait_ge(p.sem("p2ck"), bCK + c)
            for t in range(4):
                v.bn_stats(st6[:, t * 12:t * 12 + 6], xc[:, t * 1024:t * 1024 + 512])
                ins = v.bn_stats(st6[:, t * 12 + 6:t * 12 + 12], xc[:, t * 1024 + 512:t * 1024 + 1024])
            ins.then_inc(p.sem("p2ss"), 1); ssn += 1
            v.wait_ge(p.sem("p2ss"), ssn)
            for t in range(4):
                ins = v.bn_aggr(mv[:, 2 * t:2 * t + 2], st6[:, t * 12:t * 12 + 12])
            ins.then_inc(p.sem("p2v"), 1)
            v.wait_ge(p.sem("p2s"), SS(c, 1))
            for t in range(4):
                ins = v.tensor_scalar(xn_sb[:, t * 1024:(t + 1) * 1024], xc[:, t * 1024:(t + 1) * 1024],
                                      mv[:, 2 * t:2 * t + 1], rstd[:, t:t + 1], ALU.subtract, ALU.mult)
            ins.then_inc(p.sem("p2xf"), 1)
            v.wait_ge(p.sem("p2xf"), bXF + c + 1)
            ins = v.tensor_copy(xnbf[:], xn_sb[:])
            ins.then_inc(p.sem("p2v"), 1)
            for t in range(4):
                v.bn_stats(st6b[:, t * 12:t * 12 + 6], xn_sb[:, t * 1024:t * 1024 + 512])
                ins = v.bn_stats(st6b[:, t * 12 + 6:t * 12 + 12], xn_sb[:, t * 1024 + 512:t * 1024 + 1024])
            ins.then_inc(p.sem("p2ss"), 1); ssn += 1
            v.wait_ge(p.sem("p2ss"), ssn)
            for t in range(4):
                ins = v.bn_aggr(mvb[:, 2 * t:2 * t + 2], st6b[:, t * 12:t * 12 + 12])
            ins.then_inc(p.sem("p2v"), 1)
            v.wait_ge(p.sem("p2v"), V(c, 3))
            for t in range(4):
                ins = v.tensor_single_scalar(tmp4[:, t * 1024:(t + 1) * 1024],
                                             xn_sb[:, t * 1024:(t + 1) * 1024],
                                             mvb[:, 2 * t:2 * t + 1], ALU.subtract)
            ins.then_inc(p.sem("p2v"), 1)
            v.wait_ge(p.sem("p2g"), bG + c + 1)
            v.wait_ge(p.sem("p2s"), SS(c, 2))
            for t in range(4):
                ins = v.scalar_tensor_tensor(h1bf[:, t * 1024:(t + 1) * 1024],
                                             tmp4[:, t * 1024:(t + 1) * 1024], rstdb[:, t:t + 1],
                                             ctx["b0bc"][:], ALU.mult, ALU.add)
            ins.then_inc(p.sem("p2v"), 1)
            for t in range(4):
                for which in range(2):
                    g = c * 8 + t * 2 + which
                    v.wait_ge(p.sem("p2t"), bT + g + 1)
                    dst = xnT if which == 0 else h1T
                    v.tensor_copy(
                        rr(dst[:], "p (dt s) -> p dt s", s=512)[:, :, t * 128:(t + 1) * 128],
                        rr(ps_tr[:, (g % 2) * 1024:(g % 2 + 1) * 1024], "p (dt s) -> p dt s", s=128)
                    ).then_inc(p.sem("p2te"), 1)
            for t in range(4):
                v.wait_ge(p.sem("p2tf"), bTF + c * 4 + t + 1)
                if c * 4 + t >= 1:
                    v.wait_ge(p.sem("p2gm"), bGM + c * 4 + t)
                v.tensor_copy(xnTf[:], ps_trf[:]).then_inc(p.sem("p2tfe"), 1)
            def vmark(i):
                p.inc(i, "p2vc", 1)

            def vjoin():
                v.wait_ge(p.sem("p2vc"), p.val("p2vc"))
            v.wait_ge(p.sem("p2gm"), bGM + c * 4 + 4)
            for t in range(4):
                ins = v.scalar_tensor_tensor(l4[:, 8 * t:8 * t + 8], ps_g[:, 8 * t:8 * t + 8], 1.0,
                                             ctx["bgbc"][:], ALU.mult, ALU.add)
            vmark(ins); vjoin()
            for t in range(4):
                ins = v.reduce_max(m1_4[:, t:t + 1], l4[:, 8 * t:8 * t + 8], axis=AX.X)
            vmark(ins); vjoin()
            for t in range(4):
                ins = v.tensor_single_scalar(eq1[:, 8 * t:8 * t + 8], l4[:, 8 * t:8 * t + 8],
                                             m1_4[:, t:t + 1], ALU.is_equal)
            vmark(ins); vjoin()
            for t in range(4):
                ins = v.scalar_tensor_tensor(msk4[:, 8 * t:8 * t + 8], eq1[:, 8 * t:8 * t + 8], -1e30,
                                             l4[:, 8 * t:8 * t + 8], ALU.mult, ALU.add)
            vmark(ins); vjoin()
            for t in range(4):
                ins = v.reduce_max(m2_4[:, t:t + 1], msk4[:, 8 * t:8 * t + 8], axis=AX.X)
            vmark(ins); vjoin()
            for t in range(4):
                ins = v.tensor_single_scalar(gt4[:, 8 * t:8 * t + 8], msk4[:, 8 * t:8 * t + 8],
                                             m2_4[:, t:t + 1], ALU.is_equal)
            vmark(ins)
            v.tensor_sub(d4[:], m2_4[:], m1_4[:]).then_inc(p.sem("p2v"), 1)
            v.wait_ge(p.sem("p2s"), SS(c, 3))
            ins = v.tensor_scalar(w1g[:], w2g[:], -1.0, 1.0, ALU.mult, ALU.add)
            vmark(ins); vjoin()
            for t in range(4):
                ins = v.tensor_scalar_mul(eq1[:, 8 * t:8 * t + 8], eq1[:, 8 * t:8 * t + 8], w1g[:, t:t + 1])
            vmark(ins); vjoin()
            for t in range(4):
                ins = v.scalar_tensor_tensor(gates_sb[:, 8 * t:8 * t + 8], gt4[:, 8 * t:8 * t + 8],
                                             w2g[:, t:t + 1], eq1[:, 8 * t:8 * t + 8], ALU.mult, ALU.add)
            vmark(ins); vjoin()
            v.tensor_copy(gbf[:], gates_sb[:]).then_inc(p.sem("p2v"), 1)
            v.wait_ge(p.sem("p2gt"), bGT + c + 1)
            v.tensor_copy(gT[:], ps_gt[:]).then_inc(p.sem("p2v"), 1)
    p.semv["p2v"] = bV + NCH * NV
    p.semv["p2xf"] = bXF + NCH
    p.semv["p2te"] = bTE + NCH * 8
    p.semv["p2ss"] = bSS + NCH * 2
    p.semv["p2tfe"] = bTFE + NCH * 4

    @block.scalar
    def _(s):
        for c in range(NCH):
            s.wait_ge(p.sem("p2v"), V(c, 1))
            for t in range(4):
                ins = s.activation(rstd[:, t:t + 1], mv[:, 2 * t + 1:2 * t + 2], AF.Ln, bias=EPS)
            p.inc(ins, "p2sc", 1)
            s.wait_ge(p.sem("p2sc"), p.val("p2sc"))
            for t in range(4):
                ins = s.activation(rstd[:, t:t + 1], rstd[:, t:t + 1], AF.Exp, scale=-0.5)
            ins.then_inc(p.sem("p2s"), 1)
            s.wait_ge(p.sem("p2v"), V(c, 3))
            for t in range(4):
                ins = s.activation(rstdb[:, t:t + 1], mvb[:, 2 * t + 1:2 * t + 2], AF.Ln, bias=EPS)
            p.inc(ins, "p2sc", 1)
            s.wait_ge(p.sem("p2sc"), p.val("p2sc"))
            for t in range(4):
                ins = s.activation(rstdb[:, t:t + 1], rstdb[:, t:t + 1], AF.Exp, scale=-0.5)
            ins.then_inc(p.sem("p2s"), 1)
            s.wait_ge(p.sem("p2v"), V(c, 6))
            s.activation(w2g[:], d4[:], AF.Sigmoid).then_inc(p.sem("p2s"), 1)
            for och in range(32):
                m = c * 32 + och
                s.wait_ge(p.sem("p2m"), bM + m + 1)
                if m >= 2:
                    s.wait_ge(p.sem("p2xz%d" % (m % 2)), bXZ[m % 2] + 16 * (m // 2))
                if och < 16:
                    s.copy(xz2[:, (m % 2) * 512:(m % 2 + 1) * 512],
                           ps_mm[:, (och % 2) * 512:(och % 2 + 1) * 512]).then_inc(p.sem("p2e"), 1)
                else:
                    s.activation(xz2[:, (m % 2) * 512:(m % 2 + 1) * 512],
                                 ps_mm[:, (och % 2) * 512:(och % 2 + 1) * 512],
                                 AF.Silu).then_inc(p.sem("p2e"), 1)
    p.semv["p2s"] = bS + NCH * NS
    p.semv["p2e"] = bE + NCH * 32

    @block.gpsimd
    def _(g):
        for c in range(NCH):
            g.wait_ge(p.sem("p2v"), V(c, 4))
            for t in range(4):
                ins = g.tensor_mul(tmp4[:, t * 1024:(t + 1) * 1024], tmp4[:, t * 1024:(t + 1) * 1024],
                                   ctx["g0bc"][:])
            ins.then_inc(p.sem("p2g"), 1)
            if c == 0:
                g.memset(z3[:], 0.0).then_inc(p.sem("p2zz"), 1)
                g.wait_ge(p.sem("p2zz"), p.val("p2zz") + 1)
                p.semv["p2zz"] += 1
                g.dma_start(out=rr(p.T("xiT_b", write=True), "(dt pp) s -> pp dt s", pp=128)[:, :, 0:3],
                            in_=rr(z3[:], "p (dt s) -> p dt s", s=3)).then_inc(p.sem("p2sA"), 16)
            g.wait_ge(p.sem("p2v"), V(c, 2))
            g.dma_start(out=pt(p.T("xn_f", write=True), c * 512, 4, 0, 1024),
                        in_=rr(xn_sb[:], "p (t d) -> p t d", t=4)).then_inc(p.sem("p2sA"), 16)
            g.wait_ge(p.sem("p2te"), bTE + c * 8 + 8)
            g.dma_start(out=rr(p.T("xnT_b", write=True), "(dt pp) s -> pp dt s", pp=128)[:, :, c * 512:(c + 1) * 512],
                        in_=rr(xnT[:], "p (dt s) -> p dt s", s=512)).then_inc(p.sem("p2sA"), 16)
            for och in range(32):
                m = c * 32 + och
                g.wait_ge(p.sem("p2e"), bE + m + 1)
                if och < 16:
                    dst = rr(p.T("xiT_b", write=True), "(dt pp) s -> pp dt s", pp=128
                             )[:, och, 3 + c * 512:3 + (c + 1) * 512]
                else:
                    dst = rr(p.T("zT_b", write=True), "(dt pp) s -> pp dt s", pp=128
                             )[:, och - 16, c * 512:(c + 1) * 512]
                g.dma_start(out=dst, in_=xz2[:, (m % 2) * 512:(m % 2 + 1) * 512]
                            ).then_inc(p.sem("p2xz%d" % (m % 2)), 16)
            g.wait_ge(p.sem("p2v"), V(c, 7))
            g.dma_start(out=pt(p.T("gates_f", write=True), c * 512, 4, 0, 8),
                        in_=rr(gates_sb[:], "p (t e) -> p t e", e=8)).then_inc(p.sem("p2sA"), 16)
            g.wait_ge(p.sem("p2v"), V(c, 8))
            g.dma_start(out=p.T("gatesT_b", write=True)[0:8, c * 512:(c + 1) * 512],
                        in_=gT[:]).then_inc(p.sem("p2sA"), 16)
            g.wait_ge(p.sem("p2sA"), bSA + 16 * int(cum_st[c + 1]))
            g.wait_ge(p.sem("p2xz0"), bXZ[0] + 16 * (c + 1) * 16)
            g.wait_ge(p.sem("p2xz1"), bXZ[1] + 16 * (c + 1) * 16)
            g.sem_inc(p.sem("p2ck"), 1)
    p.semv["p2g"] = bG + NCH
    p.semv["p2sA"] = bSA + 16 * int(cum_st[NCH])
    p.semv["p2xz0"] = bXZ[0] + 16 * NCH * 16
    p.semv["p2xz1"] = bXZ[1] + 16 * NCH * 16
    p.semv["p2ck"] = bCK + NCH

    @block.tensor
    def _(t_):
        t_.wait_ge(p.sem("p2w"), bW + 16)
        for c in range(NCH):
            for t in range(4):
                for which in range(2):
                    g = c * 8 + t * 2 + which
                    if t == 0:
                        t_.wait_ge(p.sem("p2v"), V(c, 2 if which == 0 else 5))
                    if g >= 2:
                        t_.wait_ge(p.sem("p2te"), bTE + g - 1)
                    src = xnbf if which == 0 else h1bf
                    for dt in range(8):
                        ins = t_.transpose(
                            ps_tr[:, (g % 2) * 1024 + dt * 128:(g % 2) * 1024 + (dt + 1) * 128],
                            src[:, t * 1024 + dt * 128:t * 1024 + (dt + 1) * 128],
                            ctx["eyeb"][:])
                        if dt == 7:
                            ins.then_inc(p.sem("p2t"), 1)
            if c >= 1:
                t_.wait_ge(p.sem("p2v"), V(c - 1, 6))
            for t in range(4):
                n = c * 4 + t
                if n >= 1:
                    t_.wait_ge(p.sem("p2tfe"), bTFE + n)
                for dt in range(8):
                    ins = t_.transpose(ps_trf[:, dt * 128:(dt + 1) * 128],
                                       xn_sb[:, t * 1024 + dt * 128:t * 1024 + (dt + 1) * 128],
                                       ctx["eyef"][:])
                    if dt == 7:
                        ins.then_inc(p.sem("p2tf"), 1)
                t_.wait_ge(p.sem("p2tfe"), bTFE + n + 1)
                for kt in range(8):
                    ins = t_.matmul(ps_g[:, 8 * t:8 * t + 8],
                                    xnTf[:, kt * 128:(kt + 1) * 128],
                                    ctx["WgF"][:, kt * 8:(kt + 1) * 8],
                                    start=(kt == 0), stop=(kt == 7))
                    if kt == 7:
                        ins.then_inc(p.sem("p2gm"), 1)
            for och in range(32):
                m = c * 32 + och
                if m >= 2:
                    t_.wait_ge(p.sem("p2e"), bE + m - 1)
                for kt in range(8):
                    ins = t_.matmul(ps_mm[:, (och % 2) * 512:(och % 2 + 1) * 512],
                                    Winb_sb[:, kt * 4096 + och * 128:kt * 4096 + (och + 1) * 128],
                                    h1T[:, kt * 512:(kt + 1) * 512],
                                    start=(kt == 0), stop=(kt == 7))
                    if kt == 7:
                        ins.then_inc(p.sem("p2m"), 1)
            t_.wait_ge(p.sem("p2v"), V(c, 7))
            if c >= 1:
                t_.wait_ge(p.sem("p2v"), V(c - 1, 8))
            for t in range(4):
                ins = t_.transpose(ps_gt[0:8, t * 128:(t + 1) * 128],
                                   gbf[:, 8 * t:8 * t + 8], ctx["eyeb"][:])
                if t == 3:
                    ins.then_inc(p.sem("p2gt"), 1)
    p.semv["p2t"] = bT + NCH * 8
    p.semv["p2tf"] = bTF + NCH * 4
    p.semv["p2m"] = bM + NCH * 32
    p.semv["p2gm"] = bGM + NCH * 4
    p.semv["p2gt"] = bGT + NCH

    st.close()
    barrier(p, block)


def emit_p3(p, block, ctx):
    st = contextlib.ExitStack()
    xib = p.sb("p3_xib", [128, 16 * 515], BF16, st)
    acc0 = p.sb("p3_acc0", [128, 16 * 512], F32, st)
    acc1 = p.sb("p3_acc1", [128, 16 * 512], F32, st)
    u_sb = p.sb("p3_u", [128, 16 * 512], BF16, st)

    bL = p.val("p3l"); bV = p.val("p3v"); bS = p.val("p3s")
    bST = p.val("p3st"); bCK = p.val("p3ck")

    def xis(dt, k):
        return xib[:, dt * 515 + k:dt * 515 + k + 512]

    @block.sync
    def _(sy):
        for c in range(NCH):
            if c >= 1:
                sy.wait_ge(p.sem("p3ck"), bCK + c)
            sy.dma_start(out=rr(xib[:], "p (dt s) -> p dt s", s=515),
                         in_=rr(p.T("xiT_b"), "(dt pp) s -> pp dt s", pp=128)[:, :, c * 512:c * 512 + 515]
                         ).then_inc(p.sem("p3l"), 16)
    p.semv["p3l"] = bL + 16 * NCH

    @block.vector
    def _(v):
        nv = bV
        for c in range(NCH):
            v.wait_ge(p.sem("p3l"), bL + 16 * (c + 1))
            if c >= 1:
                v.wait_ge(p.sem("p3s"), bS + c)
            for k in range(4):
                src_acc = [acc0, acc1][(k + 1) % 2]
                dst_acc = [acc0, acc1][k % 2]
                for dt in range(16):
                    wck = ctx["Wc"][:, dt * 4 + k:dt * 4 + k + 1]
                    if k == 0:
                        ins = v.tensor_scalar_mul(dst_acc[:, dt * 512:(dt + 1) * 512], xis(dt, 0), wck)
                    else:
                        ins = v.scalar_tensor_tensor(dst_acc[:, dt * 512:(dt + 1) * 512], xis(dt, k),
                                                     wck, src_acc[:, dt * 512:(dt + 1) * 512],
                                                     ALU.mult, ALU.add)
                p.inc(ins, "p3v", 1); nv += 1
                v.wait_ge(p.sem("p3v"), nv)
    p.semv["p3v"] = bV + NCH * 4

    @block.scalar
    def _(s):
        for c in range(NCH):
            s.wait_ge(p.sem("p3v"), bV + c * 4 + 4)
            if c >= 1:
                s.wait_ge(p.sem("p3ck"), bCK + c)
            for dt in range(16):
                ins = s.activation(u_sb[:, dt * 512:(dt + 1) * 512],
                                   acc1[:, dt * 512:(dt + 1) * 512], AF.Silu,
                                   bias=ctx["bconv"][:, dt:dt + 1])
            ins.then_inc(p.sem("p3s"), 1)
    p.semv["p3s"] = bS + NCH

    @block.gpsimd
    def _(g):
        for c in range(NCH):
            g.wait_ge(p.sem("p3s"), bS + c + 1)
            g.dma_start(out=rr(p.T("uT_b", write=True), "(dt pp) s -> pp dt s", pp=128)[:, :, c * 512:(c + 1) * 512],
                        in_=rr(u_sb[:], "p (dt s) -> p dt s", s=512)).then_inc(p.sem("p3st"), 16)
            g.wait_ge(p.sem("p3st"), bST + 16 * (c + 1))
            g.sem_inc(p.sem("p3ck"), 1)
    p.semv["p3st"] = bST + 16 * NCH
    p.semv["p3ck"] = bCK + NCH
    st.close()
    barrier(p, block)


def emit_p4(p, block, ctx):
    st = contextlib.ExitStack()
    ub = p.sb("p4_ub", [128, 16 * 512], BF16, st)
    Wxb_sb = p.sb("p4_Wx", [128, 16 * 96], BF16, st)
    Wdtb_sb = p.sb("p4_Wdt", [64, 16 * 128], BF16, st)
    dtr_bf = p.sb("p4_dtr", [64, 512], BF16, st)
    BC_f = p.sb("p4_BC", [32, 512], BF16, st)
    t_abs = p.sb("p4_tabs", [128, 512], F32, st)
    t_ln = p.sb("p4_tln", [128, 512], F32, st)
    t_rel = p.sb("p4_trel", [128, 512], F32, st)
    dt_sb = p.sb("p4_dt", [128, 16 * 512], BF16, st)
    ps_x = p.ps("p4_psx", [128, 512], F32, st)
    ps_d = p.ps("p4_psd", [128, 2 * 512], F32, st)

    bL = p.val("p4l"); bW = p.val("p4w"); bX = p.val("p4x"); bXE = p.val("p4xe")
    bD = p.val("p4d"); bS = p.val("p4sc")
    bV = p.val("p4v"); bST = p.val("p4st"); bCK = p.val("p4ck")

    @block.sync
    def _(sy):
        sy.dma_start(out=rr(Wxb_sb[:], "p (kt n) -> p kt n", n=96),
                     in_=rr(p.T("Wxb"), "(kt pp) n -> pp kt n", pp=128)).then_inc(p.sem("p4w"), 16)
        sy.dma_start(out=Wdtb_sb[:], in_=p.T("Wdtb")).then_inc(p.sem("p4w"), 16)
        for c in range(NCH):
            if c >= 1:
                sy.wait_ge(p.sem("p4ck"), bCK + c)
            sy.dma_start(out=rr(ub[:], "p (dt s) -> p dt s", s=512),
                         in_=rr(p.T("uT_b"), "(dt pp) s -> pp dt s", pp=128)[:, :, c * 512:(c + 1) * 512]
                         ).then_inc(p.sem("p4l"), 16)
    p.semv["p4w"] = bW + 32
    p.semv["p4l"] = bL + 16 * NCH

    @block.tensor
    def _(t_):
        t_.wait_ge(p.sem("p4w"), bW + 32)
        for c in range(NCH):
            t_.wait_ge(p.sem("p4l"), bL + 16 * (c + 1))
            if c >= 1:
                t_.wait_ge(p.sem("p4xe"), bXE + c)
            for kt in range(16):
                ins = t_.matmul(ps_x[0:96, :], Wxb_sb[:, kt * 96:(kt + 1) * 96],
                                ub[:, kt * 512:(kt + 1) * 512], start=(kt == 0), stop=(kt == 15))
            ins.then_inc(p.sem("p4x"), 1)
            t_.wait_ge(p.sem("p4xe"), bXE + c + 1)
            for dt in range(16):
                n = c * 16 + dt
                if n >= 2:
                    t_.wait_ge(p.sem("p4sc"), bS + 3 * (n - 2) + 1)
                    t_.wait_ge(p.sem("p4v"), bV + 2 * (n - 1))
                t_.matmul(ps_d[:, (n % 2) * 512:(n % 2 + 1) * 512],
                          Wdtb_sb[:, dt * 128:(dt + 1) * 128], dtr_bf[:],
                          start=True, stop=True).then_inc(p.sem("p4d"), 1)
    p.semv["p4x"] = bX + NCH
    p.semv["p4d"] = bD + NCH * 16

    @block.vector
    def _(v):
        nv = bV
        for c in range(NCH):
            v.wait_ge(p.sem("p4x"), bX + c + 1)
            if c >= 1:
                v.wait_ge(p.sem("p4ck"), bCK + c)
                v.wait_ge(p.sem("p4d"), bD + 16 * c)
            v.tensor_copy(dtr_bf[:], ps_x[0:64, :])
            v.tensor_copy(BC_f[:], ps_x[64:96, :]).then_inc(p.sem("p4xe"), 1)
            for dt in range(16):
                n = c * 16 + dt
                v.wait_ge(p.sem("p4sc"), bS + n * 3 + 3)
                ins = v.tensor_scalar(t_rel[:], ps_d[:, (n % 2) * 512:(n % 2 + 1) * 512],
                                      ctx["bdt"][:, dt:dt + 1], 0.0, ALU.add, ALU.max)
                p.inc(ins, "p4v", 1); nv += 1
                v.wait_ge(p.sem("p4v"), nv)
                ins = v.tensor_add(dt_sb[:, dt * 512:(dt + 1) * 512], t_rel[:], t_ln[:])
                p.inc(ins, "p4v", 1); nv += 1
                v.wait_ge(p.sem("p4v"), nv)
    p.semv["p4v"] = bV + NCH * 32
    p.semv["p4xe"] = bXE + NCH

    @block.scalar
    def _(s):
        nsc = bS
        for c in range(NCH):
            for dt in range(16):
                n = c * 16 + dt
                s.wait_ge(p.sem("p4d"), bD + n + 1)
                if n >= 1:
                    s.wait_ge(p.sem("p4v"), bV + 2 * n)
                ins = s.activation(t_abs[:], ps_d[:, (n % 2) * 512:(n % 2 + 1) * 512],
                                   AF.Abs, bias=ctx["bdt"][:, dt:dt + 1])
                p.inc(ins, "p4sc", 1); nsc += 1
                s.wait_ge(p.sem("p4sc"), nsc)
                ins = s.activation(t_abs[:], t_abs[:], AF.Exp, scale=-1.0)
                p.inc(ins, "p4sc", 1); nsc += 1
                s.wait_ge(p.sem("p4sc"), nsc)
                ins = s.activation(t_ln[:], t_abs[:], AF.Ln, bias=1.0)
                p.inc(ins, "p4sc", 1); nsc += 1
    p.semv["p4sc"] = bS + NCH * 48

    @block.gpsimd
    def _(g):
        for c in range(NCH):
            g.wait_ge(p.sem("p4xe"), bXE + c + 1)
            g.dma_start(out=p.T("BmT_b", write=True)[0:16, c * 512:(c + 1) * 512],
                        in_=BC_f[0:16, :]).then_inc(p.sem("p4st"), 16)
            g.dma_start(out=p.T("CmT_b", write=True)[0:16, c * 512:(c + 1) * 512],
                        in_=BC_f[16:32, :]).then_inc(p.sem("p4st"), 16)
            g.wait_ge(p.sem("p4v"), bV + 32 * (c + 1))
            g.dma_start(out=rr(p.T("dtT_b", write=True), "(dt pp) s -> pp dt s", pp=128)[:, :, c * 512:(c + 1) * 512],
                        in_=rr(dt_sb[:], "p (dt s) -> p dt s", s=512)).then_inc(p.sem("p4st"), 16)
            g.wait_ge(p.sem("p4st"), bST + 16 * 3 * (c + 1))
            g.sem_inc(p.sem("p4ck"), 1)
    p.semv["p4st"] = bST + 16 * 3 * NCH
    p.semv["p4ck"] = bCK + NCH
    st.close()
    barrier(p, block)


def emit_p5(p, block, ctx):
    st = contextlib.ExitStack()
    dtc = p.sb("p5_dtc", [128, 16 * 512], BF16, st)
    ub = p.sb("p5_ub", [128, 16 * 512], BF16, st)
    dtu = p.sb("p5_dtu", [128, 16 * 512], BF16, st)
    BbA = p.sb("p5_BbA", [128, 16 * 512], BF16, st)
    CbA = p.sb("p5_CbA", [128, 16 * 512], BF16, st)
    m16 = p.sb("p5_m16", [128, 16 * 512], BF16, st)
    y2 = p.sb("p5_y2", [128, 2 * 512], BF16, st)
    dA2 = p.sb("p5_dA2", [128, 4 * 512], BF16, st)
    be2 = p.sb("p5_be2", [128, 4 * 512], BF16, st)
    h4 = p.sb("p5_h4", [128, 4 * 512], BF16, st)
    zb16 = p.sb("p5_zb16", [128, 16 * 512], BF16, st)
    teb2 = p.sb("p5_teb2", [128, 2 * 512], F32, st)
    yhat = p.sb("p5_yhat", [128, 16 * 512], BF16, st)
    yw_sb = p.sb("p5_yw", [128, 512], F32, st)
    Woutb_sb = p.sb("p5_Wout", [128, 16 * 512], BF16, st)
    ps_w = p.ps("p5_psw", [128, 2 * 512], F32, st)
    hst = ctx["hstate"]

    bDT = p.val("p5dt"); bU = p.val("p5u"); bWL = p.val("p5wlh")
    bDA = p.val("p5da"); bDU = p.val("p5du"); bB = p.val("p5b"); bH = p.val("p5h")
    bHS = p.val("p5hs"); bM = p.val("p5m"); bY = p.val("p5y"); bTC = p.val("p5tc")
    bEB = p.val("p5eb"); bYE = p.val("p5ye")
    bWM = p.val("p5wm"); bWE = p.val("p5we"); bCK = p.val("p5ck")
    bZ = p.val("p5z0")
    bWS = p.val("p5ws0")
    bBB_ = [p.val("p5bb0"), p.val("p5bb1")]
    bCB_ = [p.val("p5cb0"), p.val("p5cb1")]
    G = 256

    @block.sync
    def _(sy):
        for c in range(NCH):
            if c >= 1:
                sy.wait_ge(p.sem("p5ck"), bCK + c)
            sy.dma_start(out=rr(dtc[:], "p (dt s) -> p dt s", s=512),
                         in_=rr(p.T("dtT_b"), "(dt pp) s -> pp dt s", pp=128)[:, :, c * 512:(c + 1) * 512]
                         ).then_inc(p.sem("p5dt"), 16)
            sy.dma_start(out=rr(ub[:], "p (dt s) -> p dt s", s=512),
                         in_=rr(p.T("uT_b"), "(dt pp) s -> pp dt s", pp=128)[:, :, c * 512:(c + 1) * 512]
                         ).then_inc(p.sem("p5u"), 16)
            sy.dma_start(out=rr(zb16[:], "p (dt s) -> p dt s", s=512),
                         in_=rr(p.T("zT_b"), "(dt pp) s -> pp dt s", pp=128)[:, :, c * 512:(c + 1) * 512]
                         ).then_inc(p.sem("p5z0"), 16)
            for n in range(16):
                sl = n % 2
                boff = n * S + c * 512
                sy.dma_start(out=BbA[:, n * 512:(n + 1) * 512],
                             in_=bass.AP(p.T("BmT_b").tensor, boff, [[0, 128], [1, 512]])
                             ).then_inc(p.sem("p5bb%d" % sl), 16)
                sy.dma_start(out=CbA[:, n * 512:(n + 1) * 512],
                             in_=bass.AP(p.T("CmT_b").tensor, boff, [[0, 128], [1, 512]])
                             ).then_inc(p.sem("p5cb%d" % sl), 16)
            for half in range(2):
                hn = c * 2 + half
                if hn >= 1:
                    sy.wait_ge(p.sem("p5wm"), bWM + 4 * hn)
                sy.dma_start(out=rr(Woutb_sb[:], "p (dt n) -> p dt n", n=512),
                             in_=pt(p.T("Woutb"), 0, 16, half * 512, 512)
                             ).then_inc(p.sem("p5wlh"), 16)
                for i in range(4):
                    wn = c * 8 + half * 4 + i
                    sy.wait_ge(p.sem("p5we"), bWE + wn + 1)
                    sy.dma_start(out=pt(p.T("yW_f", write=True), c * 512 + i * 128, 1, half * 512, 512)[:, 0, :],
                                 in_=yw_sb[:]).then_inc(p.sem("p5ws0"), 16)
            sy.wait_ge(p.sem("p5da"), bDA + G * (c + 1))
            sy.wait_ge(p.sem("p5du"), bDU + c + 1)
            sy.wait_ge(p.sem("p5eb"), bEB + 16 * (c + 1))
            sy.wait_ge(p.sem("p5b"), bB + G * (c + 1))
            sy.wait_ge(p.sem("p5m"), bM + G * (c + 1))
            sy.wait_ge(p.sem("p5ye"), bYE + 16 * (c + 1))
            sy.wait_ge(p.sem("p5ws0"), bWS + 16 * 8 * (c + 1))
            sy.sem_inc(p.sem("p5ck"), 1)
    p.semv["p5dt"] = bDT + 16 * NCH
    p.semv["p5u"] = bU + 16 * NCH
    p.semv["p5z0"] = bZ + 16 * NCH
    p.semv["p5bb0"] = bBB_[0] + 16 * 8 * NCH
    p.semv["p5bb1"] = bBB_[1] + 16 * 8 * NCH
    p.semv["p5cb0"] = bCB_[0] + 16 * 8 * NCH
    p.semv["p5cb1"] = bCB_[1] + 16 * 8 * NCH
    p.semv["p5wlh"] = bWL + 16 * 2 * NCH
    p.semv["p5ws0"] = bWS + 16 * 8 * NCH
    p.semv["p5ck"] = bCK + NCH

    @block.scalar
    def _(s):
        for c in range(NCH):
            s.wait_ge(p.sem("p5dt"), bDT + 16 * (c + 1))
            for dt in range(16):
                for n in range(16):
                    g = dt * 16 + n
                    gg = c * G + g
                    if gg >= 4:
                        s.wait_ge(p.sem("p5h"), bH + gg - 3)
                    s.activation(dA2[:, (g % 4) * 512:(g % 4 + 1) * 512], dtc[:, dt * 512:(dt + 1) * 512],
                                 AF.Exp, scale=ctx["A"][:, dt * 16 + n:dt * 16 + n + 1]
                                 ).then_inc(p.sem("p5da"), 1)
            for half in range(2):
                for i in range(4):
                    wn = c * 8 + half * 4 + i
                    s.wait_ge(p.sem("p5wm"), bWM + wn + 1)
                    if wn >= 1:
                        s.wait_ge(p.sem("p5ws0"), bWS + 16 * wn)
                    s.copy(yw_sb[:], ps_w[:, (wn % 2) * 512:(wn % 2 + 1) * 512]).then_inc(p.sem("p5we"), 1)
    p.semv["p5da"] = bDA + NCH * G
    p.semv["p5we"] = bWE + NCH * 8

    @block.gpsimd
    def _(g_):
        for c in range(NCH):
            g_.wait_ge(p.sem("p5dt"), bDT + 16 * (c + 1))
            g_.wait_ge(p.sem("p5u"), bU + 16 * (c + 1))
            g_.wait_ge(p.sem("p5bb0"), bBB_[0] + 16 * 8 * (c + 1))
            g_.wait_ge(p.sem("p5bb1"), bBB_[1] + 16 * 8 * (c + 1))
            if c >= 1:
                g_.wait_ge(p.sem("p5b"), bB + G * c)
            for dt in range(16):
                ins = g_.tensor_mul(dtu[:, dt * 512:(dt + 1) * 512], dtc[:, dt * 512:(dt + 1) * 512],
                                    ub[:, dt * 512:(dt + 1) * 512])
            ins.then_inc(p.sem("p5du"), 1)
            g_.wait_ge(p.sem("p5du"), bDU + c + 1)
            for dt in range(16):
                for n in range(16):
                    g = dt * 16 + n
                    gg = c * G + g
                    if gg >= 4:
                        g_.wait_ge(p.sem("p5h"), bH + gg - 3)
                    g_.tensor_mul(be2[:, (g % 4) * 512:(g % 4 + 1) * 512], dtu[:, dt * 512:(dt + 1) * 512],
                                  BbA[:, n * 512:(n + 1) * 512]).then_inc(p.sem("p5b"), 1)
                    if g >= 4:
                        gs = gg - 4
                        g_.wait_ge(p.sem("p5h"), bH + gs + 1)
                        g_.tensor_copy(hst[:, gs % G:gs % G + 1],
                                       h4[:, (gs % 4) * 512 + 511:(gs % 4) * 512 + 512]
                                       ).then_inc(p.sem("p5hs"), 1)
                if dt >= 2:
                    zn = c * 16 + dt - 2
                    g_.wait_ge(p.sem("p5eb"), bEB + zn + 1)
                    g_.wait_ge(p.sem("p5z0"), bZ + 16 * (c + 1))
                    g_.tensor_mul(yhat[:, (dt - 2) * 512:(dt - 1) * 512],
                                  teb2[:, (zn % 2) * 512:(zn % 2 + 1) * 512],
                                  zb16[:, (dt - 2) * 512:(dt - 1) * 512]).then_inc(p.sem("p5ye"), 1)
            for gs in range(c * G + G - 4, c * G + G):
                g_.wait_ge(p.sem("p5h"), bH + gs + 1)
                g_.tensor_copy(hst[:, gs % G:gs % G + 1],
                               h4[:, (gs % 4) * 512 + 511:(gs % 4) * 512 + 512]).then_inc(p.sem("p5hs"), 1)
            for dtl in [14, 15]:
                zn = c * 16 + dtl
                g_.wait_ge(p.sem("p5eb"), bEB + zn + 1)
                g_.tensor_mul(yhat[:, dtl * 512:(dtl + 1) * 512],
                              teb2[:, (zn % 2) * 512:(zn % 2 + 1) * 512],
                              zb16[:, dtl * 512:(dtl + 1) * 512]).then_inc(p.sem("p5ye"), 1)
    p.semv["p5du"] = bDU + NCH
    p.semv["p5b"] = bB + NCH * G
    p.semv["p5hs"] = bHS + NCH * G
    p.semv["p5ye"] = bYE + NCH * 16

    @block.vector
    def _(v):
        for c in range(NCH):
            v.wait_ge(p.sem("p5cb0"), bCB_[0] + 16 * 8 * (c + 1))
            v.wait_ge(p.sem("p5cb1"), bCB_[1] + 16 * 8 * (c + 1))
            for dt in range(16):
                if dt == 0 and c >= 1:
                    v.wait_ge(p.sem("p5wm"), bWM + 8 * c)
                for n in range(16):
                    g = dt * 16 + n
                    gg = c * G + g
                    v.wait_ge(p.sem("p5da"), bDA + gg + 1)
                    v.wait_ge(p.sem("p5b"), bB + gg + 1)
                    if gg >= 4:
                        v.wait_ge(p.sem("p5hs"), bHS + gg - 3)
                        v.wait_ge(p.sem("p5m"), bM + gg - 3)
                    v.tensor_tensor_scan(h4[:, (g % 4) * 512:(g % 4 + 1) * 512],
                                         dA2[:, (g % 4) * 512:(g % 4 + 1) * 512],
                                         be2[:, (g % 4) * 512:(g % 4 + 1) * 512],
                                         hst[:, g:g + 1], ALU.mult, ALU.add).then_inc(p.sem("p5h"), 1)
                    v.wait_ge(p.sem("p5h"), bH + gg + 1)
                    if c * 16 + dt >= 1 and n == 0:
                        v.wait_ge(p.sem("p5y"), bY + c * 16 + dt)
                    v.tensor_mul(m16[:, n * 512:(n + 1) * 512], h4[:, (g % 4) * 512:(g % 4 + 1) * 512],
                                 CbA[:, n * 512:(n + 1) * 512]).then_inc(p.sem("p5m"), 1)
                v.wait_ge(p.sem("p5m"), bM + c * G + dt * 16 + 16)
                for lvl, stp in [(8, 1), (4, 2), (2, 4)]:
                    for i in range(lvl):
                        a0 = 2 * i * stp
                        ins = v.tensor_add(m16[:, a0 * 512:(a0 + 1) * 512],
                                           m16[:, a0 * 512:(a0 + 1) * 512],
                                           m16[:, (a0 + stp) * 512:(a0 + stp + 1) * 512])
                    p.inc(ins, "p5tc", 1)
                    v.wait_ge(p.sem("p5tc"), p.val("p5tc"))
                zn = c * 16 + dt
                if zn >= 2:
                    v.wait_ge(p.sem("p5eb"), bEB + zn - 1)
                ins = v.tensor_add(y2[:, (zn % 2) * 512:(zn % 2 + 1) * 512],
                                   m16[:, 0:512], m16[:, 8 * 512:9 * 512])
                ins.then_inc(p.sem("p5y"), 1)
                v.wait_ge(p.sem("p5y"), bY + zn + 1)
                if zn >= 2:
                    v.wait_ge(p.sem("p5ye"), bYE + zn - 1)
                v.scalar_tensor_tensor(teb2[:, (zn % 2) * 512:(zn % 2 + 1) * 512],
                                       ub[:, dt * 512:(dt + 1) * 512], ctx["Dskip"][:, dt:dt + 1],
                                       y2[:, (zn % 2) * 512:(zn % 2 + 1) * 512], ALU.mult, ALU.add
                                       ).then_inc(p.sem("p5eb"), 1)
    p.semv["p5h"] = bH + NCH * G
    p.semv["p5m"] = bM + NCH * G
    p.semv["p5y"] = bY + NCH * 16
    p.semv["p5eb"] = bEB + NCH * 16

    @block.tensor
    def _(t_):
        for c in range(NCH):
            t_.wait_ge(p.sem("p5ye"), bYE + 16 * (c + 1))
            for half in range(2):
                hn = c * 2 + half
                t_.wait_ge(p.sem("p5wlh"), bWL + 16 * (hn + 1))
                for i in range(4):
                    wn = c * 8 + half * 4 + i
                    if wn >= 1:
                        t_.wait_ge(p.sem("p5we"), bWE + wn)
                    for dt in range(16):
                        ins = t_.matmul(ps_w[:, (wn % 2) * 512:(wn % 2 + 1) * 512],
                                        yhat[:, dt * 512 + i * 128:dt * 512 + (i + 1) * 128],
                                        Woutb_sb[:, dt * 512:(dt + 1) * 512],
                                        start=(dt == 0), stop=(dt == 15))
                    ins.then_inc(p.sem("p5wm"), 1)
    p.semv["p5wm"] = bWM + NCH * 8
    st.close()
    barrier(p, block)


def emit_p56(p, block, ctx):
    st = contextlib.ExitStack()
    dt2 = p.sb("p56_dt2", [128, 2 * 512], BF16, st)
    ub2 = p.sb("p56_ub2", [128, 2 * 512], BF16, st)
    dtu2 = p.sb("p56_dtu2", [128, 2 * 512], BF16, st)
    BbA = p.sb("p56_BbA", [128, 16 * 512], BF16, st)
    CbA = p.sb("p56_CbA", [128, 16 * 512], BF16, st)
    m16 = p.sb("p56_m16", [128, 16 * 512], BF16, st)
    y2 = p.sb("p56_y2", [128, 2 * 512], BF16, st)
    dA4 = p.sb("p56_dA4", [128, 8 * 512], BF16, st)
    be4 = p.sb("p56_be4", [128, 8 * 512], BF16, st)
    h4 = p.sb("p56_h4", [128, 4 * 512], F32, st)
    zb2 = p.sb("p56_zb2", [128, 4 * 512], BF16, st)
    teb2 = p.sb("p56_teb2", [128, 2 * 512], BF16, st)
    yhat = p.sb("p56_yhat", [128, 16 * 512], BF16, st)
    yw_sb = p.sb("p56_yw", [128, 512], F32, st)
    Wob = p.sb("p56_Wob", [128, 16 * 512], BF16, st)
    xnT = p.sb("p56_xnT", [128, 8 * 512], BF16, st)
    W1r = p.sb("p56_W1r", [128, 2 * 8 * 128], BF16, st)
    W2h = p.sb("p56_W2h", [128, 16 * 512], BF16, st)
    heT = p.sb("p56_heT", [128, 16 * 512], BF16, st)
    acc = p.sb("p56_acc", [128, 4 * 1024], F32, st)
    gf_sb = p.sb("p56_gf", [128, 32], F32, st)
    gT_sb = p.sb("p56_gT", [8, 512], BF16, st)
    b2b_sb = p.sb("p56_b2b", [8, 1024], BF16, st)
    ps_w = p.ps("p56_psw", [128, 2 * 512], F32, st)
    ps_h = p.ps("p56_psh", [128, 2 * 512], F32, st)
    ps_o = p.ps("p56_pso", [128, 4 * 512], F32, st)
    hst = ctx["hstate"]
    G = 256

    bDT = [p.val("c_dt0"), p.val("c_dt1")]
    bU = [p.val("c_u0"), p.val("c_u1")]
    bZ = [p.val("c_z0"), p.val("c_z1"), p.val("c_z2"), p.val("c_z3")]
    bWL = p.val("c_wl")
    bDA = p.val("c_da"); bDU = p.val("c_du"); bB = p.val("c_b"); bH = p.val("c_h")
    bHS = p.val("c_hs"); bM = p.val("c_m"); bY = p.val("c_y"); bTC = p.val("c_tc")
    bEB = p.val("c_eb"); bYE = p.val("c_ye"); bWM = p.val("c_wm"); bWE = p.val("c_we")
    bCK = p.val("c_ck"); bWS = p.val("c_ws")
    bBB = [p.val("c_bb0"), p.val("c_bb1")]
    bCB = [p.val("c_cb0"), p.val("c_cb1")]
    bXT = p.val("c_xt"); bGF = p.val("c_gf"); bGT = p.val("c_gt"); bB2L = p.val("c_b2l")
    bW1 = [p.val("c_w10"), p.val("c_w11")]
    bW2 = p.val("c_w2")
    bM1 = p.val("c_m1"); bE1 = p.val("c_e1"); bM2 = p.val("c_m2"); bAE = p.val("c_ae")
    bB2 = p.val("c_b2"); bB2M = p.val("c_b2m"); bMS = p.val("c_ms")

    @block.sync
    def _(sy):
        sy.dma_start(out=b2b_sb[:], in_=p.T("b2b")).then_inc(p.sem("c_b2l"), 16)
        for c in range(NCH):
            if c >= 1:
                sy.wait_ge(p.sem("c_ck"), bCK + c)
            for n in range(16):
                sl = n % 2
                boff = n * S + c * 512
                sy.dma_start(out=BbA[:, n * 512:(n + 1) * 512],
                             in_=bass.AP(p.T("BmT_b").tensor, boff, [[0, 128], [1, 512]])
                             ).then_inc(p.sem("c_bb%d" % sl), 16)
                sy.dma_start(out=CbA[:, n * 512:(n + 1) * 512],
                             in_=bass.AP(p.T("CmT_b").tensor, boff, [[0, 128], [1, 512]])
                             ).then_inc(p.sem("c_cb%d" % sl), 16)
            sy.dma_start(out=rr(xnT[:], "p (dt s) -> p dt s", s=512),
                         in_=rr(p.T("xnT_b"), "(dt pp) s -> pp dt s", pp=128)[:, :, c * 512:(c + 1) * 512]
                         ).then_inc(p.sem("c_xt"), 16)
            sy.dma_start(out=rr(gf_sb[:], "p (t e) -> p t e", e=8),
                         in_=pt(p.T("gates_f"), c * 512, 4, 0, 8)).then_inc(p.sem("c_gf"), 16)
            sy.dma_start(out=gT_sb[:], in_=p.T("gatesT_b")[0:8, c * 512:(c + 1) * 512]
                         ).then_inc(p.sem("c_gt"), 16)
            for dt in range(16):
                gdt = c * 16 + dt
                sl = dt % 2
                for j8 in range(8):
                    on_l = dt * 8 + j8
                    on = c * 128 + on_l
                    e_, och_ = divmod(on_l, 16)
                    wsl = on % 2
                    if on >= 2:
                        sy.wait_ge(p.sem("c_m1"), bM1 + on - 1)
                    sy.dma_start(out=rr(W1r[:, wsl * 1024:(wsl + 1) * 1024], "p (kt n) -> p kt n", kt=8),
                                 in_=rr(rr(p.T("W1b"), "e r c -> (e r) c"), "(kt pp) n -> pp kt n", pp=128
                                        )[:, e_ * 8:(e_ + 1) * 8, och_ * 128:(och_ + 1) * 128]
                                 ).then_inc(p.sem("c_w1%d" % wsl), 16)
                if True:
                    wh_l = dt
                    wh = c * 16 + wh_l
                    e_, half_ = divmod(wh_l, 2)
                    if wh >= 1:
                        sy.wait_ge(p.sem("c_m2"), bM2 + 4 * wh)
                    sy.dma_start(out=rr(W2h[:], "p (kt n) -> p kt n", kt=16),
                                 in_=rr(rr(p.T("W2b"), "e r c -> (e r) c"), "(kt pp) n -> pp kt n", pp=128
                                        )[:, e_ * 16:(e_ + 1) * 16, half_ * 512:(half_ + 1) * 512]
                                 ).then_inc(p.sem("c_w2"), 16)
            for half in range(2):
                hn = c * 2 + half
                if hn >= 1:
                    sy.wait_ge(p.sem("c_wm"), bWM + 4 * hn)
                sy.dma_start(out=rr(Wob[:], "p (dt n) -> p dt n", n=512),
                             in_=pt(p.T("Woutb"), 0, 16, half * 512, 512)).then_inc(p.sem("c_wl"), 16)
                for i in range(4):
                    wn = c * 8 + half * 4 + i
                    sy.wait_ge(p.sem("c_we"), bWE + wn + 1)
                    sy.dma_start(out=pt(p.T("yW_f", write=True), c * 512 + i * 128, 1, half * 512, 512)[:, 0, :],
                                 in_=yw_sb[:]).then_inc(p.sem("c_ws"), 16)
            sy.wait_ge(p.sem("c_b2"), bB2 + 8 * (c + 1))
            sy.dma_start(out=pt(p.T("moe_f", write=True), c * 512, 4, 0, 1024),
                         in_=rr(acc[:], "p (t d) -> p t d", t=4)).then_inc(p.sem("c_ms"), 16)
            sy.wait_ge(p.sem("c_b"), bB + G * (c + 1))
            sy.wait_ge(p.sem("c_m"), bM + G * (c + 1))
            sy.wait_ge(p.sem("c_ye"), bYE + 16 * (c + 1))
            sy.wait_ge(p.sem("c_ws"), bWS + 16 * 8 * (c + 1))
            sy.wait_ge(p.sem("c_ms"), bMS + 16 * (c + 1))
            sy.sem_inc(p.sem("c_ck"), 1)
    p.semv["c_b2l"] = bB2L + 16
    for sl_ in range(2):
        p.semv["c_dt%d" % sl_] = bDT[sl_] + 16 * 8 * NCH
        p.semv["c_u%d" % sl_] = bU[sl_] + 16 * 8 * NCH
    for sl_ in range(4):
        p.semv["c_z%d" % sl_] = bZ[sl_] + 16 * 4 * NCH
    p.semv["c_bb0"] = bBB[0] + 16 * 8 * NCH
    p.semv["c_bb1"] = bBB[1] + 16 * 8 * NCH
    p.semv["c_cb0"] = bCB[0] + 16 * 8 * NCH
    p.semv["c_cb1"] = bCB[1] + 16 * 8 * NCH
    p.semv["c_xt"] = bXT + 16 * NCH
    p.semv["c_gf"] = bGF + 16 * NCH
    p.semv["c_gt"] = bGT + 16 * NCH
    p.semv["c_w10"] = bW1[0] + 16 * 64 * NCH
    p.semv["c_w11"] = bW1[1] + 16 * 64 * NCH
    p.semv["c_w2"] = bW2 + 16 * 16 * NCH
    p.semv["c_wl"] = bWL + 16 * 2 * NCH
    p.semv["c_ws"] = bWS + 16 * 8 * NCH
    p.semv["c_ms"] = bMS + 16 * NCH
    p.semv["c_ck"] = bCK + NCH

    @block.scalar
    def _(s):
        for c in range(NCH):
            for dt in range(16):
                gdt = c * 16 + dt
                sl = dt % 2
                s.wait_ge(p.sem("c_dt%d" % sl), bDT[sl] + 16 * (gdt // 2 + 1))
                for n in range(16):
                    g = dt * 16 + n
                    gg = c * G + g
                    if gg >= 8:
                        s.wait_ge(p.sem("c_h"), bH + gg - 7)
                    s.activation(dA4[:, (g % 8) * 512:(g % 8 + 1) * 512], dt2[:, sl * 512:(sl + 1) * 512],
                                 AF.Exp, scale=ctx["A"][:, dt * 16 + n:dt * 16 + n + 1]
                                 ).then_inc(p.sem("c_da"), 1)
                for och8 in range(8):
                    on = c * 128 + dt * 8 + och8
                    s.wait_ge(p.sem("c_m1"), bM1 + on + 1)
                    e_, och_ = divmod(on % 128, 16)
                    s.activation(heT[:, och_ * 512:(och_ + 1) * 512],
                                 ps_h[:, (on % 2) * 512:(on % 2 + 1) * 512], AF.Relu,
                                 bias=ctx["b1"][:, e_ * 16 + och_:e_ * 16 + och_ + 1]
                                 ).then_inc(p.sem("c_e1"), 1)
            for half in range(2):
                for i in range(4):
                    wn = c * 8 + half * 4 + i
                    s.wait_ge(p.sem("c_wm"), bWM + wn + 1)
                    if wn >= 1:
                        s.wait_ge(p.sem("c_ws"), bWS + 16 * wn)
                    s.copy(yw_sb[:], ps_w[:, (wn % 2) * 512:(wn % 2 + 1) * 512]).then_inc(p.sem("c_we"), 1)
    p.semv["c_da"] = bDA + NCH * G
    p.semv["c_e1"] = bE1 + NCH * 128
    p.semv["c_we"] = bWE + NCH * 8

    @block.gpsimd
    def _(g_):
        def ld_rings(gdt):
            if gdt >= NCH * 16:
                return
            cc, dtt = divmod(gdt, 16)
            sl_ = gdt % 2
            if gdt >= 2:
                g_.wait_ge(p.sem("c_da"), bDA + (gdt - 1) * 16)
                g_.wait_ge(p.sem("c_du"), bDU + gdt - 1)
            g_.dma_start(out=dt2[:, sl_ * 512:(sl_ + 1) * 512],
                         in_=rr(p.T("dtT_b"), "(dt pp) s -> pp dt s", pp=128)[:, dtt, cc * 512:(cc + 1) * 512]
                         ).then_inc(p.sem("c_dt%d" % sl_), 16)
            if gdt >= 2:
                g_.wait_ge(p.sem("c_eb"), bEB + gdt - 1)
            g_.dma_start(out=ub2[:, sl_ * 512:(sl_ + 1) * 512],
                         in_=rr(p.T("uT_b"), "(dt pp) s -> pp dt s", pp=128)[:, dtt, cc * 512:(cc + 1) * 512]
                         ).then_inc(p.sem("c_u%d" % sl_), 16)
            zsl = gdt % 4
            if gdt >= 4:
                g_.wait_ge(p.sem("c_ye"), bYE + gdt - 3)
            g_.dma_start(out=zb2[:, zsl * 512:(zsl + 1) * 512],
                         in_=rr(p.T("zT_b"), "(dt pp) s -> pp dt s", pp=128)[:, dtt, cc * 512:(cc + 1) * 512]
                         ).then_inc(p.sem("c_z%d" % zsl), 16)

        ld_rings(0)
        ld_rings(1)
        for c in range(NCH):
            for dt in range(16):
                gdt = c * 16 + dt
                sl = dt % 2
                if dt >= 2:
                    zn = c * 16 + dt - 2
                    g_.wait_ge(p.sem("c_eb"), bEB + zn + 1)
                    g_.wait_ge(p.sem("c_z%d" % (zn % 4)), bZ[zn % 4] + 16 * (zn // 4 + 1))
                    g_.tensor_mul(yhat[:, (dt - 2) * 512:(dt - 1) * 512],
                                  teb2[:, (zn % 2) * 512:(zn % 2 + 1) * 512],
                                  zb2[:, (zn % 4) * 512:(zn % 4 + 1) * 512]).then_inc(p.sem("c_ye"), 1)
                g_.wait_ge(p.sem("c_dt%d" % sl), bDT[sl] + 16 * (gdt // 2 + 1))
                g_.wait_ge(p.sem("c_u%d" % sl), bU[sl] + 16 * (gdt // 2 + 1))
                if gdt >= 2:
                    g_.wait_ge(p.sem("c_b"), bB + (gdt - 1) * 16)
                g_.tensor_mul(dtu2[:, sl * 512:(sl + 1) * 512], dt2[:, sl * 512:(sl + 1) * 512],
                              ub2[:, sl * 512:(sl + 1) * 512]).then_inc(p.sem("c_du"), 1)
                g_.wait_ge(p.sem("c_du"), bDU + gdt + 1)
                if dt == 0:
                    g_.wait_ge(p.sem("c_bb0"), bBB[0] + 16 * 8 * (c + 1))
                    g_.wait_ge(p.sem("c_bb1"), bBB[1] + 16 * 8 * (c + 1))
                for n in range(16):
                    g = dt * 16 + n
                    gg = c * G + g
                    if gg >= 8:
                        g_.wait_ge(p.sem("c_h"), bH + gg - 7)
                    g_.tensor_mul(be4[:, (g % 8) * 512:(g % 8 + 1) * 512], dtu2[:, sl * 512:(sl + 1) * 512],
                                  BbA[:, n * 512:(n + 1) * 512]).then_inc(p.sem("c_b"), 1)
                    if g % 4 == 3 and g >= 7:
                        k = g // 4 - 1
                        gs0 = c * G + 4 * k
                        g_.wait_ge(p.sem("c_h"), bH + gs0 + 4)
                        g_.tensor_copy(hst[:, 4 * k:4 * k + 4],
                                       bass.AP(h4, 511, [[2048, 128], [512, 4]])
                                       ).then_inc(p.sem("c_hs"), 4)
                if True:
                    ld_rings(gdt + 2)
            for k in range(G // 4 - 1, G // 4):
                gs0 = c * G + 4 * k
                g_.wait_ge(p.sem("c_h"), bH + gs0 + 4)
                g_.tensor_copy(hst[:, 4 * k:4 * k + 4],
                               bass.AP(h4, 511, [[2048, 128], [512, 4]])
                               ).then_inc(p.sem("c_hs"), 4)
            for dtl in [14, 15]:
                zn = c * 16 + dtl
                g_.wait_ge(p.sem("c_eb"), bEB + zn + 1)
                g_.wait_ge(p.sem("c_z%d" % (zn % 4)), bZ[zn % 4] + 16 * (zn // 4 + 1))
                g_.tensor_mul(yhat[:, dtl * 512:(dtl + 1) * 512],
                              teb2[:, (zn % 2) * 512:(zn % 2 + 1) * 512],
                              zb2[:, (zn % 4) * 512:(zn % 4 + 1) * 512]).then_inc(p.sem("c_ye"), 1)
    p.semv["c_du"] = bDU + NCH * 16
    p.semv["c_b"] = bB + NCH * G
    p.semv["c_hs"] = bHS + NCH * G
    p.semv["c_ye"] = bYE + NCH * 16

    @block.vector
    def _(v):
        for c in range(NCH):
            v.wait_ge(p.sem("c_cb0"), bCB[0] + 16 * 8 * (c + 1))
            v.wait_ge(p.sem("c_cb1"), bCB[1] + 16 * 8 * (c + 1))
            v.wait_ge(p.sem("c_gf"), bGF + 16 * (c + 1))
            if c >= 1:
                v.wait_ge(p.sem("c_ck"), bCK + c)
            for dt in range(16):
                if dt == 0 and c >= 1:
                    v.wait_ge(p.sem("c_wm"), bWM + 8 * c)
                for n in range(16):
                    g = dt * 16 + n
                    gg = c * G + g
                    v.wait_ge(p.sem("c_da"), bDA + gg + 1)
                    v.wait_ge(p.sem("c_b"), bB + gg + 1)
                    if gg >= 4:
                        v.wait_ge(p.sem("c_hs"), bHS + 4 * ((gg - 4) // 4 + 1))
                        v.wait_ge(p.sem("c_m"), bM + gg - 3)
                    v.tensor_tensor_scan(h4[:, (g % 4) * 512:(g % 4 + 1) * 512],
                                         dA4[:, (g % 8) * 512:(g % 8 + 1) * 512],
                                         be4[:, (g % 8) * 512:(g % 8 + 1) * 512],
                                         hst[:, g:g + 1], ALU.mult, ALU.add).then_inc(p.sem("c_h"), 1)
                    v.wait_ge(p.sem("c_h"), bH + gg + 1)
                    if c * 16 + dt >= 1 and n == 0:
                        v.wait_ge(p.sem("c_y"), bY + c * 16 + dt)
                    v.tensor_mul(m16[:, n * 512:(n + 1) * 512], h4[:, (g % 4) * 512:(g % 4 + 1) * 512],
                                 CbA[:, n * 512:(n + 1) * 512]).then_inc(p.sem("c_m"), 1)
                v.wait_ge(p.sem("c_m"), bM + c * G + dt * 16 + 16)
                for lvl, stp in [(8, 1), (4, 2), (2, 4)]:
                    for i in range(lvl):
                        a0 = 2 * i * stp
                        ins = v.tensor_add(m16[:, a0 * 512:(a0 + 1) * 512],
                                           m16[:, a0 * 512:(a0 + 1) * 512],
                                           m16[:, (a0 + stp) * 512:(a0 + stp + 1) * 512])
                    p.inc(ins, "c_tc", 1)
                    v.wait_ge(p.sem("c_tc"), p.val("c_tc"))
                zn = c * 16 + dt
                if zn >= 2:
                    v.wait_ge(p.sem("c_eb"), bEB + zn - 1)
                ins = v.tensor_add(y2[:, (zn % 2) * 512:(zn % 2 + 1) * 512],
                                   m16[:, 0:512], m16[:, 8 * 512:9 * 512])
                ins.then_inc(p.sem("c_y"), 1)
                v.wait_ge(p.sem("c_y"), bY + zn + 1)
                v.wait_ge(p.sem("c_u%d" % (dt % 2)), bU[dt % 2] + 16 * (zn // 2 + 1))
                if zn >= 2:
                    v.wait_ge(p.sem("c_ye"), bYE + zn - 1)
                v.scalar_tensor_tensor(teb2[:, (zn % 2) * 512:(zn % 2 + 1) * 512],
                                       ub2[:, (dt % 2) * 512:(dt % 2 + 1) * 512], ctx["Dskip"][:, dt:dt + 1],
                                       y2[:, (zn % 2) * 512:(zn % 2 + 1) * 512], ALU.mult, ALU.add
                                       ).then_inc(p.sem("c_eb"), 1)
                for q in (range(4) if dt >= 1 else []):
                    mn = c * 64 + (dt - 1) * 4 + q
                    e_, tq = divmod(mn % 64, 8)
                    half_i, t_i = divmod(tq, 4)
                    v.wait_ge(p.sem("c_m2"), bM2 + mn + 1)
                    dst = acc[:, t_i * 1024 + half_i * 512:t_i * 1024 + (half_i + 1) * 512]
                    gcol = gf_sb[:, t_i * 8 + e_:t_i * 8 + e_ + 1]
                    src2 = ps_o[:, (mn % 4) * 512:(mn % 4 + 1) * 512]
                    if e_ == 0:
                        ins = v.tensor_scalar_mul(dst, src2, gcol)
                    else:
                        ins = v.scalar_tensor_tensor(dst, src2, gcol, dst, ALU.mult, ALU.add)
                    ins.then_inc(p.sem("c_ae"), 1)
            for q in range(4):
                mn = c * 64 + 60 + q
                e_, tq = divmod(mn % 64, 8)
                half_i, t_i = divmod(tq, 4)
                v.wait_ge(p.sem("c_m2"), bM2 + mn + 1)
                dst = acc[:, t_i * 1024 + half_i * 512:t_i * 1024 + (half_i + 1) * 512]
                gcol = gf_sb[:, t_i * 8 + e_:t_i * 8 + e_ + 1]
                src2 = ps_o[:, (mn % 4) * 512:(mn % 4 + 1) * 512]
                if e_ == 0:
                    ins = v.tensor_scalar_mul(dst, src2, gcol)
                else:
                    ins = v.scalar_tensor_tensor(dst, src2, gcol, dst, ALU.mult, ALU.add)
                ins.then_inc(p.sem("c_ae"), 1)
            for bn8 in range(8):
                bn = c * 8 + bn8
                t_i, half_i = divmod(bn8, 2)
                v.wait_ge(p.sem("c_b2m"), bB2M + bn + 1)
                dst = acc[:, t_i * 1024 + half_i * 512:t_i * 1024 + (half_i + 1) * 512]
                ins = v.tensor_add(dst, dst, ps_o[:, (bn % 4) * 512:(bn % 4 + 1) * 512])
                ins.then_inc(p.sem("c_b2"), 1)
    p.semv["c_h"] = bH + NCH * G
    p.semv["c_m"] = bM + NCH * G
    p.semv["c_y"] = bY + NCH * 16
    p.semv["c_eb"] = bEB + NCH * 16
    p.semv["c_ae"] = bAE + NCH * 64
    p.semv["c_b2"] = bB2 + NCH * 8

    @block.tensor
    def _(t_):
        for c in range(NCH):
            t_.wait_ge(p.sem("c_xt"), bXT + 16 * (c + 1))
            for e in range(NE):
                en = c * 8 + e
                for och in range(16):
                    on = en * 16 + och
                    sl = on % 2
                    t_.wait_ge(p.sem("c_w1%d" % sl), bW1[sl] + 16 * (on // 2 + 1))
                    if on >= 2:
                        t_.wait_ge(p.sem("c_e1"), bE1 + on - 1)
                    for kt in range(8):
                        ins = t_.matmul(ps_h[:, (on % 2) * 512:(on % 2 + 1) * 512],
                                        W1r[:, sl * 1024 + kt * 128:sl * 1024 + (kt + 1) * 128],
                                        xnT[:, kt * 512:(kt + 1) * 512],
                                        start=(kt == 0), stop=(kt == 7))
                    ins.then_inc(p.sem("c_m1"), 1)
                t_.wait_ge(p.sem("c_e1"), bE1 + (en + 1) * 16)
                for half in range(2):
                    wh = en * 2 + half
                    t_.wait_ge(p.sem("c_w2"), bW2 + 16 * (wh + 1))
                    for t in range(4):
                        mn = en * 8 + half * 4 + t
                        if mn >= 4:
                            t_.wait_ge(p.sem("c_ae"), bAE + mn - 3)
                        for kt in range(16):
                            ins = t_.matmul(ps_o[:, (mn % 4) * 512:(mn % 4 + 1) * 512],
                                            heT[:, kt * 512 + t * 128:kt * 512 + (t + 1) * 128],
                                            W2h[:, kt * 512:(kt + 1) * 512],
                                            start=(kt == 0), stop=(kt == 15))
                        ins.then_inc(p.sem("c_m2"), 1)
            t_.wait_ge(p.sem("c_gt"), bGT + 16 * (c + 1))
            t_.wait_ge(p.sem("c_b2l"), bB2L + 16)
            t_.wait_ge(p.sem("c_ae"), bAE + 64 * (c + 1))
            for bn8 in range(8):
                bn = c * 8 + bn8
                tt, hh = divmod(bn8, 2)
                if bn >= 4:
                    t_.wait_ge(p.sem("c_b2"), bB2 + bn - 3)
                t_.matmul(ps_o[:, (bn % 4) * 512:(bn % 4 + 1) * 512],
                          gT_sb[0:8, tt * 128:(tt + 1) * 128],
                          b2b_sb[0:8, hh * 512:(hh + 1) * 512],
                          start=True, stop=True).then_inc(p.sem("c_b2m"), 1)
            t_.wait_ge(p.sem("c_ye"), bYE + 16 * (c + 1))
            for half in range(2):
                hn = c * 2 + half
                t_.wait_ge(p.sem("c_wl"), bWL + 16 * (hn + 1))
                for i in range(4):
                    wn = c * 8 + half * 4 + i
                    if wn >= 1:
                        t_.wait_ge(p.sem("c_we"), bWE + wn)
                    for dt in range(16):
                        ins = t_.matmul(ps_w[:, (wn % 2) * 512:(wn % 2 + 1) * 512],
                                        yhat[:, dt * 512 + i * 128:dt * 512 + (i + 1) * 128],
                                        Wob[:, dt * 512:(dt + 1) * 512],
                                        start=(dt == 0), stop=(dt == 15))
                    ins.then_inc(p.sem("c_wm"), 1)
    p.semv["c_m1"] = bM1 + NCH * 128
    p.semv["c_m2"] = bM2 + NCH * 64
    p.semv["c_b2m"] = bB2M + NCH * 8
    p.semv["c_wm"] = bWM + NCH * 8
    st.close()
    barrier(p, block)


def emit_p6b(p, block, ctx):
    st = contextlib.ExitStack()
    acc = p.sb("p6b_acc", [128, 4 * 1024], F32, st)
    xn_tm = p.sb("p6b_xntm", [128, 4 * 1024], F32, st)
    x2 = p.sb("p6b_x2", [128, 2 * 1024], F32, st)
    yW2 = p.sb("p6b_yW2", [128, 2 * 1024], F32, st)
    st6 = p.sb("p6b_st6", [128, 4 * 12], F32, st)
    mv = p.sb("p6b_mv", [128, 8], F32, st)
    rstdF = p.sb("p6b_rstdF", [128, 4], F32, st)
    st6b = p.sb("p6b_st6b", [128, 4 * 12], F32, st)
    mvb = p.sb("p6b_mvb", [128, 8], F32, st)
    rstd1 = p.sb("p6b_rstd1", [128, 4], F32, st)
    xn1bf = p.sb("p6b_xn1bf", [128, 4096], BF16, st)
    xn1T = p.sb("p6b_xn1T", [128, 8 * 512], BF16, st)
    ps_tr = p.ps("p6b_pstr", [128, 2 * 1024], BF16, st)

    bMO = p.val("b_mo"); bXN = p.val("b_xn"); bSS = p.val("b_ss"); bS = p.val("b_s")
    bG = p.val("b_g"); bV = p.val("b_v"); bCB = p.val("b_cb"); bT = p.val("b_t")
    bTE = p.val("b_te"); bSA = p.val("b_sA"); bCK = p.val("b_ck")
    bXL = [p.val("b_xl0"), p.val("b_xl1")]
    bYL = [p.val("b_yl0"), p.val("b_yl1")]
    NV = 4

    def V(c, k):
        return bV + c * NV + k

    @block.sync
    def _(sy):
        for c in range(NCH):
            if c >= 1:
                sy.wait_ge(p.sem("b_ck"), bCK + c)
            sy.dma_start(out=rr(acc[:], "p (t d) -> p t d", t=4),
                         in_=pt(p.T("moe_f"), c * 512, 4, 0, 1024)).then_inc(p.sem("b_mo"), 16)
            sy.dma_start(out=rr(xn_tm[:], "p (t d) -> p t d", t=4),
                         in_=pt(p.T("xn_f"), c * 512, 4, 0, 1024)).then_inc(p.sem("b_xn"), 16)
            for t in range(4):
                tn = c * 4 + t
                sl = t % 2
                if tn >= 2:
                    sy.wait_ge(p.sem("b_cb"), bCB + tn - 1)
                sy.dma_start(out=x2[:, sl * 1024:(sl + 1) * 1024],
                             in_=pt(p.T("x"), c * 512 + t * 128, 1, 0, 1024)[:, 0, :]
                             ).then_inc(p.sem("b_xl%d" % sl), 16)
                sy.dma_start(out=yW2[:, sl * 1024:(sl + 1) * 1024],
                             in_=pt(p.T("yW_f"), c * 512 + t * 128, 1, 0, 1024)[:, 0, :]
                             ).then_inc(p.sem("b_yl%d" % sl), 16)
            sy.wait_ge(p.sem("b_cb"), bCB + 4 * (c + 1))
            sy.dma_start(out=pt(p.T("x1_f", write=True), c * 512, 4, 0, 1024),
                         in_=rr(acc[:], "p (t d) -> p t d", t=4)).then_inc(p.sem("b_sA"), 16)
            sy.wait_ge(p.sem("b_te"), bTE + 4 * (c + 1))
            sy.dma_start(out=rr(p.T("xn1T_b", write=True), "(dt pp) s -> pp dt s", pp=128)[:, :, c * 512:(c + 1) * 512],
                         in_=rr(xn1T[:], "p (dt s) -> p dt s", s=512)).then_inc(p.sem("b_sA"), 16)
            sy.wait_ge(p.sem("b_sA"), bSA + 16 * 2 * (c + 1))
            sy.sem_inc(p.sem("b_ck"), 1)
    p.semv["b_mo"] = bMO + 16 * NCH
    p.semv["b_xn"] = bXN + 16 * NCH
    p.semv["b_xl0"] = bXL[0] + 16 * 2 * NCH
    p.semv["b_xl1"] = bXL[1] + 16 * 2 * NCH
    p.semv["b_yl0"] = bYL[0] + 16 * 2 * NCH
    p.semv["b_yl1"] = bYL[1] + 16 * 2 * NCH
    p.semv["b_sA"] = bSA + 16 * 2 * NCH
    p.semv["b_ck"] = bCK + NCH

    @block.vector
    def _(v):
        for c in range(NCH):
            v.wait_ge(p.sem("b_mo"), bMO + 16 * (c + 1))
            v.wait_ge(p.sem("b_xn"), bXN + 16 * (c + 1))
            for t in range(4):
                dst = acc[:, t * 1024:(t + 1) * 1024]
                ins = v.tensor_add(dst, dst, xn_tm[:, t * 1024:(t + 1) * 1024])
            p.inc(ins, "b_vs", 1)
            v.wait_ge(p.sem("b_vs"), p.val("b_vs"))
            for t in range(4):
                v.bn_stats(st6[:, t * 12:t * 12 + 6], acc[:, t * 1024:t * 1024 + 512])
                ins = v.bn_stats(st6[:, t * 12 + 6:t * 12 + 12], acc[:, t * 1024 + 512:t * 1024 + 1024])
            p.inc(ins, "b_vs", 1)
            v.wait_ge(p.sem("b_vs"), p.val("b_vs"))
            for t in range(4):
                ins = v.bn_aggr(mv[:, 2 * t:2 * t + 2], st6[:, t * 12:t * 12 + 12])
            ins.then_inc(p.sem("b_v"), 1)
            v.wait_ge(p.sem("b_v"), V(c, 1))
            for t in range(4):
                ins = v.tensor_single_scalar(acc[:, t * 1024:(t + 1) * 1024],
                                             acc[:, t * 1024:(t + 1) * 1024],
                                             mv[:, 2 * t:2 * t + 1], ALU.subtract)
            ins.then_inc(p.sem("b_v"), 1)
            v.wait_ge(p.sem("b_g"), bG + c + 1)
            v.wait_ge(p.sem("b_s"), bS + c * 2 + 1)
            for t in range(4):
                tn = c * 4 + t
                sl = t % 2
                dst = acc[:, t * 1024:(t + 1) * 1024]
                ins = v.scalar_tensor_tensor(dst, dst, rstdF[:, t:t + 1],
                                             ctx["bFbc"][:], ALU.mult, ALU.add)
                p.inc(ins, "b_vs", 1)
                v.wait_ge(p.sem("b_vs"), p.val("b_vs"))
                v.wait_ge(p.sem("b_xl%d" % sl), bXL[sl] + 16 * (tn // 2 + 1))
                v.wait_ge(p.sem("b_yl%d" % sl), bYL[sl] + 16 * (tn // 2 + 1))
                ins = v.tensor_add(dst, dst, x2[:, sl * 1024:(sl + 1) * 1024])
                p.inc(ins, "b_vs", 1)
                v.wait_ge(p.sem("b_vs"), p.val("b_vs"))
                ins = v.tensor_add(dst, dst, xn_tm[:, t * 1024:(t + 1) * 1024])
                p.inc(ins, "b_vs", 1)
                v.wait_ge(p.sem("b_vs"), p.val("b_vs"))
                ins = v.tensor_add(dst, dst, yW2[:, sl * 1024:(sl + 1) * 1024])
                ins.then_inc(p.sem("b_cb"), 1)
            v.wait_ge(p.sem("b_cb"), bCB + 4 * (c + 1))
            for t in range(4):
                v.bn_stats(st6b[:, t * 12:t * 12 + 6], acc[:, t * 1024:t * 1024 + 512])
                ins = v.bn_stats(st6b[:, t * 12 + 6:t * 12 + 12], acc[:, t * 1024 + 512:t * 1024 + 1024])
            p.inc(ins, "b_vs", 1)
            v.wait_ge(p.sem("b_vs"), p.val("b_vs"))
            for t in range(4):
                ins = v.bn_aggr(mvb[:, 2 * t:2 * t + 2], st6b[:, t * 12:t * 12 + 12])
            ins.then_inc(p.sem("b_v"), 1)
            v.wait_ge(p.sem("b_v"), V(c, 3))
            v.wait_ge(p.sem("b_s"), bS + c * 2 + 2)
            for t in range(4):
                ins = v.tensor_scalar(xn1bf[:, t * 1024:(t + 1) * 1024], acc[:, t * 1024:(t + 1) * 1024],
                                      mvb[:, 2 * t:2 * t + 1], rstd1[:, t:t + 1], ALU.subtract, ALU.mult)
            ins.then_inc(p.sem("b_v"), 1)
            for t in range(4):
                g = c * 4 + t
                v.wait_ge(p.sem("b_t"), bT + g + 1)
                v.tensor_copy(
                    rr(xn1T[:], "p (dt s) -> p dt s", s=512)[:, :, t * 128:(t + 1) * 128],
                    rr(ps_tr[:, (g % 2) * 1024:(g % 2 + 1) * 1024], "p (dt s) -> p dt s", s=128)
                ).then_inc(p.sem("b_te"), 1)
    p.semv["b_v"] = bV + NCH * NV
    p.semv["b_cb"] = bCB + NCH * 4
    p.semv["b_te"] = bTE + NCH * 4

    @block.scalar
    def _(s):
        for c in range(NCH):
            s.wait_ge(p.sem("b_v"), V(c, 1))
            for t in range(4):
                ins = s.activation(rstdF[:, t:t + 1], mv[:, 2 * t + 1:2 * t + 2], AF.Ln, bias=EPS)
            p.inc(ins, "b_sc", 1)
            s.wait_ge(p.sem("b_sc"), p.val("b_sc"))
            for t in range(4):
                ins = s.activation(rstdF[:, t:t + 1], rstdF[:, t:t + 1], AF.Exp, scale=-0.5)
            ins.then_inc(p.sem("b_s"), 1)
            s.wait_ge(p.sem("b_v"), V(c, 3))
            for t in range(4):
                ins = s.activation(rstd1[:, t:t + 1], mvb[:, 2 * t + 1:2 * t + 2], AF.Ln, bias=EPS)
            p.inc(ins, "b_sc", 1)
            s.wait_ge(p.sem("b_sc"), p.val("b_sc"))
            for t in range(4):
                ins = s.activation(rstd1[:, t:t + 1], rstd1[:, t:t + 1], AF.Exp, scale=-0.5)
            ins.then_inc(p.sem("b_s"), 1)
    p.semv["b_s"] = bS + NCH * 2

    @block.gpsimd
    def _(g_):
        for c in range(NCH):
            g_.wait_ge(p.sem("b_v"), V(c, 2))
            for t in range(4):
                ins = g_.tensor_mul(acc[:, t * 1024:(t + 1) * 1024], acc[:, t * 1024:(t + 1) * 1024],
                                    ctx["gFbc"][:])
            ins.then_inc(p.sem("b_g"), 1)
    p.semv["b_g"] = bG + NCH

    @block.tensor
    def _(t_):
        for c in range(NCH):
            for t in range(4):
                g = c * 4 + t
                t_.wait_ge(p.sem("b_v"), V(c, 4))
                if g >= 2:
                    t_.wait_ge(p.sem("b_te"), bTE + g - 1)
                for dt in range(8):
                    ins = t_.transpose(ps_tr[:, (g % 2) * 1024 + dt * 128:(g % 2) * 1024 + (dt + 1) * 128],
                                       xn1bf[:, t * 1024 + dt * 128:t * 1024 + (dt + 1) * 128],
                                       ctx["eyeb"][:])
                    if dt == 7:
                        ins.then_inc(p.sem("b_t"), 1)
    p.semv["b_t"] = bT + NCH * 4
    st.close()
    barrier(p, block)


def emit_p6(p, block, ctx):
    st = contextlib.ExitStack()
    xnT = p.sb("p6_xnT", [128, 8 * 512], BF16, st)
    W1b_sb = p.sb("p6_W1", [128, 8 * 2048], BF16, st)
    W2b_sb = p.sb("p6_W2", [128, 16 * 1024], BF16, st)
    heT = p.sb("p6_heT", [128, 16 * 512], BF16, st)
    acc = p.sb("p6_acc", [128, 4 * 1024], F32, st)
    xn_tm = p.sb("p6_xntm", [128, 4 * 1024], F32, st)
    x2 = p.sb("p6_x2", [128, 2 * 1024], F32, st)
    yW2 = p.sb("p6_yW2", [128, 2 * 1024], F32, st)
    gf_sb = p.sb("p6_gf", [128, 32], F32, st)
    gT_sb = p.sb("p6_gT", [8, 512], BF16, st)
    b2b_sb = p.sb("p6_b2b", [8, 1024], BF16, st)
    st6 = p.sb("p6_st6", [128, 4 * 12], F32, st)
    mv = p.sb("p6_mv", [128, 8], F32, st)
    rstdF = p.sb("p6_rstdF", [128, 4], F32, st)
    st6b = p.sb("p6_st6b", [128, 4 * 12], F32, st)
    mvb = p.sb("p6_mvb", [128, 8], F32, st)
    rstd1 = p.sb("p6_rstd1", [128, 4], F32, st)
    xn1bf = p.sb("p6_xn1bf", [128, 4096], BF16, st)
    xn1T = p.sb("p6_xn1T", [128, 8 * 512], BF16, st)
    ps_h = p.ps("p6_psh", [128, 2 * 512], F32, st)
    ps_o = p.ps("p6_pso", [128, 2 * 512], F32, st)
    ps_b2 = p.ps("p6_psb2", [128, 2 * 512], F32, st)
    ps_tr = p.ps("p6_pstr", [128, 2 * 1024], BF16, st)

    bXT = p.val("p6xt"); bXN = p.val("p6xn"); bGF = p.val("p6gf"); bGT = p.val("p6gt")
    bW1 = p.val("p6w1"); bW2 = p.val("p6w2"); bB2L = p.val("p6b2l")
    bM1 = p.val("p6m1"); bE1 = p.val("p6e1"); bM2 = p.val("p6m2"); bAE = p.val("p6ae")
    bB2 = p.val("p6b2"); bCB1 = p.val("p6cb1"); bSS = p.val("p6ss"); bS = p.val("p6s")
    bG = p.val("p6g"); bV = p.val("p6v"); bCB = p.val("p6cb"); bT = p.val("p6t")
    bTE = p.val("p6te"); bSA = p.val("p6sA"); bCK = p.val("p6ck")
    bXL = [p.val("p6xl0"), p.val("p6xl1")]
    bYL = [p.val("p6yl0"), p.val("p6yl1")]
    NV = 4

    def V(c, k):
        return bV + c * NV + k

    @block.sync
    def _(sy):
        sy.dma_start(out=b2b_sb[:], in_=p.T("b2b")).then_inc(p.sem("p6b2l"), 16)
        for c in range(NCH):
            if c >= 1:
                sy.wait_ge(p.sem("p6ck"), bCK + c)
            sy.dma_start(out=rr(xnT[:], "p (dt s) -> p dt s", s=512),
                         in_=rr(p.T("xnT_b"), "(dt pp) s -> pp dt s", pp=128)[:, :, c * 512:(c + 1) * 512]
                         ).then_inc(p.sem("p6xt"), 16)
            sy.dma_start(out=rr(xn_tm[:], "p (t d) -> p t d", t=4),
                         in_=pt(p.T("xn_f"), c * 512, 4, 0, 1024)).then_inc(p.sem("p6xn"), 16)
            sy.dma_start(out=rr(gf_sb[:], "p (t e) -> p t e", e=8),
                         in_=pt(p.T("gates_f"), c * 512, 4, 0, 8)).then_inc(p.sem("p6gf"), 16)
            sy.dma_start(out=gT_sb[:], in_=p.T("gatesT_b")[0:8, c * 512:(c + 1) * 512]
                         ).then_inc(p.sem("p6gt"), 16)
            for e in range(NE):
                en = c * 8 + e
                if en >= 1:
                    sy.wait_ge(p.sem("p6m1"), bM1 + 16 * en)
                sy.dma_start(out=rr(W1b_sb[:], "p (kt n) -> p kt n", kt=8),
                             in_=pt(rr(p.T("W1b"), "e r c -> (e r) c"), e * 1024, 8, 0, 2048)
                             ).then_inc(p.sem("p6w1"), 16)
                if en >= 1:
                    sy.wait_ge(p.sem("p6m2"), bM2 + 8 * en)
                sy.dma_start(out=rr(W2b_sb[:], "p (kt n) -> p kt n", kt=16),
                             in_=pt(rr(p.T("W2b"), "e r c -> (e r) c"), e * 2048, 16, 0, 1024)
                             ).then_inc(p.sem("p6w2"), 16)
            for t in range(4):
                tn = c * 4 + t
                sl = t % 2
                if tn >= 2:
                    sy.wait_ge(p.sem("p6cb"), bCB + tn - 1)
                sy.dma_start(out=x2[:, sl * 1024:(sl + 1) * 1024],
                             in_=pt(p.T("x"), c * 512 + t * 128, 1, 0, 1024)[:, 0, :]
                             ).then_inc(p.sem("p6xl%d" % sl), 16)
                sy.dma_start(out=yW2[:, sl * 1024:(sl + 1) * 1024],
                             in_=pt(p.T("yW_f"), c * 512 + t * 128, 1, 0, 1024)[:, 0, :]
                             ).then_inc(p.sem("p6yl%d" % sl), 16)
            sy.wait_ge(p.sem("p6cb"), bCB + 4 * (c + 1))
            sy.dma_start(out=pt(p.T("x1_f", write=True), c * 512, 4, 0, 1024),
                         in_=rr(acc[:], "p (t d) -> p t d", t=4)).then_inc(p.sem("p6sA"), 16)
            sy.wait_ge(p.sem("p6te"), bTE + 4 * (c + 1))
            sy.dma_start(out=rr(p.T("xn1T_b", write=True), "(dt pp) s -> pp dt s", pp=128)[:, :, c * 512:(c + 1) * 512],
                         in_=rr(xn1T[:], "p (dt s) -> p dt s", s=512)).then_inc(p.sem("p6sA"), 16)
            sy.wait_ge(p.sem("p6sA"), bSA + 16 * 2 * (c + 1))
            sy.sem_inc(p.sem("p6ck"), 1)
    p.semv["p6b2l"] = bB2L + 16
    p.semv["p6xt"] = bXT + 16 * NCH
    p.semv["p6xn"] = bXN + 16 * NCH
    p.semv["p6gf"] = bGF + 16 * NCH
    p.semv["p6gt"] = bGT + 16 * NCH
    p.semv["p6w1"] = bW1 + 16 * 8 * NCH
    p.semv["p6w2"] = bW2 + 16 * 8 * NCH
    p.semv["p6xl0"] = bXL[0] + 16 * 2 * NCH
    p.semv["p6xl1"] = bXL[1] + 16 * 2 * NCH
    p.semv["p6yl0"] = bYL[0] + 16 * 2 * NCH
    p.semv["p6yl1"] = bYL[1] + 16 * 2 * NCH
    p.semv["p6sA"] = bSA + 16 * 2 * NCH
    p.semv["p6ck"] = bCK + NCH

    @block.tensor
    def _(t_):
        for c in range(NCH):
            t_.wait_ge(p.sem("p6xt"), bXT + 16 * (c + 1))
            for e in range(NE):
                en = c * 8 + e
                t_.wait_ge(p.sem("p6w1"), bW1 + 16 * (en + 1))
                for och in range(16):
                    on = en * 16 + och
                    if on >= 2:
                        t_.wait_ge(p.sem("p6e1"), bE1 + on - 1)
                    for kt in range(8):
                        ins = t_.matmul(ps_h[:, (on % 2) * 512:(on % 2 + 1) * 512],
                                        W1b_sb[:, kt * 2048 + och * 128:kt * 2048 + (och + 1) * 128],
                                        xnT[:, kt * 512:(kt + 1) * 512],
                                        start=(kt == 0), stop=(kt == 7))
                    ins.then_inc(p.sem("p6m1"), 1)
                t_.wait_ge(p.sem("p6w2"), bW2 + 16 * (en + 1))
                t_.wait_ge(p.sem("p6e1"), bE1 + (en + 1) * 16)
                for t in range(4):
                    for half in range(2):
                        mn = en * 8 + t * 2 + half
                        if mn >= 2:
                            t_.wait_ge(p.sem("p6ae"), bAE + mn - 1)
                        for kt in range(16):
                            ins = t_.matmul(ps_o[:, (mn % 2) * 512:(mn % 2 + 1) * 512],
                                            heT[:, kt * 512 + t * 128:kt * 512 + (t + 1) * 128],
                                            W2b_sb[:, kt * 1024 + half * 512:kt * 1024 + (half + 1) * 512],
                                            start=(kt == 0), stop=(kt == 15))
                        ins.then_inc(p.sem("p6m2"), 1)
            t_.wait_ge(p.sem("p6gt"), bGT + 16 * (c + 1))
            t_.wait_ge(p.sem("p6b2l"), bB2L + 16)
            for t in range(4):
                for half in range(2):
                    bn = c * 8 + t * 2 + half
                    if bn >= 2:
                        t_.wait_ge(p.sem("p6cb1"), bCB1 + bn - 1)
                    t_.matmul(ps_b2[:, (bn % 2) * 512:(bn % 2 + 1) * 512],
                              gT_sb[0:8, t * 128:(t + 1) * 128],
                              b2b_sb[0:8, half * 512:(half + 1) * 512],
                              start=True, stop=True).then_inc(p.sem("p6b2"), 1)
            for t in range(4):
                g = c * 4 + t
                t_.wait_ge(p.sem("p6v"), V(c, 4))
                if g >= 2:
                    t_.wait_ge(p.sem("p6te"), bTE + g - 1)
                for dt in range(8):
                    ins = t_.transpose(ps_tr[:, (g % 2) * 1024 + dt * 128:(g % 2) * 1024 + (dt + 1) * 128],
                                       xn1bf[:, t * 1024 + dt * 128:t * 1024 + (dt + 1) * 128],
                                       ctx["eyeb"][:])
                    if dt == 7:
                        ins.then_inc(p.sem("p6t"), 1)
    p.semv["p6m1"] = bM1 + NCH * 128
    p.semv["p6m2"] = bM2 + NCH * 64
    p.semv["p6b2"] = bB2 + NCH * 8
    p.semv["p6t"] = bT + NCH * 4

    @block.scalar
    def _(s):
        for c in range(NCH):
            for e in range(NE):
                en = c * 8 + e
                for och in range(16):
                    on = en * 16 + och
                    s.wait_ge(p.sem("p6m1"), bM1 + on + 1)
                    s.activation(heT[:, och * 512:(och + 1) * 512],
                                 ps_h[:, (on % 2) * 512:(on % 2 + 1) * 512], AF.Relu,
                                 bias=ctx["b1"][:, e * 16 + och:e * 16 + och + 1]
                                 ).then_inc(p.sem("p6e1"), 1)
            s.wait_ge(p.sem("p6v"), V(c, 1))
            for t in range(4):
                ins = s.activation(rstdF[:, t:t + 1], mv[:, 2 * t + 1:2 * t + 2], AF.Ln, bias=EPS)
            p.inc(ins, "p6sc", 1)
            s.wait_ge(p.sem("p6sc"), p.val("p6sc"))
            for t in range(4):
                ins = s.activation(rstdF[:, t:t + 1], rstdF[:, t:t + 1], AF.Exp, scale=-0.5)
            ins.then_inc(p.sem("p6s"), 1)
            s.wait_ge(p.sem("p6v"), V(c, 3))
            for t in range(4):
                ins = s.activation(rstd1[:, t:t + 1], mvb[:, 2 * t + 1:2 * t + 2], AF.Ln, bias=EPS)
            p.inc(ins, "p6sc", 1)
            s.wait_ge(p.sem("p6sc"), p.val("p6sc"))
            for t in range(4):
                ins = s.activation(rstd1[:, t:t + 1], rstd1[:, t:t + 1], AF.Exp, scale=-0.5)
            ins.then_inc(p.sem("p6s"), 1)
    p.semv["p6e1"] = bE1 + NCH * 128
    p.semv["p6s"] = bS + NCH * 2

    @block.gpsimd
    def _(g_):
        for c in range(NCH):
            g_.wait_ge(p.sem("p6v"), V(c, 2))
            for t in range(4):
                ins = g_.tensor_mul(acc[:, t * 1024:(t + 1) * 1024], acc[:, t * 1024:(t + 1) * 1024],
                                    ctx["gFbc"][:])
            ins.then_inc(p.sem("p6g"), 1)
    p.semv["p6g"] = bG + NCH

    @block.vector
    def _(v):
        for c in range(NCH):
            v.wait_ge(p.sem("p6gf"), bGF + 16 * (c + 1))
            if c >= 1:
                v.wait_ge(p.sem("p6ck"), bCK + c)
            for e in range(NE):
                en = c * 8 + e
                for t in range(4):
                    for half in range(2):
                        mn = en * 8 + t * 2 + half
                        v.wait_ge(p.sem("p6m2"), bM2 + mn + 1)
                        dst = acc[:, t * 1024 + half * 512:t * 1024 + (half + 1) * 512]
                        gcol = gf_sb[:, t * 8 + e:t * 8 + e + 1]
                        src = ps_o[:, (mn % 2) * 512:(mn % 2 + 1) * 512]
                        if e == 0:
                            ins = v.tensor_scalar_mul(dst, src, gcol)
                        else:
                            ins = v.scalar_tensor_tensor(dst, src, gcol, dst, ALU.mult, ALU.add)
                        ins.then_inc(p.sem("p6ae"), 1)
            v.wait_ge(p.sem("p6xn"), bXN + 16 * (c + 1))
            for t in range(4):
                for half in range(2):
                    bn = c * 8 + t * 2 + half
                    v.wait_ge(p.sem("p6b2"), bB2 + bn + 1)
                    dst = acc[:, t * 1024 + half * 512:t * 1024 + (half + 1) * 512]
                    ins = v.tensor_add(dst, dst, ps_b2[:, (bn % 2) * 512:(bn % 2 + 1) * 512])
                    p.inc(ins, "p6vs", 1)
                    v.wait_ge(p.sem("p6vs"), p.val("p6vs"))
                    ins = v.tensor_add(dst, dst, xn_tm[:, t * 1024 + half * 512:t * 1024 + (half + 1) * 512])
                    ins.then_inc(p.sem("p6cb1"), 1)
            v.wait_ge(p.sem("p6cb1"), bCB1 + 8 * (c + 1))
            for t in range(4):
                v.bn_stats(st6[:, t * 12:t * 12 + 6], acc[:, t * 1024:t * 1024 + 512])
                ins = v.bn_stats(st6[:, t * 12 + 6:t * 12 + 12], acc[:, t * 1024 + 512:t * 1024 + 1024])
            p.inc(ins, "p6vs", 1)
            v.wait_ge(p.sem("p6vs"), p.val("p6vs"))
            for t in range(4):
                ins = v.bn_aggr(mv[:, 2 * t:2 * t + 2], st6[:, t * 12:t * 12 + 12])
            ins.then_inc(p.sem("p6v"), 1)
            v.wait_ge(p.sem("p6v"), V(c, 1))
            for t in range(4):
                ins = v.tensor_single_scalar(acc[:, t * 1024:(t + 1) * 1024],
                                             acc[:, t * 1024:(t + 1) * 1024],
                                             mv[:, 2 * t:2 * t + 1], ALU.subtract)
            ins.then_inc(p.sem("p6v"), 1)
            v.wait_ge(p.sem("p6g"), bG + c + 1)
            v.wait_ge(p.sem("p6s"), bS + c * 2 + 1)
            for t in range(4):
                ins = v.scalar_tensor_tensor(acc[:, t * 1024:(t + 1) * 1024],
                                             acc[:, t * 1024:(t + 1) * 1024], rstdF[:, t:t + 1],
                                             ctx["bFbc"][:], ALU.mult, ALU.add)
            p.inc(ins, "p6vs", 1)
            v.wait_ge(p.sem("p6vs"), p.val("p6vs"))
            for t in range(4):
                tn = c * 4 + t
                sl = t % 2
                v.wait_ge(p.sem("p6xl%d" % sl), bXL[sl] + 16 * (tn // 2 + 1))
                v.wait_ge(p.sem("p6yl%d" % sl), bYL[sl] + 16 * (tn // 2 + 1))
                dst = acc[:, t * 1024:(t + 1) * 1024]
                ins = v.tensor_add(dst, dst, x2[:, sl * 1024:(sl + 1) * 1024])
                p.inc(ins, "p6vs", 1)
                v.wait_ge(p.sem("p6vs"), p.val("p6vs"))
                ins = v.tensor_add(dst, dst, xn_tm[:, t * 1024:(t + 1) * 1024])
                p.inc(ins, "p6vs", 1)
                v.wait_ge(p.sem("p6vs"), p.val("p6vs"))
                ins = v.tensor_add(dst, dst, yW2[:, sl * 1024:(sl + 1) * 1024])
                ins.then_inc(p.sem("p6cb"), 1)
            v.wait_ge(p.sem("p6cb"), bCB + 4 * (c + 1))
            for t in range(4):
                v.bn_stats(st6b[:, t * 12:t * 12 + 6], acc[:, t * 1024:t * 1024 + 512])
                ins = v.bn_stats(st6b[:, t * 12 + 6:t * 12 + 12], acc[:, t * 1024 + 512:t * 1024 + 1024])
            p.inc(ins, "p6vs", 1)
            v.wait_ge(p.sem("p6vs"), p.val("p6vs"))
            for t in range(4):
                ins = v.bn_aggr(mvb[:, 2 * t:2 * t + 2], st6b[:, t * 12:t * 12 + 12])
            ins.then_inc(p.sem("p6v"), 1)
            v.wait_ge(p.sem("p6v"), V(c, 3))
            v.wait_ge(p.sem("p6s"), bS + c * 2 + 2)
            for t in range(4):
                ins = v.tensor_scalar(xn1bf[:, t * 1024:(t + 1) * 1024], acc[:, t * 1024:(t + 1) * 1024],
                                      mvb[:, 2 * t:2 * t + 1], rstd1[:, t:t + 1], ALU.subtract, ALU.mult)
            ins.then_inc(p.sem("p6v"), 1)
            for t in range(4):
                g = c * 4 + t
                v.wait_ge(p.sem("p6t"), bT + g + 1)
                v.tensor_copy(
                    rr(xn1T[:], "p (dt s) -> p dt s", s=512)[:, :, t * 128:(t + 1) * 128],
                    rr(ps_tr[:, (g % 2) * 1024:(g % 2 + 1) * 1024], "p (dt s) -> p dt s", s=128)
                ).then_inc(p.sem("p6te"), 1)
    p.semv["p6ae"] = bAE + NCH * 64
    p.semv["p6cb1"] = bCB1 + NCH * 8
    p.semv["p6v"] = bV + NCH * NV
    p.semv["p6cb"] = bCB + NCH * 4
    p.semv["p6te"] = bTE + NCH * 4
    st.close()
    barrier(p, block)


def emit_p7(p, block, ctx):
    st = contextlib.ExitStack()
    xn1T = p.sb("p7_xn1T", [128, 8 * 512], BF16, st)
    WqT_sb = p.sb("p7_WqT", [128, 8 * 1024], BF16, st)
    WkT_sb = p.sb("p7_WkT", [128, 8 * 1024], BF16, st)
    WvT_sb = p.sb("p7_WvT", [128, 8 * 1024], BF16, st)
    WoT_sb = p.sb("p7_WoT", [128, 8 * 1024], BF16, st)
    qT_c = p.sb("p7_qT", [128, 8 * 512], BF16, st)
    kT_c = p.sb("p7_kT", [128, 8 * 512], BF16, st)
    v_c = p.sb("p7_v", [128, 4 * 1024], BF16, st)
    yT_c = p.sb("p7_yT", [128, 8 * 512], BF16, st)
    Sm = p.sb("p7_Sm", [128, 256], F32, st)
    P_sb = p.sb("p7_P", [128, 256], BF16, st)
    PT_sb = p.sb("p7_PT", [128, 256], BF16, st)
    nmx = p.sb("p7_nmx", [128, 1], F32, st)
    rsum = p.sb("p7_rsum", [128, 1], F32, st)
    rinv = p.sb("p7_rinv", [128, 1], F32, st)
    x12 = p.sb("p7_x12", [128, 2 * 1024], F32, st)
    part2 = p.sb("p7_part2", [128, 2 * 1024], F32, st)
    ps_proj = p.ps("p7_psproj", [128, 2 * 512], F32, st)
    ps_S = p.ps("p7_psS", [128, 1024], F32, st)
    ps_PT = p.ps("p7_psPT", [128, 2 * 1024], BF16, st)
    ps_y = p.ps("p7_psy", [64, 2 * 512], F32, st)
    mask = p.sb("p7_mask", [128, 3 * 128], F32, st)
    kprev = p.sb("p7_kprev", [128, 8 * 128], BF16, st)
    vprev = p.sb("p7_vprev", [128, D], BF16, st)

    bWL = p.val("p7wl"); bX = p.val("p7x"); bPM = p.val("p7pm"); bPE = p.val("p7pe")
    bS7 = p.val("p7s"); bSM = p.val("p7sm"); bVC = p.val("p7vc"); bSC = p.val("p7sc")
    bPT = p.val("p7pt"); bPTE = p.val("p7pte"); bYM = p.val("p7ym"); bYE = p.val("p7ye")
    bWM = p.val("p7wm"); bWE = p.val("p7we"); bKV = p.val("p7kv"); bCK = p.val("p7ck")
    bSA0 = p.val("p7sA0"); bSA1 = p.val("p7sA1")
    bXL = [p.val("p7xl0"), p.val("p7xl1")]

    @block.sync
    def _(sy):
        for i, nm in enumerate(["WqT", "WkT", "WvT", "WoT"]):
            dst = [WqT_sb, WkT_sb, WvT_sb, WoT_sb][i]
            sy.dma_start(out=rr(dst[:], "p (kt n) -> p kt n", kt=8),
                         in_=pt(p.T(nm), 0, 8, 0, 1024)).then_inc(p.sem("p7wl"), 16)
        sy.dma_start(out=mask[:, 0:128], in_=p.T("mask_prev")).then_inc(p.sem("p7wl"), 16)
        sy.dma_start(out=mask[:, 128:256], in_=p.T("mask_cur")).then_inc(p.sem("p7wl"), 16)
        for c in range(NCH):
            if c >= 1:
                sy.wait_ge(p.sem("p7ck"), bCK + c)
            sy.dma_start(out=rr(xn1T[:], "p (dt s) -> p dt s", s=512),
                         in_=rr(p.T("xn1T_b"), "(dt pp) s -> pp dt s", pp=128)[:, :, c * 512:(c + 1) * 512]
                         ).then_inc(p.sem("p7x"), 16)
            for t in range(4):
                tn = c * 4 + t
                sl = t % 2
                if tn >= 2:
                    sy.wait_ge(p.sem("p7we"), bWE + 2 * (tn - 1))
                sy.dma_start(out=x12[:, sl * 1024:(sl + 1) * 1024],
                             in_=pt(p.T("x1_f"), c * 512 + t * 128, 1, 0, 1024)[:, 0, :]
                             ).then_inc(p.sem("p7xl%d" % sl), 16)
            for t in range(4):
                tn = c * 4 + t
                sl = t % 2
                sy.wait_ge(p.sem("p7we"), bWE + 8 * c + 2 * t + 2)
                sy.dma_start(out=pt(p.T("partial_f", write=True), c * 512 + t * 128, 1, 0, 1024)[:, 0, :],
                             in_=part2[:, sl * 1024:(sl + 1) * 1024]).then_inc(p.sem("p7sA%d" % sl), 16)
            sy.wait_ge(p.sem("p7sA0"), bSA0 + 16 * 2 * (c + 1))
            sy.wait_ge(p.sem("p7sA1"), bSA1 + 16 * 2 * (c + 1))
            sy.wait_ge(p.sem("p7kv"), bKV + (c + 1))
            sy.sem_inc(p.sem("p7ck"), 1)
    p.semv["p7wl"] = bWL + 96
    p.semv["p7x"] = bX + 16 * NCH
    p.semv["p7xl0"] = bXL[0] + 16 * 2 * NCH
    p.semv["p7xl1"] = bXL[1] + 16 * 2 * NCH
    p.semv["p7sA0"] = bSA0 + 16 * 2 * NCH
    p.semv["p7sA1"] = bSA1 + 16 * 2 * NCH
    p.semv["p7ck"] = bCK + NCH

    @block.gpsimd
    def _(g_):
        g_.memset(mask[:, 256:384], NEG)
        g_.memset(kprev[:], 0.0)
        g_.memset(vprev[:], 0.0).then_inc(p.sem("p7mi"), 1)
    p.semv["p7mi"] = p.val("p7mi") + 1 - 1 + 1

    @block.tensor
    def _(t_):
        t_.wait_ge(p.sem("p7wl"), bWL + 96)
        t_.wait_ge(p.sem("p7mi"), p.val("p7mi"))
        for c in range(NCH):
            t_.wait_ge(p.sem("p7x"), bX + 16 * (c + 1))
            if c >= 1:
                t_.wait_ge(p.sem("p7kv"), bKV + c)
                t_.wait_ge(p.sem("p7we"), bWE + 8 * c)
            pj = 0
            for wsb, dstname in [(WqT_sb, "q"), (WkT_sb, "k")]:
                for och in range(8):
                    pn = c * 24 + pj
                    if pn >= 2:
                        t_.wait_ge(p.sem("p7pe"), bPE + pn - 1)
                    for kt in range(8):
                        ins = t_.matmul(ps_proj[:, (pn % 2) * 512:(pn % 2 + 1) * 512],
                                        wsb[:, kt * 1024 + och * 128:kt * 1024 + (och + 1) * 128],
                                        xn1T[:, kt * 512:(kt + 1) * 512],
                                        start=(kt == 0), stop=(kt == 7))
                    ins.then_inc(p.sem("p7pm"), 1)
                    pj += 1
            for t in range(4):
                for half in range(2):
                    pn = c * 24 + pj
                    if pn >= 2:
                        t_.wait_ge(p.sem("p7pe"), bPE + pn - 1)
                    for kt in range(8):
                        ins = t_.matmul(ps_proj[:, (pn % 2) * 512:(pn % 2 + 1) * 512],
                                        xn1T[:, kt * 512 + t * 128:kt * 512 + (t + 1) * 128],
                                        WvT_sb[:, kt * 1024 + half * 512:kt * 1024 + (half + 1) * 512],
                                        start=(kt == 0), stop=(kt == 7))
                    ins.then_inc(p.sem("p7pm"), 1)
                    pj += 1
            t_.wait_ge(p.sem("p7pe"), bPE + c * 24 + 24)
            for b in range(4):
                for h in range(H):
                    an = (c * 4 + b) * H + h
                    ht, hp = h // 2, (h % 2) * 64
                    sslot = an % 2
                    if an >= 2:
                        t_.wait_ge(p.sem("p7sm"), bSM + 2 * (an - 1))
                    qsl = qT_c[hp:hp + 64, ht * 512 + b * 128:ht * 512 + (b + 1) * 128]
                    if b == 0:
                        kprev_sl = kprev[hp:hp + 64, ht * 128:(ht + 1) * 128]
                    else:
                        kprev_sl = kT_c[hp:hp + 64, ht * 512 + (b - 1) * 128:ht * 512 + b * 128]
                    kcur_sl = kT_c[hp:hp + 64, ht * 512 + b * 128:ht * 512 + (b + 1) * 128]
                    t_.matmul(ps_S[:, sslot * 512:sslot * 512 + 128], qsl, kprev_sl,
                              start=True, stop=True, tile_position=(hp, 0))
                    t_.matmul(ps_S[:, sslot * 512 + 128:sslot * 512 + 256], qsl, kcur_sl,
                              start=True, stop=True, tile_position=(hp, 0)).then_inc(p.sem("p7s"), 1)
                    t_.wait_ge(p.sem("p7vc"), bVC + an * 3 + 3)
                    if an >= 2:
                        t_.wait_ge(p.sem("p7pte"), bPTE + an - 1)
                    t_.transpose(ps_PT[:, sslot * 1024:sslot * 1024 + 128], P_sb[:, 0:128], ctx["eyeb"][:])
                    t_.transpose(ps_PT[:, sslot * 1024 + 128:sslot * 1024 + 256], P_sb[:, 128:256],
                                 ctx["eyeb"][:]).then_inc(p.sem("p7pt"), 1)
                    t_.wait_ge(p.sem("p7pte"), bPTE + an + 1)
                    if an >= 2:
                        t_.wait_ge(p.sem("p7ye"), bYE + an - 1)
                    if b == 0:
                        vprev_sl = vprev[:, h * 64:(h + 1) * 64]
                    else:
                        vprev_sl = v_c[:, (b - 1) * 1024 + h * 64:(b - 1) * 1024 + (h + 1) * 64]
                    vcur_sl = v_c[:, b * 1024 + h * 64:b * 1024 + (h + 1) * 64]
                    t_.matmul(ps_y[0:64, sslot * 512:sslot * 512 + 128], vprev_sl, PT_sb[:, 0:128],
                              start=True, stop=False)
                    t_.matmul(ps_y[0:64, sslot * 512:sslot * 512 + 128], vcur_sl, PT_sb[:, 128:256],
                              start=False, stop=True).then_inc(p.sem("p7ym"), 1)
            t_.wait_ge(p.sem("p7ye"), bYE + (c + 1) * 64)
            for t in range(4):
                for half in range(2):
                    wn = c * 8 + t * 2 + half
                    if wn >= 2:
                        t_.wait_ge(p.sem("p7we"), bWE + wn - 1)
                    for kt in range(8):
                        ins = t_.matmul(ps_proj[:, (wn % 2) * 512:(wn % 2 + 1) * 512],
                                        yT_c[:, kt * 512 + t * 128:kt * 512 + (t + 1) * 128],
                                        WoT_sb[:, kt * 1024 + half * 512:kt * 1024 + (half + 1) * 512],
                                        start=(kt == 0), stop=(kt == 7))
                    ins.then_inc(p.sem("p7wm"), 1)
    p.semv["p7pm"] = bPM + NCH * 24
    p.semv["p7s"] = bS7 + NCH * 64
    p.semv["p7pt"] = bPT + NCH * 64
    p.semv["p7ym"] = bYM + NCH * 64
    p.semv["p7wm"] = bWM + NCH * 8

    @block.scalar
    def _(s):
        for c in range(NCH):
            pj = 0
            for dst in [qT_c, kT_c]:
                for och in range(8):
                    pn = c * 24 + pj
                    s.wait_ge(p.sem("p7pm"), bPM + pn + 1)
                    s.copy(dst[:, och * 512:(och + 1) * 512],
                           ps_proj[:, (pn % 2) * 512:(pn % 2 + 1) * 512]).then_inc(p.sem("p7pe"), 1)
                    pj += 1
            for t in range(4):
                for half in range(2):
                    pn = c * 24 + pj
                    s.wait_ge(p.sem("p7pm"), bPM + pn + 1)
                    s.copy(v_c[:, t * 1024 + half * 512:t * 1024 + (half + 1) * 512],
                           ps_proj[:, (pn % 2) * 512:(pn % 2 + 1) * 512]).then_inc(p.sem("p7pe"), 1)
                    pj += 1
            for b in range(4):
                for h in range(H):
                    an = (c * 4 + b) * H + h
                    s.wait_ge(p.sem("p7vc"), bVC + an * 3 + 1)
                    if an >= 1:
                        s.wait_ge(p.sem("p7pt"), bPT + an)
                    s.activation(P_sb[:], Sm[:], AF.Exp, bias=nmx[:, 0:1],
                                 accum_out=rsum[:]).then_inc(p.sem("p7sc"), 1)
    p.semv["p7pe"] = bPE + NCH * 24
    p.semv["p7sc"] = bSC + NCH * 64

    @block.vector
    def _(v):
        v.wait_ge(p.sem("p7wl"), bWL + 96)
        v.wait_ge(p.sem("p7mi"), p.val("p7mi"))
        for c in range(NCH):
            for b in range(4):
                for h in range(H):
                    an = (c * 4 + b) * H + h
                    ht, hp = h // 2, (h % 2) * 64
                    sslot = an % 2
                    first = (c == 0 and b == 0)
                    v.wait_ge(p.sem("p7s"), bS7 + an + 1)
                    if an >= 1:
                        v.wait_ge(p.sem("p7sc"), bSC + an)
                    mprev = mask[:, 256:384] if first else mask[:, 0:128]
                    v.scalar_tensor_tensor(Sm[:, 0:128], ps_S[:, sslot * 512:sslot * 512 + 128],
                                           0.125, mprev, ALU.mult, ALU.add).then_inc(p.sem("p7sm"), 1)
                    ins = v.scalar_tensor_tensor(Sm[:, 128:256], ps_S[:, sslot * 512 + 128:sslot * 512 + 256],
                                                 0.125, mask[:, 128:256], ALU.mult, ALU.add)
                    ins.then_inc(p.sem("p7sm"), 1)
                    v.wait_ge(p.sem("p7sm"), bSM + 2 * an + 2)
                    v.reduce_max(nmx[:], Sm[:], axis=AX.X, negate=True).then_inc(p.sem("p7vc"), 1)
                    v.wait_ge(p.sem("p7sc"), bSC + an + 1)
                    v.reciprocal(rinv[:], rsum[:]).then_inc(p.sem("p7vc"), 1)
                    v.wait_ge(p.sem("p7vc"), bVC + an * 3 + 2)
                    v.tensor_scalar_mul(P_sb[:], P_sb[:], rinv[:, 0:1]).then_inc(p.sem("p7vc"), 1)
                    v.wait_ge(p.sem("p7pt"), bPT + an + 1)
                    v.tensor_copy(PT_sb[:, 0:128], ps_PT[:, sslot * 1024:sslot * 1024 + 128])
                    v.tensor_copy(PT_sb[:, 128:256], ps_PT[:, sslot * 1024 + 128:sslot * 1024 + 256]
                                  ).then_inc(p.sem("p7pte"), 1)
                    v.wait_ge(p.sem("p7ym"), bYM + an + 1)
                    v.tensor_copy(yT_c[hp:hp + 64, ht * 512 + b * 128:ht * 512 + (b + 1) * 128],
                                  ps_y[0:64, sslot * 512:sslot * 512 + 128]).then_inc(p.sem("p7ye"), 1)
            v.tensor_copy(rr(kprev[:], "p (ht s) -> p ht s", s=128),
                          rr(kT_c[:], "p (ht s) -> p ht s", s=512)[:, :, 384:512])
            v.tensor_copy(vprev[:], v_c[:, 3 * 1024:4 * 1024]).then_inc(p.sem("p7kv"), 1)
            for t in range(4):
                for half in range(2):
                    wn = c * 8 + t * 2 + half
                    v.wait_ge(p.sem("p7wm"), bWM + wn + 1)
                    tn = c * 4 + t
                    sl = t % 2
                    v.wait_ge(p.sem("p7xl%d" % sl), bXL[sl] + 16 * (tn // 2 + 1))
                    if tn >= 2:
                        v.wait_ge(p.sem("p7sA%d" % sl), [bSA0, bSA1][sl] + 16 * (tn // 2))
                    v.tensor_add(part2[:, sl * 1024 + half * 512:sl * 1024 + (half + 1) * 512],
                                 ps_proj[:, (wn % 2) * 512:(wn % 2 + 1) * 512],
                                 x12[:, sl * 1024 + half * 512:sl * 1024 + (half + 1) * 512]
                                 ).then_inc(p.sem("p7we"), 1)
    p.semv["p7sm"] = bSM + NCH * 128
    p.semv["p7vc"] = bVC + NCH * 192
    p.semv["p7pte"] = bPTE + NCH * 64
    p.semv["p7ye"] = bYE + NCH * 64
    p.semv["p7kv"] = bKV + NCH
    p.semv["p7we"] = bWE + NCH * 8
    st.close()
    barrier(p, block)


def emit_p8(p, block, ctx):
    st = contextlib.ExitStack()
    xfull = p.sb("p8_x", [128, 8 * 4096], BF16, st)
    w1b = p.sb("p8_w1", [128, 2 * 8 * 128], BF16, st)
    w2b = p.sb("p8_w2", [128, 2 * 8 * 128], BF16, st)
    s1b = p.sb("p8_s1", [128, 2 * 512], BF16, st)
    ab = p.sb("p8_ab", [128, 2 * 512], BF16, st)
    ps1 = p.ps("p8_ps1", [128, 2 * 512], F32, st)
    ps2 = p.ps("p8_ps2", [128, 2 * 512], F32, st)

    bX = p.val("p8x"); bW = [p.val("p8w0"), p.val("p8w1")]
    bM = p.val("p8m"); bE1 = p.val("p8e1"); bE2 = p.val("p8e2")
    bST = [p.val("p8st0"), p.val("p8st1")]

    @block.sync
    def _(sy):
        sy.dma_start(out=rr(xfull[:], "p (kt s) -> p kt s", kt=8),
                     in_=rr(p.T("xn1T_b"), "(kt pp) s -> pp kt s", pp=128)).then_inc(p.sem("p8x"), 16)
        for och in range(32):
            sl = och % 2
            if och >= 2:
                sy.wait_ge(p.sem("p8m"), bM + NCH * (och - 1))
            sy.dma_start(out=rr(w1b[:, sl * 1024:(sl + 1) * 1024], "p (kt n) -> p kt n", kt=8),
                         in_=rr(p.T("w1T"), "(kt pp) n -> pp kt n", pp=128)[:, :, och * 128:(och + 1) * 128]
                         ).then_inc(p.sem("p8w%d" % sl), 16)
            sy.dma_start(out=rr(w2b[:, sl * 1024:(sl + 1) * 1024], "p (kt n) -> p kt n", kt=8),
                         in_=rr(p.T("w2T"), "(kt pp) n -> pp kt n", pp=128)[:, :, och * 128:(och + 1) * 128]
                         ).then_inc(p.sem("p8w%d" % sl), 16)

    @block.gpsimd
    def _(g):
        for och in range(32):
            for tc in range(NCH):
                j = och * NCH + tc
                sl = j % 2
                g.wait_ge(p.sem("p8e2"), bE2 + j + 1)
                g.dma_start(out=rr(p.T("aT_b", write=True), "(oc pp) s -> pp oc s", pp=128
                                   )[:, och, tc * 512:(tc + 1) * 512],
                            in_=ab[:, sl * 512:(sl + 1) * 512]).then_inc(p.sem("p8st%d" % sl), 16)

    @block.tensor
    def _(t_):
        t_.wait_ge(p.sem("p8x"), bX + 16)
        for och in range(32):
            wsl = och % 2
            t_.wait_ge(p.sem("p8w%d" % wsl), bW[wsl] + 32 * ((och - wsl) // 2 + 1))
            for tc in range(NCH):
                j = och * NCH + tc
                psl = j % 2
                if j >= 2:
                    t_.wait_ge(p.sem("p8e1"), bE1 + j - 1)
                for kt in range(8):
                    ins = t_.matmul(ps1[:, psl * 512:(psl + 1) * 512],
                                    w1b[:, wsl * 1024 + kt * 128:wsl * 1024 + (kt + 1) * 128],
                                    xfull[:, kt * 4096 + tc * 512:kt * 4096 + (tc + 1) * 512],
                                    start=(kt == 0), stop=(kt == 7))
                if j >= 2:
                    t_.wait_ge(p.sem("p8e2"), bE2 + j - 1)
                for kt in range(8):
                    ins2 = t_.matmul(ps2[:, psl * 512:(psl + 1) * 512],
                                     w2b[:, wsl * 1024 + kt * 128:wsl * 1024 + (kt + 1) * 128],
                                     xfull[:, kt * 4096 + tc * 512:kt * 4096 + (tc + 1) * 512],
                                     start=(kt == 0), stop=(kt == 7))
                ins2.then_inc(p.sem("p8m"), 1)
    p.semv["p8x"] = bX + 16
    p.semv["p8m"] = bM + 32 * NCH
    p.semv["p8w0"] = bW[0] + 16 * 32
    p.semv["p8w1"] = bW[1] + 16 * 32

    @block.scalar
    def _(s):
        for och in range(32):
            for tc in range(NCH):
                j = och * NCH + tc
                psl = j % 2
                s.wait_ge(p.sem("p8m"), bM + j + 1)
                if j >= 2:
                    s.wait_ge(p.sem("p8e2"), bE2 + j - 1)
                s.activation(s1b[:, psl * 512:(psl + 1) * 512], ps1[:, psl * 512:(psl + 1) * 512],
                             AF.Silu, scale=ctx["w1s"][:, och:och + 1]).then_inc(p.sem("p8e1"), 1)
    p.semv["p8e1"] = bE1 + 32 * NCH

    @block.vector
    def _(v):
        for och in range(32):
            for tc in range(NCH):
                j = och * NCH + tc
                psl = j % 2
                v.wait_ge(p.sem("p8m"), bM + j + 1)
                v.wait_ge(p.sem("p8e1"), bE1 + j + 1)
                if j >= 2:
                    v.wait_ge(p.sem("p8st%d" % psl), bST[psl] + 16 * (j // 2))
                v.scalar_tensor_tensor(ab[:, psl * 512:(psl + 1) * 512],
                                       ps2[:, psl * 512:(psl + 1) * 512], ctx["w2s"][:, och:och + 1],
                                       s1b[:, psl * 512:(psl + 1) * 512], ALU.mult, ALU.mult
                                       ).then_inc(p.sem("p8e2"), 1)
    p.semv["p8e2"] = bE2 + 32 * NCH
    p.semv["p8st0"] = bST[0] + 16 * 16 * NCH
    p.semv["p8st1"] = bST[1] + 16 * 16 * NCH

    @block.sync
    def _(sy):
        sy.wait_ge(p.sem("p8st0"), bST[0] + 16 * 16 * NCH)
        sy.wait_ge(p.sem("p8st1"), bST[1] + 16 * 16 * NCH)
    st.close()
    barrier(p, block)


def emit_hpass(p, block, ctx, src_nm, w_nm, s_key, dst_nm, tag):
    st = contextlib.ExitStack()
    aq = p.sb(tag + "_aq", [128, 32 * 1024], BF16, st)
    wb = p.sb(tag + "_wb", [128, 2 * 32 * 128], BF16, st)
    bb = p.sb(tag + "_bb", [128, 2 * 512], BF16, st)
    ps = p.ps(tag + "_ps", [128, 2 * 512], F32, st)

    bA = p.val(tag + "a"); bW = [p.val(tag + "w0"), p.val(tag + "w1")]
    bM = p.val(tag + "m"); bE = p.val(tag + "e")
    bST = [p.val(tag + "st0"), p.val(tag + "st1")]
    bCK = p.val(tag + "ck")

    @block.sync
    def _(sy):
        for q in range(4):
            if q >= 1:
                sy.wait_ge(p.sem(tag + "ck"), bCK + q)
            sy.dma_start(out=rr(aq[:], "p (kt s) -> p kt s", kt=32),
                         in_=rr(p.T(src_nm), "(kt pp) s -> pp kt s", pp=128)[:, :, q * 1024:(q + 1) * 1024]
                         ).then_inc(p.sem(tag + "a"), 16)
            for och in range(32):
                n = q * 32 + och
                sl = n % 2
                if n >= 2:
                    sy.wait_ge(p.sem(tag + "m"), bM + 2 * (n - 1))
                sy.dma_start(out=rr(wb[:, sl * 4096:(sl + 1) * 4096], "p (kt n) -> p kt n", kt=32),
                             in_=rr(p.T(w_nm), "(kt pp) n -> pp kt n", pp=128)[:, :, och * 128:(och + 1) * 128]
                             ).then_inc(p.sem(tag + "w%d" % sl), 16)
            sy.wait_ge(p.sem(tag + "m"), bM + 2 * 32 * (q + 1))
            sy.sem_inc(p.sem(tag + "ck"), 1)
    p.semv[tag + "a"] = bA + 16 * 4
    p.semv[tag + "ck"] = bCK + 4

    @block.tensor
    def _(t_):
        for q in range(4):
            t_.wait_ge(p.sem(tag + "a"), bA + 16 * (q + 1))
            for och in range(32):
                n = q * 32 + och
                wsl = n % 2
                t_.wait_ge(p.sem(tag + "w%d" % wsl), bW[wsl] + 16 * (n // 2 + 1))
                for tt in range(2):
                    j = n * 2 + tt
                    psl = j % 2
                    if j >= 2:
                        t_.wait_ge(p.sem(tag + "e"), bE + j - 1)
                    for kt in range(32):
                        ins = t_.matmul(ps[:, psl * 512:(psl + 1) * 512],
                                        wb[:, wsl * 4096 + kt * 128:wsl * 4096 + (kt + 1) * 128],
                                        aq[:, kt * 1024 + tt * 512:kt * 1024 + (tt + 1) * 512],
                                        start=(kt == 0), stop=(kt == 31))
                    ins.then_inc(p.sem(tag + "m"), 1)
    p.semv[tag + "m"] = bM + 256
    p.semv[tag + "w0"] = bW[0] + 16 * 64
    p.semv[tag + "w1"] = bW[1] + 16 * 64

    @block.scalar
    def _(s):
        for q in range(4):
            for och in range(32):
                for tt in range(2):
                    j = (q * 32 + och) * 2 + tt
                    psl = j % 2
                    s.wait_ge(p.sem(tag + "m"), bM + j + 1)
                    if j >= 2:
                        s.wait_ge(p.sem(tag + "st%d" % psl), bST[psl] + 16 * (j // 2))
                    s.activation(bb[:, psl * 512:(psl + 1) * 512], ps[:, psl * 512:(psl + 1) * 512],
                                 AF.Gelu, scale=ctx[s_key][:, och:och + 1]).then_inc(p.sem(tag + "e"), 1)
    p.semv[tag + "e"] = bE + 256

    @block.gpsimd
    def _(g):
        for q in range(4):
            for och in range(32):
                for tt in range(2):
                    j = (q * 32 + och) * 2 + tt
                    psl = j % 2
                    g.wait_ge(p.sem(tag + "e"), bE + j + 1)
                    g.dma_start(out=rr(p.T(dst_nm, write=True), "(oc pp) s -> pp oc s", pp=128
                                       )[:, och, q * 1024 + tt * 512:q * 1024 + (tt + 1) * 512],
                                in_=bb[:, psl * 512:(psl + 1) * 512]).then_inc(p.sem(tag + "st%d" % psl), 16)
            g.wait_ge(p.sem(tag + "st0"), bST[0] + 16 * 32 * (q + 1))
            g.wait_ge(p.sem(tag + "st1"), bST[1] + 16 * 32 * (q + 1))
    p.semv[tag + "st0"] = bST[0] + 16 * 128
    p.semv[tag + "st1"] = bST[1] + 16 * 128
    st.close()
    barrier(p, block)


def emit_p9(p, block, ctx):
    emit_hpass(p, block, ctx, "aT_b", "h1T", "h1s", "bT_b", "p9")


def emit_p10(p, block, ctx):
    emit_hpass(p, block, ctx, "bT_b", "h2T", "h2s", "cT_b", "p10")


def emit_p11(p, block, ctx):
    st = contextlib.ExitStack()
    w3_sb = p.sb("p11_w3", [128, 32 * 1024], BF16, st)
    cc = p.sb("p11_c", [128, 32 * 512], BF16, st)
    pr = p.sb("p11_pr", [128, 2 * 1024], F32, st)
    ob = p.sb("p11_ob", [128, 2 * 1024], F32, st)
    ps = p.ps("p11_ps", [128, 2 * 512], F32, st)

    bW = p.val("p11w"); bC = p.val("p11c"); bM = p.val("p11m"); bE = p.val("p11e")
    bST = [p.val("p11st0"), p.val("p11st1")]
    bPR = [p.val("p11pr0"), p.val("p11pr1")]
    bCK = p.val("p11ck")

    @block.sync
    def _(sy):
        sy.dma_start(out=rr(w3_sb[:], "p (kt n) -> p kt n", kt=32),
                     in_=pt(p.T("w3Ts"), 0, 32, 0, 1024)).then_inc(p.sem("p11w"), 16)
        for c in range(NCH):
            if c >= 1:
                sy.wait_ge(p.sem("p11ck"), bCK + c)
            sy.dma_start(out=rr(cc[:], "p (kt s) -> p kt s", kt=32),
                         in_=rr(p.T("cT_b"), "(kt pp) s -> pp kt s", pp=128)[:, :, c * 512:(c + 1) * 512]
                         ).then_inc(p.sem("p11c"), 16)
            for t in range(4):
                tn = c * 4 + t
                sl = t % 2
                if tn >= 2:
                    sy.wait_ge(p.sem("p11e"), bE + 2 * (tn - 1))
                sy.dma_start(out=pr[:, sl * 1024:(sl + 1) * 1024],
                             in_=pt(p.T("partial_f"), c * 512 + t * 128, 1, 0, 1024)[:, 0, :]
                             ).then_inc(p.sem("p11pr%d" % sl), 16)
            for t in range(4):
                tn = c * 4 + t
                sl = t % 2
                sy.wait_ge(p.sem("p11e"), bE + 2 * tn + 2)
                sy.dma_start(out=pt(p.T("out", write=True), c * 512 + t * 128, 1, 0, 1024)[:, 0, :],
                             in_=ob[:, sl * 1024:(sl + 1) * 1024]).then_inc(p.sem("p11st%d" % sl), 16)
            sy.wait_ge(p.sem("p11st0"), bST[0] + 16 * 2 * (c + 1))
            sy.wait_ge(p.sem("p11st1"), bST[1] + 16 * 2 * (c + 1))
            sy.sem_inc(p.sem("p11ck"), 1)
    p.semv["p11w"] = bW + 16
    p.semv["p11c"] = bC + 16 * NCH
    p.semv["p11pr0"] = bPR[0] + 16 * 2 * NCH
    p.semv["p11pr1"] = bPR[1] + 16 * 2 * NCH
    p.semv["p11st0"] = bST[0] + 16 * 2 * NCH
    p.semv["p11st1"] = bST[1] + 16 * 2 * NCH
    p.semv["p11ck"] = bCK + NCH

    @block.tensor
    def _(t_):
        t_.wait_ge(p.sem("p11w"), bW + 16)
        for c in range(NCH):
            t_.wait_ge(p.sem("p11c"), bC + 16 * (c + 1))
            for t in range(4):
                for half in range(2):
                    j = (c * 4 + t) * 2 + half
                    psl = j % 2
                    if j >= 2:
                        t_.wait_ge(p.sem("p11e"), bE + j - 1)
                    for kt in range(32):
                        ins = t_.matmul(ps[:, psl * 512:(psl + 1) * 512],
                                        cc[:, kt * 512 + t * 128:kt * 512 + (t + 1) * 128],
                                        w3_sb[:, kt * 1024 + half * 512:kt * 1024 + (half + 1) * 512],
                                        start=(kt == 0), stop=(kt == 31))
                    ins.then_inc(p.sem("p11m"), 1)
    p.semv["p11m"] = bM + NCH * 8

    @block.vector
    def _(v):
        for c in range(NCH):
            for t in range(4):
                tn = c * 4 + t
                sl = t % 2
                for half in range(2):
                    j = tn * 2 + half
                    psl = j % 2
                    v.wait_ge(p.sem("p11m"), bM + j + 1)
                    v.wait_ge(p.sem("p11pr%d" % sl), bPR[sl] + 16 * (tn // 2 + 1))
                    if tn >= 2:
                        v.wait_ge(p.sem("p11st%d" % sl), bST[sl] + 16 * (tn // 2))
                    v.tensor_add(ob[:, sl * 1024 + half * 512:sl * 1024 + (half + 1) * 512],
                                 ps[:, psl * 512:(psl + 1) * 512],
                                 pr[:, sl * 1024 + half * 512:sl * 1024 + (half + 1) * 512]
                                 ).then_inc(p.sem("p11e"), 1)
    p.semv["p11e"] = bE + NCH * 8
    st.close()
    barrier(p, block)
